# version 23
# baseline (speedup 1.0000x reference)
import numpy as np
import ml_dtypes
import concourse.bass as bass
import concourse.mybir as mybir
from concourse.bass_utils import run_bass_kernel_spmd

F32 = mybir.dt.float32
BF16 = mybir.dt.bfloat16
OP = mybir.AluOpType
AF = mybir.ActivationFunctionType
AX = mybir.AxisListType.X
BF = ml_dtypes.bfloat16

S, E, H, G, D = 2048, 2048, 16, 4, 128
NT, NE = 16, 16
KSEL = 256
DFF = 8192
SCL = float(np.float32(1.0) / np.sqrt(np.float32(D)))
EPS = 1e-5
BIG = 1.0e6


class Plan:
    ENGINES = ("tensor", "vector", "scalar", "sync", "gpsimd")

    def __init__(self):
        self.ops = {e: [] for e in self.ENGINES}
        self.cnt = {}

    def now(self, sem):
        return self.cnt.get(sem, 0)

    def op(self, eng, fn, waits=(), inc=None, inc_n=1, serial=False):
        w = []
        for s, v in waits:
            assert v <= self.now(s), f"forward wait {s}>={v} (now {self.now(s)})"
            if v > 0:
                w.append((s, v))
        self.ops[eng].append((fn, w, inc, inc_n, serial))
        if inc is not None:
            self.cnt[inc] = self.cnt.get(inc, 0) + inc_n
        return self.cnt.get(inc, 0)


def _host_consts():
    i128b = np.eye(128, dtype=BF)
    i128f = np.eye(128, dtype=np.float32)
    triu128s = np.triu(np.ones((128, 128), dtype=np.float32), 1).astype(BF)
    triu16s = np.triu(np.ones((16, 16), dtype=np.float32), 1).astype(BF)
    m = np.tril(np.ones((128, 128), dtype=np.float32), -1) * (-1200.0)
    mask512 = np.tile(m, (1, 4)).astype(BF)
    iotab = np.broadcast_to(np.arange(KSEL, dtype=np.float32), (128, KSEL)).astype(BF)
    onesrow_b = np.ones((1, 128), dtype=BF)
    onesrow_f = np.ones((1, 128), dtype=np.float32)
    onescol_b = np.ones((128, 1), dtype=BF)
    epscol = np.full((128, 1), EPS, dtype=np.float32)
    onecol_f = np.ones((128, 1), dtype=np.float32)
    return dict(epscol=epscol, onecol_f=onecol_f, i128b=i128b, i128f=i128f, triu128s=triu128s, triu16s=triu16s,
                mask512=mask512, iotab=iotab, onesrow_b=onesrow_b,
                onesrow_f=onesrow_f, onescol_b=onescol_b)


def build(debug=False):
    nc = bass.Bass("TRN2", target_bir_lowering=False,
                   detect_race_conditions=False)
    p = Plan()

    x0t = nc.dram_tensor("x0", [S, E], F32, kind="ExternalInput")
    wqt = nc.dram_tensor("wq", [H, E, D], F32, kind="ExternalInput")
    bqt = nc.dram_tensor("bq", [H, D], F32, kind="ExternalInput")
    wkt = nc.dram_tensor("wk", [G, E, D], F32, kind="ExternalInput")
    bkt = nc.dram_tensor("bk", [G, D], F32, kind="ExternalInput")
    wvt = nc.dram_tensor("wv", [G, E, D], F32, kind="ExternalInput")
    bvt = nc.dram_tensor("bv", [G, D], F32, kind="ExternalInput")
    wrt = nc.dram_tensor("wr", [E, 1], F32, kind="ExternalInput")
    brt = nc.dram_tensor("br", [1], F32, kind="ExternalInput")
    w1t = nc.dram_tensor("w1", [E, DFF], F32, kind="ExternalInput")
    b1t = nc.dram_tensor("b1", [DFF], F32, kind="ExternalInput")
    w2t = nc.dram_tensor("w2", [DFF, E], F32, kind="ExternalInput")
    b2t = nc.dram_tensor("b2", [E], F32, kind="ExternalInput")
    outt = nc.dram_tensor("out", [S, E], F32, kind="ExternalOutput")
    x1s = nc.dram_tensor("x1s", [S, E], F32,
                         kind="ExternalOutput" if debug else "Internal")
    dbgt = {}
    if debug:
        for nm, sh in [("dbg_logits", [128, 16]), ("dbg_m", [128, 16]),
                       ("dbg_q", [128, 16]), ("dbg_w", [128, 16]),
                       ("dbg_cg", [128, 16]),
                       ("dbg_fxT", [128, NE * 256]), ("dbg_y0T", [128, S]),
                       ("dbg_kt", [128, S]), ("dbg_qh", [128, S]),
                       ("dbg_hT", [128, 2048]), ("dbg_xw", [128, E]),
                       ("dbg_y1", [128, E]), ("dbg_wg", [128, 2]),
                       ("dbg_rp", [128, 16]), ("dbg_y1T", [128, E]),
                       ("dbg_sums", [128, 16]), ("dbg_sumsq", [128, 16])]:
            dbgt[nm] = nc.dram_tensor(nm, sh, F32, kind="ExternalOutput")

    cn = _host_consts()
    inl = {k: nc.inline_tensor(v, k) for k, v in cn.items()}

    A_ = lambda name, shape, dt, off: nc.alloc_sbuf_tensor_at(
        name, shape, dt, offset=off)
    o_const = 16512
    o_M = 23936
    o_F = 28032
    o_xw = 23936
    o_A = 32128
    o_B = 97664
    o_C = 163200
    o_D = 196480
    o_E = 204672

    off = o_const
    def calloc(name, shape, dt):
        nonlocal off
        nb = (shape[1] * mybir.dt.size(dt) + 31) // 32 * 32
        t = A_(name, shape, dt, off)
        off += nb
        return t
    i128b_s = calloc("i128b", [128, 128], BF16)
    triu128s_s = calloc("triu128s", [128, 128], BF16)
    mask512_s = calloc("mask512", [128, 512], BF16)
    iotab_s = calloc("iotab", [128, KSEL], BF16)
    i128f_s = calloc("i128f", [128, 128], F32)
    triu16s_s = calloc("triu16s", [16, 16], BF16)
    onesrow_b_s = calloc("onesrow_b", [1, 128], BF16)
    onesrow_f_s = calloc("onesrow_f", [1, 128], F32)
    onescol_b_s = calloc("onescol_b", [128, 1], BF16)
    epscol_s = calloc("epscol", [128, 1], F32)
    onecol_f_s = calloc("onecol_f", [128, 1], F32)
    logits_s = calloc("logits", [128, 16], F32)
    cg_s = calloc("cg", [128, 16], F32)
    m_s = calloc("msel", [128, 16], F32)
    mb_s = calloc("mb", [128, 16], BF16)
    q_s = calloc("qsel", [128, 16], F32)
    t1_s = calloc("t1", [128, 16], F32)
    lm_s = calloc("lm", [128, 16], F32)
    wu_s = calloc("wu", [128, 16], F32)
    w_s = calloc("wsel", [128, 16], F32)
    wb_s = calloc("wb", [128, 16], BF16)
    wg_s = calloc("wg", [128, 2], F32)
    rstdp1_s = calloc("rstdp1", [128, 16], F32)
    nbias1_s = calloc("nbias1", [128, 16], F32)
    bqS = calloc("bqS", [128, 16], F32)
    bkS = calloc("bkS", [128, 4], F32)
    wrS = calloc("wrS", [128, 16], BF16)
    brcol = calloc("brcol", [128, 1], F32)
    bv_row = calloc("bv_row", [1, 512], BF16)
    diagstage = A_("diagstage", [128, 512], BF16,
                   nc.lookup_mloc(bv_row).addr)
    stat_a = calloc("stat_a", [128, 1], F32)
    stat_b = calloc("stat_b", [128, 1], F32)
    stat_c = calloc("stat_c", [128, 1], F32)
    stat_d = calloc("stat_d", [128, 1], F32)
    stat_e = calloc("stat_e", [128, 1], F32)
    stat_f = calloc("stat_f", [128, 1], F32)
    stat_g = calloc("stat_g", [128, 1], F32)
    stat_h = calloc("stat_h", [128, 1], F32)
    stat_r = calloc("stat_r", [128, 1], F32)
    stat_c2 = calloc("stat_c2", [128, 1], F32)
    stata4 = calloc("stata4", [128, 4], F32)
    statb4 = calloc("statb4", [128, 4], F32)
    sdum4 = calloc("sdum4", [128, 4], F32)
    sdum4b = calloc("sdum4b", [128, 4], F32)
    stat_a2 = calloc("stat_a2", [128, 1], F32)
    stat_b2 = calloc("stat_b2", [128, 1], F32)
    stat_e2 = calloc("stat_e2", [128, 1], F32)
    stat_r2 = calloc("stat_r2", [128, 1], F32)
    stat_a3 = calloc("stat_a3", [128, 1], F32)
    stat_b3 = calloc("stat_b3", [128, 1], F32)
    stat_e3 = calloc("stat_e3", [128, 1], F32)
    stat_r3 = calloc("stat_r3", [128, 1], F32)
    stat_a4 = calloc("stat_a4", [128, 1], F32)
    stat_b4 = calloc("stat_b4", [128, 1], F32)
    stat_e4 = calloc("stat_e4", [128, 1], F32)
    stat_r4 = calloc("stat_r4", [128, 1], F32)
    sdum = calloc("sdum", [128, 1], F32)
    hkcol = calloc("hkcol", [128, 1], F32)
    nlog2 = calloc("nlog2", [128, 2], F32)
    cgalt2 = calloc("cgalt2", [128, 2], F32)
    vdum = calloc("vdum", [128, 1], F32)
    lm2_s = calloc("lm2", [128, 16], F32)
    dbg_sums = calloc("dbg_sums_sb", [128, 16], F32)
    dbg_sumsq = calloc("dbg_sumsq_sb", [128, 16], F32)
    recipz = calloc("recipz", [128, 4], F32)
    mx1_s = calloc("mx1", [128, 1], F32)
    negmx_s = calloc("negmx", [128, 1], F32)
    zc_s = calloc("zc", [128, 1], F32)
    rz_s = calloc("rz", [1, 1], F32)
    rzc_s = calloc("rzc", [128, 1], F32)
    ts16_s = calloc("ts16", [16, 1], BF16)
    offs_s = calloc("offs", [16, 1], BF16)
    offsrow_s = calloc("offsrow", [1, 16], BF16)
    assert off <= o_M, f"consts overflow: {off} > {o_M}"

    y0T = A_("y0T", [128, NE * S], BF16, o_A)
    y1 = A_("y1", [128, NT * E], BF16, o_A)
    hT = A_("hT", [128, 64 * 256], BF16, o_A)
    Qall = A_("Qall", [128, H * S], BF16, o_B)
    wkS = A_("wkS", [128, G * E], BF16, o_B)
    wvS = A_("wvS", [128, G * E], BF16, o_B + 16384)
    w1ring = [A_(f"w1r{i}", [128, NE * 512], BF16, o_B + i * 16384)
              for i in range(4)]
    x1re = [A_(f"x1re{i}", [128, E], F32, o_B + i * 8192) for i in range(6)]
    outstage = [A_(f"outst{i}", [128, E], F32, o_B + 49152 + i * 8192)
                for i in range(2)]
    tstage = A_("tstage", [128, E], F32, o_E)
    KT = A_("KT", [128, G * S], BF16, o_C)
    Vs = A_("Vs", [128, G * 16 * 132], BF16, o_C + 16384)
    fxT = A_("fxT", [128, NE * 256], BF16, o_C)
    logrow = A_("logrow", [1, S], F32, o_C)
    LB = A_("LB", [128, S], F32, o_C + 8192)
    b1row = A_("b1row", [1, DFF], BF16, o_C + 8192)
    PT = A_("PT", [128, NT * 256], BF16, o_E + 8192)
    gdump = A_("gdump", [128, S], F32, o_C + 16384)
    log16 = A_("log16", [16, 128], F32, o_C + 16384)
    Psb = A_("Psb", [128, 2 * S], BF16, o_C + 24576)
    ones2k = A_("ones2k", [128, S], F32, o_C + 24576)
    w2ring = ([A_(f"w2ra{i}", [128, 2 * E], BF16, o_A + 32768 + i * 8192)
               for i in range(4)] +
              [A_(f"w2rc{i}", [128, 2 * E], BF16, o_C + [16384, 0, 8192][i])
               for i in range(3)])
    wqS = [A_(f"wqS{i}", [128, E], BF16, o_D + i * 4096) for i in range(2)]
    y0tB = A_("y0tB", [128, E], BF16, o_D)
    attnrowB = A_("attnrowB", [128, E], F32, o_D)
    b2row = A_("b2row", [1, E], BF16, o_D)
    x0tile = [A_(f"x0t{i}", [128, E], F32, o_E + i * 8192) for i in range(2)]
    x0p = [A_(f"x0p{i}", [128, E], F32, o_B + 32768 + i * 8192)
           for i in range(4)]
    y0p = [A_(f"y0p{i}", [128, E], BF16, o_E + 8192 + i * 4096)
           for i in range(4)]
    y0tA = A_("y0tA", [128, E], BF16, o_E + 16384)
    attnrowA = A_("attnrowA", [128, E], F32, o_E + 16384)
    attnrow = [attnrowA, attnrowB]
    WrB = A_("WrB", [128, E], BF16, o_M)
    wrrow = A_("wrrow", [1, 512], BF16, o_F)
    expTbig = A_("expTbig", [128, 2048], BF16, o_F)
    expT = [expTbig[:, i * 512:(i + 1) * 512] for i in range(4)]
    expT.append(diagstage[:])
    bq_nat = A_("bq_nat", [16, 128], F32, o_F)
    bk_nat = A_("bk_nat", [4, 128], F32, o_F + 512)
    wr_nat = A_("wr_nat", [16, 128], F32, o_F + 1024)
    br_nat = A_("br_nat", [1, 1], F32, o_F + 1536)
    htile = [A_(f"htile{i}", [128, 512], BF16, o_F + i * 1024) for i in range(4)]
    xwsb = [A_(f"xw{i}", [128, E], BF16, o_xw + i * 4096) for i in range(2)]

    PS = nc.alloc_psum_tensor("PS", [128, 4096], F32)
    bank = lambda i: PS[:, i * 512:(i + 1) * 512]
    brel = {i: None for i in range(8)}

    def bw(i):
        return [brel[i]] if brel[i] is not None else []

    def dbg_dump(nm, src, waits, f32src=False):
        if not debug:
            return
        eng = "sync" if f32src else "gpsimd"
        p.op(eng, lambda s, nm=nm, src=src: s.dma_start(dbgt[nm].ap(), src),
             waits, "dD", 16)

    for name, sb in [("i128b", i128b_s), ("triu128s", triu128s_s),
                     ("mask512", mask512_s), ("iotab", iotab_s),
                     ("i128f", i128f_s), ("triu16s", triu16s_s),
                     ("onesrow_b", onesrow_b_s), ("onesrow_f", onesrow_f_s),
                     ("onescol_b", onescol_b_s), ("epscol", epscol_s),
                     ("onecol_f", onecol_f_s)]:
        ap = inl[name].ap()
        p.op("gpsimd", lambda g, sb=sb, ap=ap: g.dma_start(sb[:], ap), (), "dC", 16)
    p.op("sync", lambda s: s.dma_start(bq_nat[:], bqt.ap()), (), "dP", 16)
    p.op("sync", lambda s: s.dma_start(bk_nat[:], bkt.ap()), (), "dP", 16)
    p.op("sync", lambda s: s.dma_start(
        wr_nat[:], wrt.ap().rearrange("(t p) o -> t p o", t=16)), (), "dP", 16)
    p.op("sync", lambda s: s.dma_start(
        br_nat[:], brt.ap().rearrange("(a b) -> a b", a=1)), (), "dP", 16)
    p.op("gpsimd", lambda g: g.dma_start(
        bv_row[:], bvt.ap().rearrange("g d -> (g d)").rearrange(
            "(a f) -> a f", a=1)), (), "dC", 16)
    for g_ in range(G):
        p.op("gpsimd", lambda g, g_=g_: g.dma_start(
            wkS[:, g_ * E:(g_ + 1) * E].rearrange("p (t d) -> p t d", t=16),
            wkt.ap()[g_].rearrange("(t p) d -> p t d", p=128)), (), "dKV", 16)
        p.op("gpsimd", lambda g, g_=g_: g.dma_start(
            wvS[:, g_ * E:(g_ + 1) * E].rearrange("p (t d) -> p t d", t=16),
            wvt.ap()[g_].rearrange("(t p) d -> p t d", p=128)), (), "dKV", 16)
    dC_tot = p.now("dC")
    dKV_tot = p.now("dKV")
    w_cd = [("dP", p.now("dP")), ("dC", dC_tot)]
    p.op("tensor", lambda t: t.matmul(PS[:, 3072:3088], bq_nat[:],
                                      i128f_s[0:16, 0:16], start=True, stop=True),
         w_cd, "tS")
    p.op("tensor", lambda t: t.matmul(PS[:, 3088:3092], bk_nat[:],
                                      i128f_s[0:4, 0:4], start=True, stop=True),
         (), "tS")
    p.op("tensor", lambda t: t.matmul(PS[:, 3104:3120], wr_nat[:],
                                      i128f_s[0:16, 0:16], start=True, stop=True),
         (), "tS")
    p.op("tensor", lambda t: t.matmul(PS[:, 3120:3121], onesrow_f_s[:],
                                      br_nat[:], start=True, stop=True), (), "tS")
    tS_par = p.now("tS")

    x0dma = []
    x0rel = [None, None, None, None]
    y0rel = [None, None, None, None]
    def emit_x0dma(st):
        sl = st % 4
        waits = [x0rel[sl]] if x0rel[sl] is not None else []
        sem = f"dX{sl}"
        x0dma.append((sem, p.op("sync", lambda s, st=st, sl=sl: s.dma_start(
            x0p[sl][:], x0t.ap()[st * 128:(st + 1) * 128, :]), waits, sem, 16)))
    for _st in range(4):
        emit_x0dma(_st)
    SA4 = [stat_a, stat_a2, stat_a3, stat_a4]
    SB4 = [stat_b, stat_b2, stat_b3, stat_b4]
    SE4 = [stat_e, stat_e2, stat_e3, stat_e4]
    SR4 = [stat_r, stat_r2, stat_r3, stat_r4]

    def emit_stats(st):
        sl = st % 4
        sb = SB4[st % 4]
        wd = [x0dma[st]]
        ywaits0 = [y0rel[sl]] if y0rel[sl] is not None else []
        p.op("vector", lambda v, sl=sl, sb=sb: v.scalar_tensor_tensor(
            y0p[sl][:], x0p[sl][:], 1.0, x0p[sl][:], OP.mult, OP.mult,
            accum_out=sb[:]), wd + ywaits0, "vS")
        return p.now("vS")

    def emit_sum(st):
        sl = st % 4
        sa = SA4[st % 4]
        p.op("scalar", lambda se, sl=sl, sa=sa: se.activation(
            tstage[:], x0p[sl][:], AF.Identity, accum_out=sa[:]),
            [x0dma[st]], "sS")

    vS_stats_l = {}
    y0T_tdone = {}
    y0ev = {}

    def emit_p0_part1(st):
        sl = st % 4
        y0t_ = y0p[sl]
        sa, sb = SA4[st % 4], SB4[st % 4]
        se_, sr_ = SE4[st % 4], SR4[st % 4]
        p.op("scalar", lambda se: se.copy(sdum[:], onecol_f_s[:]),
             [("vS", vS_stats_l[st])], "sS")
        p.op("scalar", lambda se: se.copy(sdum[:], onecol_f_s[:]), (), "sS")
        if debug:
            p.op("scalar", lambda se, st=st, sa=sa: se.copy(
                dbg_sums[:, st:st + 1], sa[:]), (), "sS")
            p.op("scalar", lambda se, st=st, sb=sb: se.copy(
                dbg_sumsq[:, st:st + 1], sb[:]), (), "sS")
        p.op("scalar", lambda se, sa=sa: se.mul(
            stat_c[:], sa[:], 1.0 / E), (), "sS")
        p.op("scalar", lambda se, sb=sb: se.mul(
            stat_d[:], sb[:], 1.0 / E), (), "sS")
        p.op("scalar", lambda se: se.activation(
            stat_f[:], stat_c[:], AF.Square), (), "sS")
        p.op("scalar", lambda se: se.activation(
            stat_g[:], stat_f[:], AF.Identity, bias=stat_d[:], scale=-1.0),
            (), "sS")
        p.op("scalar", lambda se: se.activation(
            stat_h[:], stat_g[:], AF.Ln, bias=epscol_s[:]), (), "sS")
        p.op("scalar", lambda se, sr_=sr_: se.activation(
            sr_[:], stat_h[:], AF.Exp, scale=-0.5), (), "sS")
        p.op("scalar", lambda se: se.mul(stat_c2[:], stat_c[:], -1.0), (), "sS")
        p.op("scalar", lambda se, se_=se_, sr_=sr_: se.activation(
            se_[:], stat_c2[:], AF.Identity, scale=sr_[:]), (), "sS")
        sS_ch0 = p.now("sS")
        ywaits = [("sS", sS_ch0)]
        if y0rel[sl] is not None:
            ywaits.append(y0rel[sl])
        p.op("vector", lambda v, sl=sl, y0t_=y0t_, se_=se_, sr_=sr_:
             v.tensor_scalar(y0t_[:], x0p[sl][:], sr_[:], se_[:],
                             OP.mult, OP.add), ywaits, "vS")
        y0ev[st] = ("vS", p.now("vS"))
        x0rel[sl] = y0ev[st]
        if st + 4 < NT:
            emit_x0dma(st + 4)

    def emit_p0_part2(st):
        sl = st % 4
        y0t_ = y0p[sl]
        for grp in range(4):
            bk_ = 6 if grp % 2 == 0 else 7
            for j in range(4):
                et = grp * 4 + j
                twaits = [y0ev[st]] + bw(bk_) if j == 0 else ()
                p.op("tensor", lambda t, y0t_=y0t_, et=et, bk_=bk_, j=j: t.matmul(
                    PS[:, bk_ * 512 + j * 128: bk_ * 512 + (j + 1) * 128],
                    y0t_[:, et * 128:(et + 1) * 128], i128b_s[:],
                    start=True, stop=True), twaits, "tS")
            tS_tr = p.now("tS")
            def ev(v, grp=grp, st=st, bk_=bk_):
                dst = y0T[:].rearrange("p (e s) -> p e s", e=NE)[
                    :, grp * 4:grp * 4 + 4, st * 128:(st + 1) * 128]
                return v.tensor_copy(dst, bank(bk_))
            brel[bk_] = ("vS", p.op("vector", ev, [("tS", tS_tr)], "vS"))
        y0rel[sl] = ("tS", p.now("tS"))
        y0T_tdone[st] = ("vS", p.now("vS"))

    for st in range(4):
        emit_sum(st)
        vS_stats_l[st] = emit_stats(st)
    for st in range(4):
        emit_p0_part1(st)
    p.op("vector", lambda v: v.tensor_copy(wrS[:], PS[:, 3104:3120]),
         [("tS", tS_par)], "vS")
    p.op("vector", lambda v: v.tensor_copy(brcol[:], PS[:, 3120:3121]), (), "vS")
    p.op("vector", lambda v: v.memset(
        Vs[:].rearrange("p (g k c) -> p g k c", g=G, k=16)[:, :, :, 128:129],
        1.0), (), "vS")
    vS_par2 = p.now("vS")
    p.op("scalar", lambda se: se.copy(bqS[:], PS[:, 3072:3088]),
         [("tS", tS_par), ("vS", vS_par2)], "sS")
    p.op("scalar", lambda se: se.copy(bkS[:], PS[:, 3088:3092]), (), "sS")
    brel[6] = ("sS", p.now("sS"))
    brel[7] = brel[6]
    for cch in range(4):
        for st in range(4 * cch, 4 * cch + 4):
            emit_p0_part2(st)
        for g_ in range(G):
            bk_ = 4 + (g_ % 2)
            for et in range(NE):
                waits = ()
                if et == 0:
                    waits = [y0T_tdone[4 * cch + 3]] + bw(bk_)
                    if cch == 0 and g_ == 0:
                        waits = waits + [("dKV", dKV_tot)]
                p.op("tensor", lambda t, g_=g_, cch=cch, et=et, bk_=bk_: t.matmul(
                    bank(bk_), wkS[:, g_ * E + et * 128:g_ * E + (et + 1) * 128],
                    y0T[:, et * S + cch * 512: et * S + (cch + 1) * 512],
                    start=(et == 0), stop=(et == NE - 1)), waits, "tS")
            tS_k = p.now("tS")
            brel[bk_] = ("sS", p.op(
                "scalar", lambda se, g_=g_, cch=cch, bk_=bk_: se.activation(
                    KT[:, g_ * S + cch * 512: g_ * S + (cch + 1) * 512],
                    bank(bk_), AF.Identity, bias=bkS[:, g_:g_ + 1]),
                [("tS", tS_k)], "sS"))
        if cch + 1 < 4:
            for st in range(4 * cch + 4, 4 * cch + 8):
                emit_sum(st)
                vS_stats_l[st] = emit_stats(st)
        for st in range(4 * cch, 4 * cch + 4):
            for et in range(NE):
                for g_ in range(G):
                    waits = ()
                    if et == 0:
                        waits = bw(g_)
                        if st == 0 and g_ == 0:
                            waits = waits + [("dKV", dKV_tot)]
                    p.op("tensor", lambda t, st=st, et=et, g_=g_: t.matmul(
                        PS[:, g_ * 512:g_ * 512 + 128],
                        y0T[:, et * S + st * 128: et * S + (st + 1) * 128],
                        wvS[:, g_ * E + et * 128: g_ * E + (et + 1) * 128],
                        start=(et == 0), stop=False), waits, "tS")
            for g_ in range(G):
                p.op("tensor", lambda t, g_=g_: t.matmul(
                    PS[:, g_ * 512:g_ * 512 + 128], onesrow_b_s[:],
                    bv_row[0:1, g_ * 128:(g_ + 1) * 128], start=False,
                    stop=True), (), "tS")
            tS_v = p.now("tS")
            for g_ in range(G):
                brel[g_] = ("vS", p.op(
                    "vector", lambda v, st=st, g_=g_: v.tensor_copy(
                        Vs[:, g_ * 2112 + st * 132: g_ * 2112 + st * 132 + 128],
                        PS[:, g_ * 512:g_ * 512 + 128]), [("tS", tS_v)], "vS"))
        if cch + 1 < 4:
            for st in range(4 * cch + 4, 4 * cch + 8):
                emit_p0_part1(st)
    vS_y0T = p.now("vS")
    tS_P0 = p.now("tS")
    for c_ in range(4):
        dwr = p.op("gpsimd", lambda g, c_=c_: g.dma_start(
            wrrow[:], wrt.ap()[c_ * 512:(c_ + 1) * 512, :].rearrange(
                "(a f) o -> a (f o)", a=1)),
            [] if c_ == 0 else [("sS", p.now("sS"))], "dWR", 16)
        p.op("tensor", lambda t, c_=c_: t.matmul(
            PS[:, 3584:4096], onesrow_b_s[:], wrrow[:],
            start=True, stop=True), [("dWR", dwr)] + (bw(7) if c_ == 0 else []),
            "tS")
        p.op("scalar", lambda se, c_=c_: se.copy(
            WrB[:, c_ * 512:(c_ + 1) * 512], PS[:, 3584:4096]),
            [("tS", p.now("tS"))], "sS")
    brel[7] = ("sS", p.now("sS"))
    sS_wrb = p.now("sS")
    dbg_dump("dbg_y0T", y0T[:, 0:S], [("vS", vS_y0T)])
    dbg_dump("dbg_sums", dbg_sums[:], [("vS", vS_y0T)], f32src=True)
    dbg_dump("dbg_sumsq", dbg_sumsq[:], [("vS", vS_y0T)], f32src=True)
    sS_K = p.now("sS")
    dbg_dump("dbg_kt", KT[:, 0:S], [("sS", sS_K)])
    sS_P1 = p.now("sS")
    vS_P1 = p.now("vS")

    wqdma = []
    wq_rel = [None, None]
    def emit_wqdma(h_):
        sl = h_ % 2
        waits = []
        if wq_rel[sl] is not None:
            waits.append(wq_rel[sl])
        sem = f"dQ{sl}"
        wqdma.append((sem, p.op("gpsimd", lambda g, h_=h_, sl=sl: g.dma_start(
            wqS[sl][:].rearrange("p (t d) -> p t d", t=16),
            wqt.ap()[h_].rearrange("(t p) d -> p t d", p=128)),
            waits, sem, 16)))
    emit_wqdma(0)
    emit_wqdma(1)
    qevac = []
    qpair = 0
    for h_ in range(H):
        sl = h_ % 2
        for pair in range(2):
            b0 = 4 + 2 * (qpair % 2)
            qpair += 1
            for et in range(NE):
                for cc in range(2):
                    cch = pair * 2 + cc
                    bk_ = b0 + cc
                    waits = ()
                    if et == 0 and cc == 0:
                        waits = [wqdma[h_]] + bw(b0) + bw(b0 + 1)
                    p.op("tensor",
                         lambda t, sl=sl, cch=cch, et=et, bk_=bk_: t.matmul(
                             bank(bk_), wqS[sl][:, et * 128:(et + 1) * 128],
                             y0T[:, et * S + cch * 512: et * S + (cch + 1) * 512],
                             start=(et == 0), stop=(et == NE - 1)), waits, "tS")
            tS_q = p.now("tS")
            for cc in range(2):
                cch = pair * 2 + cc
                bk_ = b0 + cc
                brel[bk_] = ("sS", p.op(
                    "scalar", lambda se, h_=h_, cch=cch, bk_=bk_: se.activation(
                        Qall[:, h_ * S + cch * 512: h_ * S + (cch + 1) * 512],
                        bank(bk_), AF.Identity, bias=bqS[:, h_:h_ + 1]),
                    [("tS", tS_q)], "sS"))
        qevac.append(p.now("sS"))
        wq_rel[sl] = ("tS", p.now("tS"))
        if h_ + 2 < H:
            emit_wqdma(h_ + 2)
    dbg_dump("dbg_qh", Qall[:, 0:S], [("sS", qevac[0])])

    chunk_avdone = []
    chunk_expdone = []
    x0dma3 = []
    x0rel3 = [None, None]
    attnrow_rel = [None, None]
    av_rel = [("vS", vS_P1)]
    x1dma = []
    y1evac = []
    tail_defer = []

    def flush_tail():
        for fn in tail_defer:
            fn()
        tail_defer.clear()
    chunk_id = 0
    qall_r = Qall[:].rearrange("p (h s) -> p h s", h=H)
    vs_r = Vs[:]
    for qt in range(NT):
        sl3 = qt % 2
        ar = qt % 2
        waits = [x0rel3[sl3]] if x0rel3[sl3] is not None else [("sS", qevac[0])]
        sem = f"dX{sl3}"
        x0dma3.append((sem, p.op("sync", lambda s, qt=qt, sl3=sl3: s.dma_start(
            x0tile[sl3][:], x0t.ap()[qt * 128:(qt + 1) * 128, :]), waits,
            sem, 16)))
        slot_user = {}
        uctr = [0]
        carry = [None]
        for g_ in range(G):
            pendq = []

            def emit_av(pend, g_=g_):
                for (cid, esl, kt, avw, sflag, eflag) in pend:
                    if sflag:
                        avw = avw + av_rel
                    for m_ in range(4):
                        p.op("tensor",
                             lambda t, m_=m_, esl=esl, g_=g_, kt=kt,
                             sflag=sflag, eflag=eflag:
                             t.matmul(
                                 PS[:, m_ * 512:m_ * 512 + 129],
                                 expT[esl][:, m_ * 128:(m_ + 1) * 128],
                                 vs_r[:, g_ * 2112 + kt * 132:
                                      g_ * 2112 + kt * 132 + 129],
                                 start=sflag, stop=eflag),
                             avw if m_ == 0 else (), "tS")
                    while len(chunk_avdone) <= cid:
                        chunk_avdone.append(0)
                    chunk_avdone[cid] = p.now("tS")
                    slot_user[esl] = p.now("tS")

            def emit_diag(gg, qt=qt):
                nonlocal chunk_id
                uslot = uctr[0] % 2
                uctr[0] += 1
                cid = chunk_id
                chunk_id += 1
                sbk = 4 + 2 * uslot
                swaits = [("sS", qevac[gg * 4 + 3])] + bw(sbk)
                p.op("tensor", lambda t, gg=gg, qt=qt, sbk=sbk: t.matmul(
                    bank(sbk),
                    KT[:, gg * S + qt * 128:gg * S + (qt + 1) * 128],
                    qall_r[:, gg * 4:(gg + 1) * 4, qt * 128:(qt + 1) * 128],
                    start=True, stop=False), swaits, "tS")
                p.op("tensor", lambda t, sbk=sbk: t.matmul(
                    bank(sbk), i128b_s[:], mask512_s[:],
                    start=False, stop=True), (), "tS")
                tS_sc = p.now("tS")
                ewaits = [("tS", tS_sc)]
                if 4 in slot_user:
                    ewaits.append(("tS", slot_user[4]))
                rel = p.op("scalar", lambda se, sbk=sbk: se.activation(
                    expT[4][:], bank(sbk), AF.Exp, scale=SCL), ewaits, "sS")
                brel[sbk] = ("sS", rel)
                return [(cid, 4, qt, [("sS", rel)], True, qt == 0)]

            units = [(qt,)]
            nkt = qt + 1
            j = 0
            while j + 1 < nkt - 1:
                units.append((j, j + 1))
                j += 2
            if j < nkt - 1:
                units.append((j,))
                j += 1
            cbase = 0
            if carry[0] is not None:
                pendq.append(carry[0])
                carry[0] = None
                units = units[1:]
                cbase = 1

            for ui, unit in enumerate(units):
                uslot = uctr[0] % 2
                uctr[0] += 1
                scw_extra = []
                if qt == 0 and g_ == 0 and ui == 0:
                    scw_extra = [("vS", vS_P1), ("sS", sS_P1)]
                ch = []
                for k_i, kt in enumerate(unit):
                    cid = chunk_id
                    chunk_id += 1
                    if len(unit) == 2:
                        esl = 2 * uslot + k_i
                    else:
                        esl = 4 if unit[0] == qt else 2 * uslot
                    sbk = 4 + 2 * uslot + k_i if len(unit) == 2 else 4 + 2 * uslot
                    swaits = [("sS", qevac[g_ * 4 + 3])] + scw_extra + bw(sbk)
                    scw_extra = []
                    is_d = (kt == qt)
                    p.op("tensor",
                         lambda t, g_=g_, qt=qt, kt=kt, sbk=sbk, is_d=is_d:
                         t.matmul(
                             bank(sbk),
                             KT[:, g_ * S + kt * 128:g_ * S + (kt + 1) * 128],
                             qall_r[:, g_ * 4:(g_ + 1) * 4,
                                    qt * 128:(qt + 1) * 128],
                             start=True, stop=not is_d), swaits, "tS")
                    if is_d:
                        mkw = [("dC", dC_tot)] if (qt == 0 and g_ == 0) else ()
                        p.op("tensor", lambda t, sbk=sbk: t.matmul(
                            bank(sbk), i128b_s[:], mask512_s[:],
                            start=False, stop=True), mkw, "tS")
                    ch.append((cid, esl, kt, sbk))
                tS_sc = p.now("tS")
                is_diag = (unit[-1] == qt)
                ewaits = [("tS", tS_sc)]
                for (cid, esl, kt, sbk) in ch:
                    if esl in slot_user:
                        ewaits.append(("tS", slot_user[esl]))
                if len(ch) == 2:
                    (c0, e0, k0, b0_), (c1, e1, k1, b1_) = ch
                    rel = p.op("scalar", lambda se, e0=e0, b0_=b0_: se.activation(
                        expTbig[:, e0 * 512:(e0 + 2) * 512],
                        PS[:, b0_ * 512:(b0_ + 2) * 512], AF.Exp, scale=SCL),
                        ewaits, "sS")
                    brel[b0_] = ("sS", rel)
                    brel[b0_ + 1] = ("sS", rel)
                    avw0 = [("sS", rel)]
                    avw1 = []
                else:
                    (c0, e0, k0, b0_) = ch[0]
                    rel = p.op("scalar", lambda se, e0=e0, b0_=b0_:
                               se.activation(expT[e0][:], bank(b0_), AF.Exp,
                                             scale=SCL), ewaits, "sS")
                    brel[b0_] = ("sS", rel)
                    avw0 = [("sS", rel)]
                    if is_diag and g_ == 0 and tail_defer:
                        flush_tail()
                pend = []
                nch_before = cbase + sum(len(u) for u in units[:ui])
                for idx, (cid, esl, kt, sbk) in enumerate(ch):
                    avw = avw0 if idx == 0 else []
                    first_chunk = (nch_before + idx == 0)
                    last_chunk = (nch_before + idx == nkt - 1)
                    pend.append((cid, esl, kt, avw, first_chunk, last_chunk))
                pendq.append(pend)
                if len(pendq) > 1:
                    emit_av(pendq.pop(0))
            for pend in pendq:
                emit_av(pend)
            if g_ < 3:
                carry[0] = emit_diag(g_ + 1)
            tS_av = p.now("tS")
            zap = PS[:].rearrange("p (b c) -> p b c", b=8)[:, 0:4, 128:129]
            zap = zap.rearrange("p b o -> p (b o)")
            vS_rz = p.op("vector", lambda v, zap=zap: v.reciprocal(
                recipz[:], zap), [("tS", tS_av)], "vS")
            p.op("vector", lambda v: v.tensor_copy(vdum[:], recipz[:, 0:1]),
                 (), "vS")
            arw = [x0dma3[qt]]
            if g_ == 0 and attnrow_rel[ar] is not None:
                arw.append(attnrow_rel[ar])
            for m_ in range(4):
                p.op("vector", lambda v, m_=m_, g_=g_, ar=ar, sl3=sl3:
                     v.scalar_tensor_tensor(
                         attnrow[ar][:, g_ * 512 + m_ * 128:
                                     g_ * 512 + (m_ + 1) * 128],
                         PS[:, m_ * 512:m_ * 512 + 128],
                         recipz[:, m_:m_ + 1],
                         x0tile[sl3][:, g_ * 512 + m_ * 128:
                                     g_ * 512 + (m_ + 1) * 128],
                         OP.mult, OP.add), arw if m_ == 0 else (), "vS")
            av_rel = [("vS", p.now("vS"))]
            p.op("vector", lambda v, g_=g_, ar=ar: v.tensor_reduce(
                stata4[:, g_:g_ + 1],
                attnrow[ar][:, g_ * 512:(g_ + 1) * 512], AX, OP.add), (), "vS")
            p.op("vector", lambda v, g_=g_, ar=ar, qt=qt:
                 v.scalar_tensor_tensor(
                     y1[:, qt * E + g_ * 512:qt * E + (g_ + 1) * 512],
                     attnrow[ar][:, g_ * 512:(g_ + 1) * 512], 1.0,
                     attnrow[ar][:, g_ * 512:(g_ + 1) * 512],
                     OP.mult, OP.mult, accum_out=statb4[:, g_:g_ + 1]),
                 (), "vS")
        vS_add = p.now("vS")
        x0rel3[sl3] = ("vS", vS_add)
        sem = "dXA" if ar == 0 else "dXB"
        x1dma.append((sem, p.op("sync", lambda s, qt=qt, ar=ar: s.dma_start(
            x1s.ap()[qt * 128:(qt + 1) * 128, :], attnrow[ar][:]),
            [("vS", vS_add)], sem, 16)))
        vS_stats = p.now("vS")

        def tail_all(qt=qt, ar=ar, sl3=sl3, vS_stats=vS_stats):
            p.op("scalar", lambda se: se.copy(sdum[:], onecol_f_s[:]),
                 [("vS", vS_stats)], "sS")
            p.op("scalar", lambda se: se.copy(sdum[:], onecol_f_s[:]), (), "sS")
            p.op("scalar", lambda se: se.activation(
                sdum4[:], stata4[:], AF.Identity, accum_out=stat_a[:]),
                (), "sS")
            p.op("scalar", lambda se: se.activation(
                sdum4b[:], statb4[:], AF.Identity, accum_out=stat_b[:]),
                (), "sS")
            p.op("scalar", lambda se: se.copy(sdum[:], onecol_f_s[:]), (), "sS")
            p.op("scalar", lambda se: se.copy(sdum[:], onecol_f_s[:]), (), "sS")
            p.op("scalar", lambda se: se.mul(
                stat_c[:], stat_a[:], 1.0 / E), (), "sS")
            p.op("scalar", lambda se: se.mul(
                stat_d[:], stat_b[:], 1.0 / E), (), "sS")
            p.op("scalar", lambda se: se.activation(
                stat_f[:], stat_c[:], AF.Square), (), "sS")
            p.op("scalar", lambda se: se.activation(
                stat_g[:], stat_f[:], AF.Identity, bias=stat_d[:], scale=-1.0),
                (), "sS")
            p.op("scalar", lambda se: se.activation(
                stat_h[:], stat_g[:], AF.Ln, bias=epscol_s[:]), (), "sS")
            p.op("scalar", lambda se: se.activation(
                stat_r[:], stat_h[:], AF.Exp, scale=-0.5), (), "sS")
            p.op("scalar", lambda se: se.mul(
                stat_c2[:], stat_c[:], -1.0), (), "sS")
            p.op("scalar", lambda se: se.activation(
                stat_e[:], stat_c2[:], AF.Identity, scale=stat_r[:]), (), "sS")
            p.op("scalar", lambda se, qt=qt: se.copy(
                nbias1_s[:, qt:qt + 1], stat_e[:]), (), "sS")
            p.op("scalar", lambda se, qt=qt: se.activation(
                rstdp1_s[:, qt:qt + 1], stat_r[:], AF.Identity,
                bias=onecol_f_s[:]), (), "sS")
            sS_chain = p.now("sS")
            p.op("vector", lambda v, ar=ar, qt=qt: v.tensor_scalar(
                y1[:, qt * E:(qt + 1) * E], attnrow[ar][:], stat_r[:],
                stat_e[:], OP.mult, OP.add), [("sS", sS_chain)], "vS")
            y1evac.append(("vS", p.now("vS")))
            attnrow_rel[ar] = x1dma[qt]
            p.op("vector", lambda v, qt=qt, sl3=sl3: v.scalar_tensor_tensor(
                x0tile[sl3][:], y1[:, qt * E:(qt + 1) * E], 1.0, WrB[:],
                OP.mult, OP.mult, accum_out=stat_f[:]),
                [("sS", sS_wrb)], "vS")
            x0rel3[sl3] = ("vS", p.now("vS"))
            p.op("vector", lambda v: v.tensor_copy(vdum[:], stat_f[:]), (), "vS")
            p.op("vector", lambda v, qt=qt: v.tensor_add(
                logits_s[:, qt:qt + 1], stat_f[:], brcol[:]), (), "vS")
        tail_defer.append(tail_all)
    flush_tail()
    tS_attn = p.now("tS")
    vS_attn = p.now("vS")
    sS_attn = p.now("sS")

    w1dma = []
    w1_rel = {}
    for slq in range(4):
        sem = f"dW1{slq}"
        w1dma.append((sem, p.op("gpsimd", lambda g, slq=slq: g.dma_start(
            w1ring[slq][:].rearrange("p (t f) -> p t f", t=16),
            w1t.ap()[:, slq * 512:(slq + 1) * 512].rearrange(
                "(t p) f -> p t f", p=128)), [("tS", tS_attn)], sem, 16)))

    p.op("tensor", lambda t: t.matmul(PS[0:16, 3072:3200], logits_s[:],
                                      i128f_s[:], start=True, stop=True),
         [("vS", vS_attn)] + bw(6), "tS")
    tS_lt = p.now("tS")
    vS_lr = p.op("vector", lambda v: v.tensor_copy(
        log16[:], PS[0:16, 3072:3200]), [("tS", tS_lt)], "vS")
    brel[6] = ("vS", vS_lr)
    dX_lr = p.op("sync", lambda s: s.dma_start(logrow[:], log16[:]),
                 [("vS", vS_lr)], "dLR", 16)
    for cch in range(4):
        bk_ = 7 if cch % 2 == 0 else 6
        p.op("tensor", lambda t, cch=cch, bk_=bk_: t.matmul(
            bank(bk_), onesrow_f_s[:],
            logrow[0:1, cch * 512:(cch + 1) * 512], start=True, stop=True),
            [("dLR", dX_lr)] + bw(bk_), "tS")
        tS_lb = p.now("tS")
        brel[bk_] = ("sS", p.op("scalar", lambda se, cch=cch, bk_=bk_: se.activation(
            LB[:, cch * 512:(cch + 1) * 512], bank(bk_), AF.Identity),
            [("tS", tS_lb)], "sS"))
    sS_lb = p.now("sS")
    p.op("vector", lambda v: v.memset(ones2k[:], 1.0), [("sS", sS_lb)], "vS")
    for st in range(1, NT, 2):
        p.op("vector", lambda v, st=st: v.scalar_tensor_tensor(
            gdump[:], LB[:], logits_s[:, st:st + 1], ones2k[:],
            OP.is_gt, OP.mult, accum_out=cg_s[:, st:st + 1]), (), "vS")
    evens = list(range(0, NT, 2))
    p.op("scalar", lambda se: se.mul(hkcol[:], onecol_f_s[:], 1023.5),
         [("vS", vS_attn)], "sS")
    p.op("scalar", lambda se: se.mul(
        nlog2[:, 0:1], logits_s[:, evens[0]:evens[0] + 1], -1.0), (), "sS")
    p.op("scalar", lambda se: se.mul(
        nlog2[:, 1:2], logits_s[:, evens[1]:evens[1] + 1], -1.0), (), "sS")
    for i, st in enumerate(evens):
        pr = i % 2
        p.op("scalar", lambda se, pr=pr: se.activation(
            tstage[:], LB[:], AF.Sign, bias=nlog2[:, pr:pr + 1],
            accum_out=cgalt2[:, pr:pr + 1]), (), "sS")
        if i + 2 < len(evens):
            p.op("scalar", lambda se, i=i: se.mul(
                nlog2[:, i % 2:i % 2 + 1],
                logits_s[:, evens[i + 2]:evens[i + 2] + 1], -1.0), (), "sS")
        else:
            p.op("scalar", lambda se: se.copy(sdum[:], onecol_f_s[:]),
                 (), "sS")
        if i >= 1:
            p.op("scalar", lambda se, i=i: se.activation(
                cg_s[:, evens[i - 1]:evens[i - 1] + 1],
                cgalt2[:, (i - 1) % 2:(i - 1) % 2 + 1], AF.Identity,
                bias=hkcol[:], scale=0.5), (), "sS")
    p.op("scalar", lambda se: se.copy(sdum[:], onecol_f_s[:]), (), "sS")
    p.op("scalar", lambda se: se.copy(sdum[:], onecol_f_s[:]), (), "sS")
    p.op("scalar", lambda se: se.activation(
        cg_s[:, evens[-1]:evens[-1] + 1], cgalt2[:, 1:2], AF.Identity,
        bias=hkcol[:], scale=0.5), (), "sS")
    sS_cnt = p.now("sS")
    p.op("vector", lambda v: v.tensor_copy(vdum[:], cg_s[:, 0:1]),
         [("sS", sS_cnt)], "vS")
    p.op("vector", lambda v: v.tensor_copy(vdum[:], cg_s[:, 0:1]), (), "vS")
    p.op("vector", lambda v: v.tensor_scalar(
        m_s[:], cg_s[:], float(KSEL), None, OP.is_lt), (), "vS", serial=True)
    p.op("vector", lambda v: v.tensor_copy(mb_s[:], m_s[:]), (), "vS", serial=True)
    vS_m = p.now("vS")
    dbg_dump("dbg_cg", cg_s[:], [("vS", vS_m)], f32src=True)
    dbg_dump("dbg_m", m_s[:], [("vS", vS_m)], f32src=True)
    dbg_dump("dbg_logits", logits_s[:], [("vS", vS_m)], f32src=True)
    p.op("tensor", lambda t: t.matmul(PS[0:16, 3584:3585], mb_s[:],
                                      onescol_b_s[:], start=True, stop=True),
         [("vS", vS_m)] + bw(7), "tS", serial=True)
    tS_c1 = p.now("tS")
    p.op("vector", lambda v: v.tensor_copy(ts16_s[:], PS[0:16, 3584:3585]),
         [("tS", tS_c1)], "vS", serial=True)
    vS_c1 = p.now("vS")
    p.op("tensor", lambda t: t.matmul(PS[0:16, 3712:3713], triu16s_s[:],
                                      ts16_s[:], start=True, stop=True),
         [("vS", vS_c1)], "tS", serial=True)
    tS_c2 = p.now("tS")
    p.op("vector", lambda v: v.tensor_copy(offs_s[:], PS[0:16, 3712:3713]),
         [("tS", tS_c2)], "vS", serial=True)
    vS_c2 = p.now("vS")
    p.op("tensor", lambda t: t.matmul(PS[0:1, 3840:3856], offs_s[:],
                                      i128b_s[0:16, 0:16], start=True,
                                      stop=True), [("vS", vS_c2)], "tS",
         serial=True)
    tS_c3 = p.now("tS")
    p.op("vector", lambda v: v.tensor_copy(offsrow_s[:], PS[0:1, 3840:3856]),
         [("tS", tS_c3)], "vS", serial=True)
    vS_c3 = p.now("vS")
    p.op("tensor", lambda t: t.matmul(PS[:, 3328:3344], triu128s_s[:],
                                      mb_s[:], start=True, stop=False),
         [("vS", vS_c3)], "tS", serial=True)
    p.op("tensor", lambda t: t.matmul(PS[:, 3328:3344], onesrow_b_s[:],
                                      offsrow_s[:], start=False, stop=True),
         (), "tS", serial=True)
    tS_pos = p.now("tS")
    p.op("vector", lambda v: v.tensor_scalar(
        t1_s[:], m_s[:], -BIG, BIG, OP.mult, OP.add), [("tS", tS_pos)], "vS", serial=True)
    p.op("vector", lambda v: v.tensor_copy(vdum[:], t1_s[:, 0:1]), (), "vS")
    p.op("vector", lambda v: v.tensor_copy(vdum[:], t1_s[:, 1:2]), (), "vS")
    p.op("vector", lambda v: v.tensor_add(q_s[:], t1_s[:], PS[:, 3328:3344]),
         (), "vS", serial=True)
    p.op("vector", lambda v: v.tensor_copy(vdum[:], q_s[:, 0:1]), (), "vS")
    brel[7] = ("vS", p.now("vS"))
    vS_q = p.now("vS")
    dbg_dump("dbg_q", q_s[:], [("vS", vS_q)], f32src=True)
    vS_ptc = []
    for st in range(NT):
        p.op("vector", lambda v, st=st: v.tensor_scalar(
            PT[:, st * 256:(st + 1) * 256], iotab_s[:], q_s[:, st:st + 1],
            None, OP.is_equal), (), "vS", serial=True)
        vS_ptc.append(p.now("vS"))
    vS_pt = p.now("vS")
    fx_evac = []
    for et in range(NE):
        bk_ = et % 2
        for st in range(NT):
            waits = ()
            if et == 0:
                waits = [("vS", vS_ptc[st])] + (bw(bk_) if st == 0 else [])
            elif st == 0:
                waits = bw(bk_)
            p.op("tensor", lambda t, et=et, st=st, bk_=bk_: t.matmul(
                PS[:, bk_ * 512:bk_ * 512 + 256],
                y1[:, st * E + et * 128: st * E + (et + 1) * 128],
                PT[:, st * 256:(st + 1) * 256],
                start=(st == 0), stop=(st == NT - 1)), waits, "tS")
        tS_g = p.now("tS")
        brel[bk_] = ("sS", p.op("scalar", lambda se, et=et, bk_=bk_: se.activation(
            fxT[:, et * 256:(et + 1) * 256], PS[:, bk_ * 512:bk_ * 512 + 256],
            AF.Identity), [("tS", tS_g)], "sS"))
        fx_evac.append(p.now("sS"))
    sS_fx = p.now("sS")
    tS_gather = p.now("tS")
    psel = {}
    vS_P_l = [None]

    def emit_postsel():
        p.op("vector", lambda v: v.tensor_scalar(
            lm_s[:], m_s[:], BIG, -BIG, OP.mult, OP.add), [("vS", vS_m)], "vS", serial=True)
        p.op("vector", lambda v: v.tensor_copy(vdum[:], lm_s[:, 0:1]), (), "vS")
        p.op("vector", lambda v: v.tensor_add(lm2_s[:], lm_s[:], logits_s[:]),
             (), "vS", serial=True)
        p.op("vector", lambda v: v.tensor_copy(vdum[:], lm2_s[:, 0:1]), (), "vS")
        p.op("vector", lambda v: v.tensor_reduce(mx1_s[:], lm2_s[:], AX, OP.max),
             (), "vS", serial=True)
        vS_mx = p.now("vS")
        p.op("tensor", lambda t: t.matmul(PS[0:1, 3072:3200], mx1_s[:],
                                          i128f_s[:], start=True, stop=True),
             [("vS", vS_mx)] + bw(6), "tS")
        tS_mx = p.now("tS")
        p.op("vector", lambda v: v.tensor_reduce(
            negmx_s[0:1, :], PS[0:1, 3072:3200], AX, OP.max), [("tS", tS_mx)], "vS")
        vS_mm = p.now("vS")
        p.op("tensor", lambda t: t.matmul(PS[:, 3200:3201], onesrow_f_s[:],
                                          negmx_s[0:1, :], start=True, stop=True),
             [("vS", vS_mm)], "tS")
        tS_mb = p.now("tS")
        p.op("scalar", lambda se: se.mul(negmx_s[:], PS[:, 3200:3201], -1.0),
             [("tS", tS_mb)], "sS", serial=True)
        brel[6] = ("sS", p.now("sS"))
        p.op("scalar", lambda se: se.copy(sdum[:], negmx_s[:]), (), "sS")
        p.op("scalar", lambda se: se.activation(
            wu_s[:], lm2_s[:], AF.Exp, bias=negmx_s[:]), (), "sS", serial=True)
        sS_wu = p.now("sS")
        p.op("vector", lambda v: v.tensor_reduce(zc_s[:], wu_s[:], AX, OP.add),
             [("sS", sS_wu)], "vS")
        vS_z1 = p.now("vS")
        p.op("tensor", lambda t: t.matmul(PS[0:1, 3072:3200], zc_s[:],
                                          i128f_s[:], start=True, stop=True),
             [("vS", vS_z1)] + bw(6), "tS")
        tS_z = p.now("tS")
        p.op("vector", lambda v: v.tensor_reduce(
            zc_s[0:1, :], PS[0:1, 3072:3200], AX, OP.add), [("tS", tS_z)], "vS", serial=True)
        p.op("vector", lambda v: v.tensor_copy(vdum[:], cg_s[:, 0:1]), (), "vS")
        p.op("vector", lambda v: v.tensor_copy(vdum[:], cg_s[:, 1:2]), (), "vS")
        p.op("vector", lambda v: v.reciprocal(rz_s[:], zc_s[0:1, :]), (), "vS", serial=True)
        vS_rzz = p.now("vS")
        p.op("tensor", lambda t: t.matmul(PS[:, 3200:3201], onesrow_f_s[:],
                                          rz_s[:], start=True, stop=True),
             [("vS", vS_rzz)], "tS")
        tS_rb = p.now("tS")
        p.op("vector", lambda v: v.tensor_copy(rzc_s[:], PS[:, 3200:3201]),
             [("tS", tS_rb)], "vS", serial=True)
        brel[6] = ("vS", p.now("vS"))
        p.op("vector", lambda v: v.tensor_copy(vdum[:], rzc_s[:]), (), "vS")
        p.op("vector", lambda v: v.tensor_scalar(
            w_s[:], wu_s[:], rzc_s[:], None, OP.mult), (), "vS", serial=True)
        p.op("vector", lambda v: v.tensor_copy(vdum[:], w_s[:, 0:1]), (), "vS")
        p.op("vector", lambda v: v.tensor_copy(vdum[:], w_s[:, 1:2]), (), "vS")
        p.op("vector", lambda v: v.tensor_copy(wb_s[:], w_s[:]), (), "vS", serial=True)
        vS_w = p.now("vS")
        dbg_dump("dbg_w", w_s[:], [("vS", vS_w)], f32src=True)
        for ih in range(2):
            for sg in range(4):
                bk_ = 6 if (ih * 4 + sg) % 2 == 0 else 7
                for j in range(4):
                    st = sg * 4 + j
                    twaits = ([("vS", vS_pt)] + bw(bk_)) if j == 0 else ()
                    p.op("tensor", lambda t, st=st, ih=ih, bk_=bk_, j=j: t.matmul(
                        PS[:, bk_ * 512 + j * 128: bk_ * 512 + (j + 1) * 128],
                        PT[:, st * 256 + ih * 128: st * 256 + ih * 128 + 128],
                        i128b_s[:], start=True, stop=True), twaits, "tS")
                tS_p = p.now("tS")
                brel[bk_] = ("vS", p.op(
                    "vector", lambda v, ih=ih, sg=sg, bk_=bk_: v.tensor_copy(
                        Psb[:, ih * S + sg * 512:(ih * S + (sg + 1) * 512)],
                        bank(bk_)), [("tS", tS_p)], "vS"))
        vS_P_l[0] = ("vS", p.now("vS"))
        for ih in range(2):
            bk_ = 2 + ih
            for st in range(NT):
                waits = ([("vS", vS_w)] + bw(bk_)) if st == 0 else ()
                p.op("tensor", lambda t, ih=ih, st=st, bk_=bk_: t.matmul(
                    PS[:, bk_ * 512:bk_ * 512 + 1],
                    PT[:, st * 256 + ih * 128: st * 256 + ih * 128 + 128],
                    wb_s[:, st:st + 1], start=(st == 0), stop=(st == NT - 1)),
                    waits, "tS")
            tS_wg = p.now("tS")
            brel[bk_] = ("sS", p.op("scalar", lambda se, ih=ih, bk_=bk_: se.copy(
                wg_s[:, ih:ih + 1], PS[:, bk_ * 512:bk_ * 512 + 1]),
                [("tS", tS_wg)], "sS"))

        psel["vS_wg"] = ("sS", p.now("sS"))
        psel["vS_P"] = vS_P_l[0]
        dbg_dump("dbg_wg", wg_s[:], [psel["vS_wg"]], f32src=True)

    dbg_dump("dbg_fxT", fxT[:], [("sS", sS_fx)])
    dbg_dump("dbg_y1", y1[:, 0:E], [("sS", sS_fx)])

    p.op("gpsimd", lambda g: g.dma_start(
        b1row[:], b1t.ap().rearrange("(a f) -> a f", a=1)),
        [("tS", tS_gather)], "dB1", 16)
    dW_b1 = p.now("dB1")
    hT_evac = []
    for slq in range(16):
        rsl = slq % 4
        if slq >= 4:
            sem = f"dW1{rsl}"
            w1dma.append((sem, p.op("gpsimd", lambda g, slq=slq, rsl=rsl: g.dma_start(
                w1ring[rsl][:].rearrange("p (t f) -> p t f", t=16),
                w1t.ap()[:, slq * 512:(slq + 1) * 512].rearrange(
                    "(t p) f -> p t f", p=128)),
                [w1_rel[rsl]], sem, 16)))
        hb = [0, 1] if slq % 2 == 0 else [2, 3]
        for et in range(NE):
            for ih in range(2):
                bk_ = hb[ih]
                waits = ()
                if et == 0:
                    waits = [w1dma[slq], ("sS", sS_fx),
                             ("dB1", dW_b1)] + bw(bk_)
                p.op("tensor", lambda t, slq=slq, rsl=rsl, et=et, ih=ih,
                     bk_=bk_: t.matmul(
                         bank(bk_),
                         fxT[:, et * 256 + ih * 128: et * 256 + ih * 128 + 128],
                         w1ring[rsl][:, et * 512:(et + 1) * 512],
                         start=(et == 0), stop=False), waits, "tS")
        for ih in range(2):
            p.op("tensor", lambda t, slq=slq, ih=ih, bk_=hb[ih]: t.matmul(
                bank(bk_), onesrow_b_s[:],
                b1row[0:1, slq * 512:(slq + 1) * 512],
                start=False, stop=True), (), "tS")
        w1_rel[rsl] = ("tS", p.now("tS"))
        tS_h = p.now("tS")
        for ih in range(2):
            bk_ = hb[ih]
            hsl = 2 * (slq % 2) + ih
            brel[bk_] = ("sS", p.op(
                "scalar", lambda se, hsl=hsl, bk_=bk_: se.activation(
                    htile[hsl][:], bank(bk_), AF.Gelu), [("tS", tS_h)], "sS"))
            sS_g = p.now("sS")
            tbk = 4 if slq % 2 == 0 else 5
            for j in range(4):
                twaits = ([("sS", sS_g)] + bw(tbk)) if j == 0 else ()
                p.op("tensor", lambda t, hsl=hsl, j=j, tbk=tbk: t.matmul(
                    PS[:, tbk * 512 + j * 128: tbk * 512 + (j + 1) * 128],
                    htile[hsl][:, j * 128:(j + 1) * 128], i128b_s[:],
                    start=True, stop=True), twaits, "tS")
            tS_t = p.now("tS")
            def hev(v, slq=slq, ih=ih, tbk=tbk):
                dst = hT[:, slq * 1024: (slq + 1) * 1024].rearrange(
                    "p (j c) -> p j c", j=4)[:, :, ih * 128: ih * 128 + 128]
                return v.tensor_copy(dst, bank(tbk))
            brel[tbk] = ("vS", p.op("vector", hev, [("tS", tS_t)], "vS"))
            hT_evac.append(p.now("vS"))
        if slq == 0:
            emit_postsel()
    vS_hT = p.now("vS")
    tS_W1 = p.now("tS")
    dbg_dump("dbg_hT", hT[:, 0:2048], [("vS", vS_hT)])

    p.op("gpsimd", lambda g: g.dma_start(
        b2row[:], b2t.ap().rearrange("(a f) -> a f", a=1)),
        [x1dma[NT - 1], y1evac[NT - 1]], "dB2", 16)
    dW_b2 = p.now("dB2")
    w2dma = []
    w2_rel = {}
    w2_first = [("tS", tS_gather)] * 4 + [("tS", tS_W1)] * 3
    def emit_w2dma(ch):
        rsl = ch % 7
        waits = [w2_first[rsl]] if ch < 7 else [w2_rel[rsl]]
        sem = f"dW2{rsl}"
        w2dma.append((sem, p.op("gpsimd", lambda g, ch=ch, rsl=rsl: g.dma_start(
            w2ring[rsl][:].rearrange("p (j e) -> p j e", j=2),
            w2t.ap()[ch * 256:(ch + 1) * 256, :].rearrange(
                "(j p) e -> p j e", p=128)), waits, sem, 16)))
    for ch in range(7):
        emit_w2dma(ch)
    for ch in range(32):
        rsl = ch % 7
        for j in range(2):
            ft = ch * 2 + j
            for ih in range(2):
                for ec in range(4):
                    bk_ = ih * 4 + ec
                    waits = ()
                    if ft == 0 and ih == 0 and ec == 0:
                        waits = [w2dma[ch], ("vS", vS_hT)]
                        for bb in range(8):
                            waits += bw(bb)
                    elif ih == 0 and ec == 0:
                        waits = [w2dma[ch]]
                    p.op("tensor", lambda t, ft=ft, j=j, ih=ih, ec=ec, rsl=rsl,
                         bk_=bk_: t.matmul(
                             bank(bk_),
                             hT[:, ft * 256 + ih * 128: ft * 256 + ih * 128 + 128],
                             w2ring[rsl][:, j * E + ec * 512: j * E + (ec + 1) * 512],
                             start=(ft == 0), stop=False), waits, "tS")
        w2_rel[rsl] = ("tS", p.now("tS"))
        if ch + 7 < 32:
            emit_w2dma(ch + 7)
    for ih in range(2):
        for ec in range(4):
            bk_ = ih * 4 + ec
            p.op("tensor", lambda t, ec=ec, bk_=bk_: t.matmul(
                bank(bk_), onesrow_b_s[:], b2row[0:1, ec * 512:(ec + 1) * 512],
                start=False, stop=True), [("dB2", dW_b2)] if bk_ == 0 else (),
                "tS")
    tS_W2 = p.now("tS")
    for ih in range(2):
        for ec in range(4):
            bk_ = ih * 4 + ec
            brel[bk_] = ("sS", p.op(
                "scalar", lambda se, ih=ih, ec=ec, bk_=bk_: se.activation(
                    xwsb[ih][:, ec * 512:(ec + 1) * 512], bank(bk_),
                    AF.Identity, scale=wg_s[:, ih:ih + 1]),
                [("tS", tS_W2), psel["vS_wg"]] if bk_ == 0 else (), "sS"))
    sS_xw = p.now("sS")
    dbg_dump("dbg_xw", xwsb[0][:], [("sS", sS_xw)])

    x1dmar = []
    x1r_rel = {}
    def emit_x1dma(st):
        rsl = st % 6
        waits = [("dXA", p.now("dXA")), ("dXB", p.now("dXB")),
                 ("tS", tS_W1)] if st < 6 else [x1r_rel[rsl]]
        sem = f"dR{rsl}"
        x1dmar.append((sem, p.op("sync", lambda s, st=st, rsl=rsl: s.dma_start(
            x1re[rsl][:], x1s.ap()[st * 128:(st + 1) * 128, :]), waits,
            sem, 16)))
    for st in range(6):
        emit_x1dma(st)
    out_rel = [None, None]
    ts_rel = None
    for st in range(NT):
        rsl = st % 6
        osl = st % 2
        bks = [0, 1, 2, 3] if st % 2 == 0 else [4, 5, 6, 7]
        for ec in range(4):
            bk_ = bks[ec]
            for ih in range(2):
                waits = ()
                if ec == 0 and ih == 0:
                    waits = [("sS", sS_xw), psel["vS_P"]]
                    for bb in bks:
                        waits += bw(bb)
                p.op("tensor", lambda t, st=st, ec=ec, ih=ih, bk_=bk_: t.matmul(
                    bank(bk_), Psb[:, ih * S + st * 128: ih * S + (st + 1) * 128],
                    xwsb[ih][:, ec * 512:(ec + 1) * 512],
                    start=(ih == 0), stop=(ih == 1)), waits, "tS")
        tS_d = p.now("tS")
        twaits = [x1dmar[st]]
        if ts_rel is not None:
            twaits.append(ts_rel)
        p.op("scalar", lambda se, st=st, rsl=rsl: se.activation(
            tstage[:], x1re[rsl][:], AF.Identity,
            bias=nbias1_s[:, st:st + 1], scale=rstdp1_s[:, st:st + 1]),
            twaits, "sS")
        sS_t = p.now("sS")
        x1r_rel[rsl] = ("sS", sS_t)
        if st + 6 < NT:
            emit_x1dma(st + 6)
        ovw = [("sS", sS_t), ("tS", tS_d)]
        if out_rel[osl] is not None:
            ovw.append(out_rel[osl])
        vv = None
        for ec in range(4):
            bk_ = bks[ec]
            vv = p.op("vector", lambda v, osl=osl, ec=ec, bk_=bk_: v.tensor_add(
                outstage[osl][:, ec * 512:(ec + 1) * 512],
                tstage[:, ec * 512:(ec + 1) * 512], bank(bk_)),
                ovw if ec == 0 else (), "vS")
            brel[bk_] = ("vS", vv)
        ts_rel = ("vS", vv)
        sem = f"dO{osl}"
        oc = p.op("sync", lambda s, st=st, osl=osl: s.dma_start(
            outt.ap()[st * 128:(st + 1) * 128, :], outstage[osl][:]),
            [("vS", vv)], sem, 16)
        out_rel[osl] = (sem, oc)
    fin = [("dO0", p.now("dO0")), ("dO1", p.now("dO1"))]
    if debug:
        fin.append(("dD", p.now("dD")))
    p.op("sync", lambda s: s.nop(), fin)

    for e in Plan.ENGINES:
        for fn, w, inc, n, _srl in p.ops[e]:
            for snm, v in w:
                assert v <= p.now(snm), (e, snm, v, p.now(snm))


    semnames = set(p.cnt)
    for e in Plan.ENGINES:
        for fn, w, inc, n, _srl in p.ops[e]:
            for snm, v in w:
                semnames.add(snm)
    from contextlib import ExitStack
    with ExitStack() as ctx:
        sems = {s: ctx.enter_context(nc.semaphore(s))
                for s in sorted(semnames)}
        blk = ctx.enter_context(nc.Block())

        def mk(engname):
            def body(eng):
                lw = {}
                cnt = {}
                for fn, waits, inc, inc_n, srl in p.ops[engname]:
                    for snm, v in waits:
                        if lw.get(snm, 0) < v:
                            eng.wait_ge(sems[snm], v)
                            lw[snm] = v
                    instr = fn(eng)
                    if inc is not None:
                        cnt[inc] = cnt.get(inc, 0) + inc_n
                        if instr is None:
                            eng.sem_inc(sems[inc], inc_n)
                        else:
                            instr.then_inc(sems[inc], inc_n)
                        if srl:
                            eng.wait_ge(sems[inc], cnt[inc])
                            lw[inc] = cnt[inc]
            return body
        blk.tensor(mk("tensor"))
        blk.vector(mk("vector"))
        blk.scalar(mk("scalar"))
        blk.sync(mk("sync"))
        blk.gpsimd(mk("gpsimd"))
    return nc


_NC_CACHE = {}


def _get_nc(debug=False):
    if debug not in _NC_CACHE:
        _NC_CACHE[debug] = build(debug=debug)
    return _NC_CACHE[debug]


def kernel(L0_attn_x, L0_attn_Wq, L0_attn_bq, L0_attn_Wk, L0_attn_bk,
           L0_attn_Wv, L0_attn_bv, L0_ffn_Wr, L0_ffn_br, L0_ffn_W1,
           L0_ffn_b1, L0_ffn_W2, L0_ffn_b2):
    f = lambda a: np.ascontiguousarray(np.asarray(a, dtype=np.float32))
    shared = {
        "wq": f(L0_attn_Wq), "bq": f(L0_attn_bq), "wk": f(L0_attn_Wk),
        "bk": f(L0_attn_bk), "wv": f(L0_attn_Wv), "bv": f(L0_attn_bv),
        "wr": f(L0_ffn_Wr), "br": f(L0_ffn_br), "w1": f(L0_ffn_W1),
        "b1": f(L0_ffn_b1), "w2": f(L0_ffn_W2), "b2": f(L0_ffn_b2),
    }
    x = f(L0_attn_x)
    nc = _get_nc(False)
    in_maps = [dict(shared, x0=np.ascontiguousarray(x[b])) for b in range(8)]
    res = run_bass_kernel_spmd(nc, in_maps, core_ids=list(range(8)))
    return np.stack([r["out"] for r in res.results], axis=0)


# version 24
# speedup vs baseline: 1.0017x; 1.0017x over previous
import numpy as np
import ml_dtypes
import concourse.bass as bass
import concourse.mybir as mybir
from concourse.bass_utils import run_bass_kernel_spmd

F32 = mybir.dt.float32
BF16 = mybir.dt.bfloat16
OP = mybir.AluOpType
AF = mybir.ActivationFunctionType
AX = mybir.AxisListType.X
BF = ml_dtypes.bfloat16

S, E, H, G, D = 2048, 2048, 16, 4, 128
NT, NE = 16, 16
KSEL = 256
DFF = 8192
SCL = float(np.float32(1.0) / np.sqrt(np.float32(D)))
EPS = 1e-5
BIG = 1.0e6


class Plan:
    ENGINES = ("tensor", "vector", "scalar", "sync", "gpsimd")

    def __init__(self):
        self.ops = {e: [] for e in self.ENGINES}
        self.cnt = {}

    def now(self, sem):
        return self.cnt.get(sem, 0)

    def op(self, eng, fn, waits=(), inc=None, inc_n=1, serial=False):
        w = []
        for s, v in waits:
            assert v <= self.now(s), f"forward wait {s}>={v} (now {self.now(s)})"
            if v > 0:
                w.append((s, v))
        self.ops[eng].append((fn, w, inc, inc_n, serial))
        if inc is not None:
            self.cnt[inc] = self.cnt.get(inc, 0) + inc_n
        return self.cnt.get(inc, 0)


def _host_consts():
    i128b = np.eye(128, dtype=BF)
    i128f = np.eye(128, dtype=np.float32)
    triu128s = np.triu(np.ones((128, 128), dtype=np.float32), 1).astype(BF)
    triu16s = np.triu(np.ones((16, 16), dtype=np.float32), 1).astype(BF)
    m = np.tril(np.ones((128, 128), dtype=np.float32), -1) * (-1200.0)
    mask512 = np.tile(m, (1, 4)).astype(BF)
    iotab = np.broadcast_to(np.arange(KSEL, dtype=np.float32), (128, KSEL)).astype(BF)
    onesrow_b = np.ones((1, 128), dtype=BF)
    onesrow_f = np.ones((1, 128), dtype=np.float32)
    onescol_b = np.ones((128, 1), dtype=BF)
    epscol = np.full((128, 1), EPS, dtype=np.float32)
    onecol_f = np.ones((128, 1), dtype=np.float32)
    return dict(epscol=epscol, onecol_f=onecol_f, i128b=i128b, i128f=i128f, triu128s=triu128s, triu16s=triu16s,
                mask512=mask512, iotab=iotab, onesrow_b=onesrow_b,
                onesrow_f=onesrow_f, onescol_b=onescol_b)


def build(debug=False):
    nc = bass.Bass("TRN2", target_bir_lowering=False,
                   detect_race_conditions=False)
    p = Plan()

    x0t = nc.dram_tensor("x0", [S, E], F32, kind="ExternalInput")
    wqt = nc.dram_tensor("wq", [H, E, D], F32, kind="ExternalInput")
    bqt = nc.dram_tensor("bq", [H, D], F32, kind="ExternalInput")
    wkt = nc.dram_tensor("wk", [G, E, D], F32, kind="ExternalInput")
    bkt = nc.dram_tensor("bk", [G, D], F32, kind="ExternalInput")
    wvt = nc.dram_tensor("wv", [G, E, D], F32, kind="ExternalInput")
    bvt = nc.dram_tensor("bv", [G, D], F32, kind="ExternalInput")
    wrt = nc.dram_tensor("wr", [E, 1], F32, kind="ExternalInput")
    brt = nc.dram_tensor("br", [1], F32, kind="ExternalInput")
    w1t = nc.dram_tensor("w1", [E, DFF], F32, kind="ExternalInput")
    b1t = nc.dram_tensor("b1", [DFF], F32, kind="ExternalInput")
    w2t = nc.dram_tensor("w2", [DFF, E], F32, kind="ExternalInput")
    b2t = nc.dram_tensor("b2", [E], F32, kind="ExternalInput")
    outt = nc.dram_tensor("out", [S, E], F32, kind="ExternalOutput")
    x1s = nc.dram_tensor("x1s", [S, E], F32,
                         kind="ExternalOutput" if debug else "Internal")
    dbgt = {}
    if debug:
        for nm, sh in [("dbg_logits", [128, 16]), ("dbg_m", [128, 16]),
                       ("dbg_q", [128, 16]), ("dbg_w", [128, 16]),
                       ("dbg_cg", [128, 16]),
                       ("dbg_fxT", [128, NE * 256]), ("dbg_y0T", [128, S]),
                       ("dbg_kt", [128, S]), ("dbg_qh", [128, S]),
                       ("dbg_hT", [128, 2048]), ("dbg_xw", [128, E]),
                       ("dbg_y1", [128, E]), ("dbg_wg", [128, 2]),
                       ("dbg_rp", [128, 16]), ("dbg_y1T", [128, E]),
                       ("dbg_sums", [128, 16]), ("dbg_sumsq", [128, 16])]:
            dbgt[nm] = nc.dram_tensor(nm, sh, F32, kind="ExternalOutput")

    cn = _host_consts()
    inl = {k: nc.inline_tensor(v, k) for k, v in cn.items()}

    A_ = lambda name, shape, dt, off: nc.alloc_sbuf_tensor_at(
        name, shape, dt, offset=off)
    o_const = 16512
    o_M = 23936
    o_F = 28032
    o_xw = 23936
    o_A = 32128
    o_B = 97664
    o_C = 163200
    o_D = 196480
    o_E = 204672

    off = o_const
    def calloc(name, shape, dt):
        nonlocal off
        nb = (shape[1] * mybir.dt.size(dt) + 31) // 32 * 32
        t = A_(name, shape, dt, off)
        off += nb
        return t
    i128b_s = calloc("i128b", [128, 128], BF16)
    triu128s_s = calloc("triu128s", [128, 128], BF16)
    mask512_s = calloc("mask512", [128, 512], BF16)
    iotab_s = calloc("iotab", [128, KSEL], BF16)
    i128f_s = calloc("i128f", [128, 128], F32)
    triu16s_s = calloc("triu16s", [16, 16], BF16)
    onesrow_b_s = calloc("onesrow_b", [1, 128], BF16)
    onesrow_f_s = calloc("onesrow_f", [1, 128], F32)
    onescol_b_s = calloc("onescol_b", [128, 1], BF16)
    epscol_s = calloc("epscol", [128, 1], F32)
    onecol_f_s = calloc("onecol_f", [128, 1], F32)
    logits_s = calloc("logits", [128, 16], F32)
    cg_s = calloc("cg", [128, 16], F32)
    m_s = calloc("msel", [128, 16], F32)
    mb_s = calloc("mb", [128, 16], BF16)
    q_s = calloc("qsel", [128, 16], F32)
    t1_s = calloc("t1", [128, 16], F32)
    lm_s = calloc("lm", [128, 16], F32)
    wu_s = calloc("wu", [128, 16], F32)
    w_s = calloc("wsel", [128, 16], F32)
    wb_s = calloc("wb", [128, 16], BF16)
    wg_s = calloc("wg", [128, 2], F32)
    rstdp1_s = calloc("rstdp1", [128, 16], F32)
    nbias1_s = calloc("nbias1", [128, 16], F32)
    bqS = calloc("bqS", [128, 16], F32)
    bkS = calloc("bkS", [128, 4], F32)
    wrS = calloc("wrS", [128, 16], BF16)
    brcol = calloc("brcol", [128, 1], F32)
    bv_row = calloc("bv_row", [1, 512], BF16)
    diagstage = A_("diagstage", [128, 512], BF16,
                   nc.lookup_mloc(bv_row).addr)
    stat_a = calloc("stat_a", [128, 1], F32)
    stat_b = calloc("stat_b", [128, 1], F32)
    stat_c = calloc("stat_c", [128, 1], F32)
    stat_d = calloc("stat_d", [128, 1], F32)
    stat_e = calloc("stat_e", [128, 1], F32)
    stat_f = calloc("stat_f", [128, 1], F32)
    stat_g = calloc("stat_g", [128, 1], F32)
    stat_h = calloc("stat_h", [128, 1], F32)
    stat_r = calloc("stat_r", [128, 1], F32)
    stat_c2 = calloc("stat_c2", [128, 1], F32)
    stata4 = calloc("stata4", [128, 4], F32)
    statb4 = calloc("statb4", [128, 4], F32)
    sdum4 = calloc("sdum4", [128, 4], F32)
    sdum4b = calloc("sdum4b", [128, 4], F32)
    stat_a2 = calloc("stat_a2", [128, 1], F32)
    stat_b2 = calloc("stat_b2", [128, 1], F32)
    stat_e2 = calloc("stat_e2", [128, 1], F32)
    stat_r2 = calloc("stat_r2", [128, 1], F32)
    stat_a3 = calloc("stat_a3", [128, 1], F32)
    stat_b3 = calloc("stat_b3", [128, 1], F32)
    stat_e3 = calloc("stat_e3", [128, 1], F32)
    stat_r3 = calloc("stat_r3", [128, 1], F32)
    stat_a4 = calloc("stat_a4", [128, 1], F32)
    stat_b4 = calloc("stat_b4", [128, 1], F32)
    stat_e4 = calloc("stat_e4", [128, 1], F32)
    stat_r4 = calloc("stat_r4", [128, 1], F32)
    sdum = calloc("sdum", [128, 1], F32)
    hkcol = calloc("hkcol", [128, 1], F32)
    nlog2 = calloc("nlog2", [128, 2], F32)
    cgalt2 = calloc("cgalt2", [128, 2], F32)
    vdum = calloc("vdum", [128, 1], F32)
    lm2_s = calloc("lm2", [128, 16], F32)
    dbg_sums = calloc("dbg_sums_sb", [128, 16], F32)
    dbg_sumsq = calloc("dbg_sumsq_sb", [128, 16], F32)
    recipz = calloc("recipz", [128, 4], F32)
    mx1_s = calloc("mx1", [128, 1], F32)
    negmx_s = calloc("negmx", [128, 1], F32)
    zc_s = calloc("zc", [128, 1], F32)
    rz_s = calloc("rz", [1, 1], F32)
    rzc_s = calloc("rzc", [128, 1], F32)
    ts16_s = calloc("ts16", [16, 1], BF16)
    offs_s = calloc("offs", [16, 1], BF16)
    offsrow_s = calloc("offsrow", [1, 16], BF16)
    assert off <= o_M, f"consts overflow: {off} > {o_M}"

    y0T = A_("y0T", [128, NE * S], BF16, o_A)
    y1 = A_("y1", [128, NT * E], BF16, o_A)
    hT = A_("hT", [128, 64 * 256], BF16, o_A)
    Qall = A_("Qall", [128, H * S], BF16, o_B)
    wkS = A_("wkS", [128, G * E], BF16, o_B)
    wvS = A_("wvS", [128, G * E], BF16, o_B + 16384)
    w1ring = [A_(f"w1r{i}", [128, NE * 512], BF16, o_B + i * 16384)
              for i in range(4)]
    x1re = [A_(f"x1re{i}", [128, E], F32, o_B + i * 8192) for i in range(6)]
    outstage = [A_(f"outst{i}", [128, E], F32, o_B + 49152 + i * 8192)
                for i in range(2)]
    tstage = A_("tstage", [128, E], F32, o_E)
    KT = A_("KT", [128, G * S], BF16, o_C)
    Vs = A_("Vs", [128, G * 16 * 132], BF16, o_C + 16384)
    fxT = A_("fxT", [128, NE * 256], BF16, o_C)
    logrow = A_("logrow", [1, S], F32, o_C)
    LB = A_("LB", [128, S], F32, o_C + 8192)
    b1row = A_("b1row", [1, DFF], BF16, o_C + 8192)
    PT = A_("PT", [128, NT * 256], BF16, o_E + 8192)
    gdump = A_("gdump", [128, S], F32, o_C + 16384)
    log16 = A_("log16", [16, 128], F32, o_C + 16384)
    Psb = A_("Psb", [128, 2 * S], BF16, o_C + 24576)
    ones2k = A_("ones2k", [128, S], F32, o_C + 24576)
    w2ring = ([A_(f"w2ra{i}", [128, 2 * E], BF16, o_A + 32768 + i * 8192)
               for i in range(4)] +
              [A_(f"w2rc{i}", [128, 2 * E], BF16, o_C + [16384, 0, 8192][i])
               for i in range(3)])
    wqS = [A_(f"wqS{i}", [128, E], BF16, o_D + i * 4096) for i in range(2)]
    y0tB = A_("y0tB", [128, E], BF16, o_D)
    attnrowB = A_("attnrowB", [128, E], F32, o_D)
    b2row = A_("b2row", [1, E], BF16, o_D)
    x0tile = [A_(f"x0t{i}", [128, E], F32, o_E + i * 8192) for i in range(2)]
    x0p = [A_(f"x0p{i}", [128, E], F32, o_B + 32768 + i * 8192)
           for i in range(4)]
    y0p = [A_(f"y0p{i}", [128, E], BF16, o_E + 8192 + i * 4096)
           for i in range(4)]
    y0tA = A_("y0tA", [128, E], BF16, o_E + 16384)
    attnrowA = A_("attnrowA", [128, E], F32, o_E + 16384)
    attnrow = [attnrowA, attnrowB]
    WrB = A_("WrB", [128, E], BF16, o_M)
    wrrow = A_("wrrow", [1, 512], BF16, o_F)
    expTbig = A_("expTbig", [128, 2048], BF16, o_F)
    expT = [expTbig[:, i * 512:(i + 1) * 512] for i in range(4)]
    expT.append(diagstage[:])
    bq_nat = A_("bq_nat", [16, 128], F32, o_F)
    bk_nat = A_("bk_nat", [4, 128], F32, o_F + 512)
    wr_nat = A_("wr_nat", [16, 128], F32, o_F + 1024)
    br_nat = A_("br_nat", [1, 1], F32, o_F + 1536)
    htile = [A_(f"htile{i}", [128, 512], BF16, o_F + i * 1024) for i in range(4)]
    xwsb = [A_(f"xw{i}", [128, E], BF16, o_xw + i * 4096) for i in range(2)]

    PS = nc.alloc_psum_tensor("PS", [128, 4096], F32)
    bank = lambda i: PS[:, i * 512:(i + 1) * 512]
    brel = {i: None for i in range(8)}

    def bw(i):
        return [brel[i]] if brel[i] is not None else []

    def dbg_dump(nm, src, waits, f32src=False):
        if not debug:
            return
        eng = "sync" if f32src else "gpsimd"
        p.op(eng, lambda s, nm=nm, src=src: s.dma_start(dbgt[nm].ap(), src),
             waits, "dD", 16)

    for name, sb in [("i128b", i128b_s), ("triu128s", triu128s_s),
                     ("mask512", mask512_s), ("iotab", iotab_s),
                     ("i128f", i128f_s), ("triu16s", triu16s_s),
                     ("onesrow_b", onesrow_b_s), ("onesrow_f", onesrow_f_s),
                     ("onescol_b", onescol_b_s), ("epscol", epscol_s),
                     ("onecol_f", onecol_f_s)]:
        ap = inl[name].ap()
        p.op("gpsimd", lambda g, sb=sb, ap=ap: g.dma_start(sb[:], ap), (), "dC", 16)
    p.op("sync", lambda s: s.dma_start(bq_nat[:], bqt.ap()), (), "dP", 16)
    p.op("sync", lambda s: s.dma_start(bk_nat[:], bkt.ap()), (), "dP", 16)
    p.op("sync", lambda s: s.dma_start(
        wr_nat[:], wrt.ap().rearrange("(t p) o -> t p o", t=16)), (), "dP", 16)
    p.op("sync", lambda s: s.dma_start(
        br_nat[:], brt.ap().rearrange("(a b) -> a b", a=1)), (), "dP", 16)
    p.op("gpsimd", lambda g: g.dma_start(
        bv_row[:], bvt.ap().rearrange("g d -> (g d)").rearrange(
            "(a f) -> a f", a=1)), (), "dC", 16)
    for g_ in range(G):
        p.op("gpsimd", lambda g, g_=g_: g.dma_start(
            wkS[:, g_ * E:(g_ + 1) * E].rearrange("p (t d) -> p t d", t=16),
            wkt.ap()[g_].rearrange("(t p) d -> p t d", p=128)), (), "dKV", 16)
        p.op("gpsimd", lambda g, g_=g_: g.dma_start(
            wvS[:, g_ * E:(g_ + 1) * E].rearrange("p (t d) -> p t d", t=16),
            wvt.ap()[g_].rearrange("(t p) d -> p t d", p=128)), (), "dKV", 16)
    dC_tot = p.now("dC")
    dKV_tot = p.now("dKV")
    w_cd = [("dP", p.now("dP")), ("dC", dC_tot)]
    p.op("tensor", lambda t: t.matmul(PS[:, 3072:3088], bq_nat[:],
                                      i128f_s[0:16, 0:16], start=True, stop=True),
         w_cd, "tS")
    p.op("tensor", lambda t: t.matmul(PS[:, 3088:3092], bk_nat[:],
                                      i128f_s[0:4, 0:4], start=True, stop=True),
         (), "tS")
    p.op("tensor", lambda t: t.matmul(PS[:, 3104:3120], wr_nat[:],
                                      i128f_s[0:16, 0:16], start=True, stop=True),
         (), "tS")
    p.op("tensor", lambda t: t.matmul(PS[:, 3120:3121], onesrow_f_s[:],
                                      br_nat[:], start=True, stop=True), (), "tS")
    tS_par = p.now("tS")

    x0dma = []
    x0rel = [None, None, None, None]
    y0rel = [None, None, None, None]
    def emit_x0dma(st):
        sl = st % 4
        waits = [x0rel[sl]] if x0rel[sl] is not None else []
        sem = f"dX{sl}"
        x0dma.append((sem, p.op("sync", lambda s, st=st, sl=sl: s.dma_start(
            x0p[sl][:], x0t.ap()[st * 128:(st + 1) * 128, :]), waits, sem, 16)))
    for _st in range(4):
        emit_x0dma(_st)
    SA4 = [stat_a, stat_a2, stat_a3, stat_a4]
    SB4 = [stat_b, stat_b2, stat_b3, stat_b4]
    SE4 = [stat_e, stat_e2, stat_e3, stat_e4]
    SR4 = [stat_r, stat_r2, stat_r3, stat_r4]

    def emit_stats(st):
        sl = st % 4
        sb = SB4[st % 4]
        wd = [x0dma[st]]
        ywaits0 = [y0rel[sl]] if y0rel[sl] is not None else []
        p.op("vector", lambda v, sl=sl, sb=sb: v.scalar_tensor_tensor(
            y0p[sl][:], x0p[sl][:], 1.0, x0p[sl][:], OP.mult, OP.mult,
            accum_out=sb[:]), wd + ywaits0, "vS")
        return p.now("vS")

    def emit_sum(st):
        sl = st % 4
        sa = SA4[st % 4]
        p.op("scalar", lambda se, sl=sl, sa=sa: se.activation(
            tstage[:], x0p[sl][:], AF.Identity, accum_out=sa[:]),
            [x0dma[st]], "sS")

    vS_stats_l = {}
    y0T_tdone = {}
    y0ev = {}

    def emit_p0_part1(st):
        sl = st % 4
        y0t_ = y0p[sl]
        sa, sb = SA4[st % 4], SB4[st % 4]
        se_, sr_ = SE4[st % 4], SR4[st % 4]
        p.op("scalar", lambda se: se.copy(sdum[:], onecol_f_s[:]),
             [("vS", vS_stats_l[st])], "sS")
        p.op("scalar", lambda se: se.copy(sdum[:], onecol_f_s[:]), (), "sS")
        if debug:
            p.op("scalar", lambda se, st=st, sa=sa: se.copy(
                dbg_sums[:, st:st + 1], sa[:]), (), "sS")
            p.op("scalar", lambda se, st=st, sb=sb: se.copy(
                dbg_sumsq[:, st:st + 1], sb[:]), (), "sS")
        p.op("scalar", lambda se, sa=sa: se.mul(
            stat_c[:], sa[:], 1.0 / E), (), "sS")
        p.op("scalar", lambda se, sb=sb: se.mul(
            stat_d[:], sb[:], 1.0 / E), (), "sS")
        p.op("scalar", lambda se: se.activation(
            stat_f[:], stat_c[:], AF.Square), (), "sS")
        p.op("scalar", lambda se: se.activation(
            stat_g[:], stat_f[:], AF.Identity, bias=stat_d[:], scale=-1.0),
            (), "sS")
        p.op("scalar", lambda se: se.activation(
            stat_h[:], stat_g[:], AF.Ln, bias=epscol_s[:]), (), "sS")
        p.op("scalar", lambda se, sr_=sr_: se.activation(
            sr_[:], stat_h[:], AF.Exp, scale=-0.5), (), "sS")
        p.op("scalar", lambda se: se.mul(stat_c2[:], stat_c[:], -1.0), (), "sS")
        p.op("scalar", lambda se, se_=se_, sr_=sr_: se.activation(
            se_[:], stat_c2[:], AF.Identity, scale=sr_[:]), (), "sS")
        sS_ch0 = p.now("sS")
        ywaits = [("sS", sS_ch0)]
        if y0rel[sl] is not None:
            ywaits.append(y0rel[sl])
        p.op("vector", lambda v, sl=sl, y0t_=y0t_, se_=se_, sr_=sr_:
             v.tensor_scalar(y0t_[:], x0p[sl][:], sr_[:], se_[:],
                             OP.mult, OP.add), ywaits, "vS")
        y0ev[st] = ("vS", p.now("vS"))
        x0rel[sl] = y0ev[st]
        if st + 4 < NT:
            emit_x0dma(st + 4)

    def emit_p0_part2(st):
        sl = st % 4
        y0t_ = y0p[sl]
        for grp in range(4):
            bk_ = 6 if grp % 2 == 0 else 7
            for j in range(4):
                et = grp * 4 + j
                twaits = [y0ev[st]] + bw(bk_) if j == 0 else ()
                p.op("tensor", lambda t, y0t_=y0t_, et=et, bk_=bk_, j=j: t.matmul(
                    PS[:, bk_ * 512 + j * 128: bk_ * 512 + (j + 1) * 128],
                    y0t_[:, et * 128:(et + 1) * 128], i128b_s[:],
                    start=True, stop=True), twaits, "tS")
            tS_tr = p.now("tS")
            def ev(v, grp=grp, st=st, bk_=bk_):
                dst = y0T[:].rearrange("p (e s) -> p e s", e=NE)[
                    :, grp * 4:grp * 4 + 4, st * 128:(st + 1) * 128]
                return v.tensor_copy(dst, bank(bk_))
            brel[bk_] = ("vS", p.op("vector", ev, [("tS", tS_tr)], "vS"))
        y0rel[sl] = ("tS", p.now("tS"))
        y0T_tdone[st] = ("vS", p.now("vS"))

    for st in range(4):
        emit_sum(st)
        vS_stats_l[st] = emit_stats(st)
    for st in range(4):
        emit_p0_part1(st)
    p.op("vector", lambda v: v.tensor_copy(wrS[:], PS[:, 3104:3120]),
         [("tS", tS_par)], "vS")
    p.op("vector", lambda v: v.tensor_copy(brcol[:], PS[:, 3120:3121]), (), "vS")
    p.op("vector", lambda v: v.memset(
        Vs[:].rearrange("p (g k c) -> p g k c", g=G, k=16)[:, :, :, 128:129],
        1.0), (), "vS")
    vS_par2 = p.now("vS")
    p.op("scalar", lambda se: se.copy(bqS[:], PS[:, 3072:3088]),
         [("tS", tS_par), ("vS", vS_par2)], "sS")
    p.op("scalar", lambda se: se.copy(bkS[:], PS[:, 3088:3092]), (), "sS")
    brel[6] = ("sS", p.now("sS"))
    brel[7] = brel[6]
    for cch in range(4):
        for st in range(4 * cch, 4 * cch + 4):
            emit_p0_part2(st)
        for g_ in range(G):
            bk_ = 4 + (g_ % 2)
            for et in range(NE):
                waits = ()
                if et == 0:
                    waits = [y0T_tdone[4 * cch + 3]] + bw(bk_)
                    if cch == 0 and g_ == 0:
                        waits = waits + [("dKV", dKV_tot)]
                p.op("tensor", lambda t, g_=g_, cch=cch, et=et, bk_=bk_: t.matmul(
                    bank(bk_), wkS[:, g_ * E + et * 128:g_ * E + (et + 1) * 128],
                    y0T[:, et * S + cch * 512: et * S + (cch + 1) * 512],
                    start=(et == 0), stop=(et == NE - 1)), waits, "tS")
            tS_k = p.now("tS")
            brel[bk_] = ("sS", p.op(
                "scalar", lambda se, g_=g_, cch=cch, bk_=bk_: se.activation(
                    KT[:, g_ * S + cch * 512: g_ * S + (cch + 1) * 512],
                    bank(bk_), AF.Identity, bias=bkS[:, g_:g_ + 1]),
                [("tS", tS_k)], "sS"))
        if cch + 1 < 4:
            for st in range(4 * cch + 4, 4 * cch + 8):
                emit_sum(st)
                vS_stats_l[st] = emit_stats(st)
        for st in range(4 * cch, 4 * cch + 4):
            for et in range(NE):
                for g_ in range(G):
                    waits = ()
                    if et == 0:
                        waits = bw(g_)
                        if st == 0 and g_ == 0:
                            waits = waits + [("dKV", dKV_tot)]
                    p.op("tensor", lambda t, st=st, et=et, g_=g_: t.matmul(
                        PS[:, g_ * 512:g_ * 512 + 128],
                        y0T[:, et * S + st * 128: et * S + (st + 1) * 128],
                        wvS[:, g_ * E + et * 128: g_ * E + (et + 1) * 128],
                        start=(et == 0), stop=False), waits, "tS")
            for g_ in range(G):
                p.op("tensor", lambda t, g_=g_: t.matmul(
                    PS[:, g_ * 512:g_ * 512 + 128], onesrow_b_s[:],
                    bv_row[0:1, g_ * 128:(g_ + 1) * 128], start=False,
                    stop=True), (), "tS")
            tS_v = p.now("tS")
            for g_ in range(G):
                brel[g_] = ("vS", p.op(
                    "vector", lambda v, st=st, g_=g_: v.tensor_copy(
                        Vs[:, g_ * 2112 + st * 132: g_ * 2112 + st * 132 + 128],
                        PS[:, g_ * 512:g_ * 512 + 128]), [("tS", tS_v)], "vS"))
        if cch + 1 < 4:
            for st in range(4 * cch + 4, 4 * cch + 8):
                emit_p0_part1(st)
    vS_y0T = p.now("vS")
    tS_P0 = p.now("tS")
    for c_ in range(4):
        dwr = p.op("gpsimd", lambda g, c_=c_: g.dma_start(
            wrrow[:], wrt.ap()[c_ * 512:(c_ + 1) * 512, :].rearrange(
                "(a f) o -> a (f o)", a=1)),
            [] if c_ == 0 else [("sS", p.now("sS"))], "dWR", 16)
        p.op("tensor", lambda t, c_=c_: t.matmul(
            PS[:, 3584:4096], onesrow_b_s[:], wrrow[:],
            start=True, stop=True), [("dWR", dwr)] + (bw(7) if c_ == 0 else []),
            "tS")
        p.op("scalar", lambda se, c_=c_: se.copy(
            WrB[:, c_ * 512:(c_ + 1) * 512], PS[:, 3584:4096]),
            [("tS", p.now("tS"))], "sS")
    brel[7] = ("sS", p.now("sS"))
    sS_wrb = p.now("sS")
    dbg_dump("dbg_y0T", y0T[:, 0:S], [("vS", vS_y0T)])
    dbg_dump("dbg_sums", dbg_sums[:], [("vS", vS_y0T)], f32src=True)
    dbg_dump("dbg_sumsq", dbg_sumsq[:], [("vS", vS_y0T)], f32src=True)
    sS_K = p.now("sS")
    dbg_dump("dbg_kt", KT[:, 0:S], [("sS", sS_K)])
    sS_P1 = p.now("sS")
    vS_P1 = p.now("vS")

    wqdma = []
    wq_rel = [None, None]
    def emit_wqdma(h_):
        sl = h_ % 2
        waits = []
        if wq_rel[sl] is not None:
            waits.append(wq_rel[sl])
        sem = f"dQ{sl}"
        wqdma.append((sem, p.op("gpsimd", lambda g, h_=h_, sl=sl: g.dma_start(
            wqS[sl][:].rearrange("p (t d) -> p t d", t=16),
            wqt.ap()[h_].rearrange("(t p) d -> p t d", p=128)),
            waits, sem, 16)))
    emit_wqdma(0)
    emit_wqdma(1)
    qevac = []
    qpair = 0
    for h_ in range(H):
        sl = h_ % 2
        for pair in range(2):
            b0 = 4 + 2 * (qpair % 2)
            qpair += 1
            for et in range(NE):
                for cc in range(2):
                    cch = pair * 2 + cc
                    bk_ = b0 + cc
                    waits = ()
                    if et == 0 and cc == 0:
                        waits = [wqdma[h_]] + bw(b0) + bw(b0 + 1)
                    p.op("tensor",
                         lambda t, sl=sl, cch=cch, et=et, bk_=bk_: t.matmul(
                             bank(bk_), wqS[sl][:, et * 128:(et + 1) * 128],
                             y0T[:, et * S + cch * 512: et * S + (cch + 1) * 512],
                             start=(et == 0), stop=(et == NE - 1)), waits, "tS")
            tS_q = p.now("tS")
            for cc in range(2):
                cch = pair * 2 + cc
                bk_ = b0 + cc
                brel[bk_] = ("sS", p.op(
                    "scalar", lambda se, h_=h_, cch=cch, bk_=bk_: se.activation(
                        Qall[:, h_ * S + cch * 512: h_ * S + (cch + 1) * 512],
                        bank(bk_), AF.Identity, bias=bqS[:, h_:h_ + 1]),
                    [("tS", tS_q)], "sS"))
        qevac.append(p.now("sS"))
        wq_rel[sl] = ("tS", p.now("tS"))
        if h_ + 2 < H:
            emit_wqdma(h_ + 2)
    dbg_dump("dbg_qh", Qall[:, 0:S], [("sS", qevac[0])])

    chunk_avdone = []
    chunk_expdone = []
    x0dma3 = []
    x0rel3 = [None, None]
    attnrow_rel = [None, None]
    av_rel = [("vS", vS_P1)]
    x1dma = []
    y1evac = []
    tail_defer = []

    def flush_tail():
        for fn in tail_defer:
            fn()
        tail_defer.clear()
    chunk_id = 0
    qall_r = Qall[:].rearrange("p (h s) -> p h s", h=H)
    vs_r = Vs[:]
    slot_user = {}
    uctr = [0]
    carry = [None]
    emit_diag_l = [None]
    for qt in range(NT):
        sl3 = qt % 2
        ar = qt % 2
        waits = [x0rel3[sl3]] if x0rel3[sl3] is not None else [("sS", qevac[0])]
        sem = f"dX{sl3}"
        x0dma3.append((sem, p.op("sync", lambda s, qt=qt, sl3=sl3: s.dma_start(
            x0tile[sl3][:], x0t.ap()[qt * 128:(qt + 1) * 128, :]), waits,
            sem, 16)))
        if qt > 0:
            carry[0] = emit_diag_l[0](0, qt=qt)
            flush_tail()
        for g_ in range(G):
            pendq = []

            def emit_av(pend, g_=g_):
                for (cid, esl, kt, avw, sflag, eflag) in pend:
                    if sflag:
                        avw = avw + av_rel
                    for m_ in range(4):
                        p.op("tensor",
                             lambda t, m_=m_, esl=esl, g_=g_, kt=kt,
                             sflag=sflag, eflag=eflag:
                             t.matmul(
                                 PS[:, m_ * 512:m_ * 512 + 129],
                                 expT[esl][:, m_ * 128:(m_ + 1) * 128],
                                 vs_r[:, g_ * 2112 + kt * 132:
                                      g_ * 2112 + kt * 132 + 129],
                                 start=sflag, stop=eflag),
                             avw if m_ == 0 else (), "tS")
                    while len(chunk_avdone) <= cid:
                        chunk_avdone.append(0)
                    chunk_avdone[cid] = p.now("tS")
                    slot_user[esl] = p.now("tS")

            def emit_diag(gg, qt=qt):
                nonlocal chunk_id
                uslot = uctr[0] % 2
                uctr[0] += 1
                cid = chunk_id
                chunk_id += 1
                sbk = 4 + 2 * uslot
                swaits = [("sS", qevac[gg * 4 + 3])] + bw(sbk)
                p.op("tensor", lambda t, gg=gg, qt=qt, sbk=sbk: t.matmul(
                    bank(sbk),
                    KT[:, gg * S + qt * 128:gg * S + (qt + 1) * 128],
                    qall_r[:, gg * 4:(gg + 1) * 4, qt * 128:(qt + 1) * 128],
                    start=True, stop=False), swaits, "tS")
                p.op("tensor", lambda t, sbk=sbk: t.matmul(
                    bank(sbk), i128b_s[:], mask512_s[:],
                    start=False, stop=True), (), "tS")
                tS_sc = p.now("tS")
                ewaits = [("tS", tS_sc)]
                if 4 in slot_user:
                    ewaits.append(("tS", slot_user[4]))
                rel = p.op("scalar", lambda se, sbk=sbk: se.activation(
                    expT[4][:], bank(sbk), AF.Exp, scale=SCL), ewaits, "sS")
                brel[sbk] = ("sS", rel)
                return [(cid, 4, qt, [("sS", rel)], True, qt == 0)]
            emit_diag_l[0] = emit_diag

            units = [(qt,)]
            nkt = qt + 1
            j = 0
            while j + 1 < nkt - 1:
                units.append((j, j + 1))
                j += 2
            if j < nkt - 1:
                units.append((j,))
                j += 1
            cbase = 0
            if carry[0] is not None:
                pendq.append(carry[0])
                carry[0] = None
                units = units[1:]
                cbase = 1

            for ui, unit in enumerate(units):
                uslot = uctr[0] % 2
                uctr[0] += 1
                scw_extra = []
                if qt == 0 and g_ == 0 and ui == 0:
                    scw_extra = [("vS", vS_P1), ("sS", sS_P1)]
                ch = []
                for k_i, kt in enumerate(unit):
                    cid = chunk_id
                    chunk_id += 1
                    if len(unit) == 2:
                        esl = 2 * uslot + k_i
                    else:
                        esl = 4 if unit[0] == qt else 2 * uslot
                    sbk = 4 + 2 * uslot + k_i if len(unit) == 2 else 4 + 2 * uslot
                    swaits = [("sS", qevac[g_ * 4 + 3])] + scw_extra + bw(sbk)
                    scw_extra = []
                    is_d = (kt == qt)
                    p.op("tensor",
                         lambda t, g_=g_, qt=qt, kt=kt, sbk=sbk, is_d=is_d:
                         t.matmul(
                             bank(sbk),
                             KT[:, g_ * S + kt * 128:g_ * S + (kt + 1) * 128],
                             qall_r[:, g_ * 4:(g_ + 1) * 4,
                                    qt * 128:(qt + 1) * 128],
                             start=True, stop=not is_d), swaits, "tS")
                    if is_d:
                        mkw = [("dC", dC_tot)] if (qt == 0 and g_ == 0) else ()
                        p.op("tensor", lambda t, sbk=sbk: t.matmul(
                            bank(sbk), i128b_s[:], mask512_s[:],
                            start=False, stop=True), mkw, "tS")
                    ch.append((cid, esl, kt, sbk))
                tS_sc = p.now("tS")
                is_diag = (unit[-1] == qt)
                ewaits = [("tS", tS_sc)]
                for (cid, esl, kt, sbk) in ch:
                    if esl in slot_user:
                        ewaits.append(("tS", slot_user[esl]))
                if len(ch) == 2:
                    (c0, e0, k0, b0_), (c1, e1, k1, b1_) = ch
                    rel = p.op("scalar", lambda se, e0=e0, b0_=b0_: se.activation(
                        expTbig[:, e0 * 512:(e0 + 2) * 512],
                        PS[:, b0_ * 512:(b0_ + 2) * 512], AF.Exp, scale=SCL),
                        ewaits, "sS")
                    brel[b0_] = ("sS", rel)
                    brel[b0_ + 1] = ("sS", rel)
                    avw0 = [("sS", rel)]
                    avw1 = []
                else:
                    (c0, e0, k0, b0_) = ch[0]
                    rel = p.op("scalar", lambda se, e0=e0, b0_=b0_:
                               se.activation(expT[e0][:], bank(b0_), AF.Exp,
                                             scale=SCL), ewaits, "sS")
                    brel[b0_] = ("sS", rel)
                    avw0 = [("sS", rel)]
                pend = []
                nch_before = cbase + sum(len(u) for u in units[:ui])
                for idx, (cid, esl, kt, sbk) in enumerate(ch):
                    avw = avw0 if idx == 0 else []
                    first_chunk = (nch_before + idx == 0)
                    last_chunk = (nch_before + idx == nkt - 1)
                    pend.append((cid, esl, kt, avw, first_chunk, last_chunk))
                pendq.append(pend)
                if len(pendq) > 1:
                    emit_av(pendq.pop(0))
            for pend in pendq:
                emit_av(pend)
            if g_ < 3:
                carry[0] = emit_diag(g_ + 1)
            tS_av = p.now("tS")
            zap = PS[:].rearrange("p (b c) -> p b c", b=8)[:, 0:4, 128:129]
            zap = zap.rearrange("p b o -> p (b o)")
            vS_rz = p.op("vector", lambda v, zap=zap: v.reciprocal(
                recipz[:], zap), [("tS", tS_av)], "vS")
            p.op("vector", lambda v: v.tensor_copy(vdum[:], recipz[:, 0:1]),
                 (), "vS")
            arw = [x0dma3[qt]]
            if g_ == 0 and attnrow_rel[ar] is not None:
                arw.append(attnrow_rel[ar])
            for m_ in range(4):
                p.op("vector", lambda v, m_=m_, g_=g_, ar=ar, sl3=sl3:
                     v.scalar_tensor_tensor(
                         attnrow[ar][:, g_ * 512 + m_ * 128:
                                     g_ * 512 + (m_ + 1) * 128],
                         PS[:, m_ * 512:m_ * 512 + 128],
                         recipz[:, m_:m_ + 1],
                         x0tile[sl3][:, g_ * 512 + m_ * 128:
                                     g_ * 512 + (m_ + 1) * 128],
                         OP.mult, OP.add), arw if m_ == 0 else (), "vS")
            av_rel = [("vS", p.now("vS"))]
            p.op("vector", lambda v, g_=g_, ar=ar: v.tensor_reduce(
                stata4[:, g_:g_ + 1],
                attnrow[ar][:, g_ * 512:(g_ + 1) * 512], AX, OP.add), (), "vS")
            p.op("vector", lambda v, g_=g_, ar=ar, qt=qt:
                 v.scalar_tensor_tensor(
                     y1[:, qt * E + g_ * 512:qt * E + (g_ + 1) * 512],
                     attnrow[ar][:, g_ * 512:(g_ + 1) * 512], 1.0,
                     attnrow[ar][:, g_ * 512:(g_ + 1) * 512],
                     OP.mult, OP.mult, accum_out=statb4[:, g_:g_ + 1]),
                 (), "vS")
        vS_add = p.now("vS")
        x0rel3[sl3] = ("vS", vS_add)
        sem = "dXA" if ar == 0 else "dXB"
        x1dma.append((sem, p.op("sync", lambda s, qt=qt, ar=ar: s.dma_start(
            x1s.ap()[qt * 128:(qt + 1) * 128, :], attnrow[ar][:]),
            [("vS", vS_add)], sem, 16)))
        vS_stats = p.now("vS")

        def tail_all(qt=qt, ar=ar, sl3=sl3, vS_stats=vS_stats):
            p.op("scalar", lambda se: se.copy(sdum[:], onecol_f_s[:]),
                 [("vS", vS_stats)], "sS")
            p.op("scalar", lambda se: se.copy(sdum[:], onecol_f_s[:]), (), "sS")
            p.op("scalar", lambda se: se.activation(
                sdum4[:], stata4[:], AF.Identity, accum_out=stat_a[:]),
                (), "sS")
            p.op("scalar", lambda se: se.activation(
                sdum4b[:], statb4[:], AF.Identity, accum_out=stat_b[:]),
                (), "sS")
            p.op("scalar", lambda se: se.copy(sdum[:], onecol_f_s[:]), (), "sS")
            p.op("scalar", lambda se: se.copy(sdum[:], onecol_f_s[:]), (), "sS")
            p.op("scalar", lambda se: se.mul(
                stat_c[:], stat_a[:], 1.0 / E), (), "sS")
            p.op("scalar", lambda se: se.mul(
                stat_d[:], stat_b[:], 1.0 / E), (), "sS")
            p.op("scalar", lambda se: se.activation(
                stat_f[:], stat_c[:], AF.Square), (), "sS")
            p.op("scalar", lambda se: se.activation(
                stat_g[:], stat_f[:], AF.Identity, bias=stat_d[:], scale=-1.0),
                (), "sS")
            p.op("scalar", lambda se: se.activation(
                stat_h[:], stat_g[:], AF.Ln, bias=epscol_s[:]), (), "sS")
            p.op("scalar", lambda se: se.activation(
                stat_r[:], stat_h[:], AF.Exp, scale=-0.5), (), "sS")
            p.op("scalar", lambda se: se.mul(
                stat_c2[:], stat_c[:], -1.0), (), "sS")
            p.op("scalar", lambda se: se.activation(
                stat_e[:], stat_c2[:], AF.Identity, scale=stat_r[:]), (), "sS")
            p.op("scalar", lambda se, qt=qt: se.copy(
                nbias1_s[:, qt:qt + 1], stat_e[:]), (), "sS")
            p.op("scalar", lambda se, qt=qt: se.activation(
                rstdp1_s[:, qt:qt + 1], stat_r[:], AF.Identity,
                bias=onecol_f_s[:]), (), "sS")
            sS_chain = p.now("sS")
            p.op("vector", lambda v, ar=ar, qt=qt: v.tensor_scalar(
                y1[:, qt * E:(qt + 1) * E], attnrow[ar][:], stat_r[:],
                stat_e[:], OP.mult, OP.add), [("sS", sS_chain)], "vS")
            y1evac.append(("vS", p.now("vS")))
            attnrow_rel[ar] = x1dma[qt]
            p.op("vector", lambda v, qt=qt, sl3=sl3: v.scalar_tensor_tensor(
                x0tile[sl3][:], y1[:, qt * E:(qt + 1) * E], 1.0, WrB[:],
                OP.mult, OP.mult, accum_out=stat_f[:]),
                [("sS", sS_wrb)], "vS")
            x0rel3[sl3] = ("vS", p.now("vS"))
            p.op("vector", lambda v: v.tensor_copy(vdum[:], stat_f[:]), (), "vS")
            p.op("vector", lambda v, qt=qt: v.tensor_add(
                logits_s[:, qt:qt + 1], stat_f[:], brcol[:]), (), "vS")
        tail_defer.append(tail_all)
    flush_tail()
    tS_attn = p.now("tS")
    vS_attn = p.now("vS")
    sS_attn = p.now("sS")

    w1dma = []
    w1_rel = {}
    for slq in range(4):
        sem = f"dW1{slq}"
        w1dma.append((sem, p.op("gpsimd", lambda g, slq=slq: g.dma_start(
            w1ring[slq][:].rearrange("p (t f) -> p t f", t=16),
            w1t.ap()[:, slq * 512:(slq + 1) * 512].rearrange(
                "(t p) f -> p t f", p=128)), [("tS", tS_attn)], sem, 16)))

    p.op("tensor", lambda t: t.matmul(PS[0:16, 3072:3200], logits_s[:],
                                      i128f_s[:], start=True, stop=True),
         [("vS", vS_attn)] + bw(6), "tS")
    tS_lt = p.now("tS")
    vS_lr = p.op("vector", lambda v: v.tensor_copy(
        log16[:], PS[0:16, 3072:3200]), [("tS", tS_lt)], "vS")
    brel[6] = ("vS", vS_lr)
    dX_lr = p.op("sync", lambda s: s.dma_start(logrow[:], log16[:]),
                 [("vS", vS_lr)], "dLR", 16)
    for cch in range(4):
        bk_ = 7 if cch % 2 == 0 else 6
        p.op("tensor", lambda t, cch=cch, bk_=bk_: t.matmul(
            bank(bk_), onesrow_f_s[:],
            logrow[0:1, cch * 512:(cch + 1) * 512], start=True, stop=True),
            [("dLR", dX_lr)] + bw(bk_), "tS")
        tS_lb = p.now("tS")
        brel[bk_] = ("sS", p.op("scalar", lambda se, cch=cch, bk_=bk_: se.activation(
            LB[:, cch * 512:(cch + 1) * 512], bank(bk_), AF.Identity),
            [("tS", tS_lb)], "sS"))
    sS_lb = p.now("sS")
    p.op("vector", lambda v: v.memset(ones2k[:], 1.0), [("sS", sS_lb)], "vS")
    for st in range(1, NT, 2):
        p.op("vector", lambda v, st=st: v.scalar_tensor_tensor(
            gdump[:], LB[:], logits_s[:, st:st + 1], ones2k[:],
            OP.is_gt, OP.mult, accum_out=cg_s[:, st:st + 1]), (), "vS")
    evens = list(range(0, NT, 2))
    p.op("scalar", lambda se: se.mul(hkcol[:], onecol_f_s[:], 1023.5),
         [("vS", vS_attn)], "sS")
    p.op("scalar", lambda se: se.mul(
        nlog2[:, 0:1], logits_s[:, evens[0]:evens[0] + 1], -1.0), (), "sS")
    p.op("scalar", lambda se: se.mul(
        nlog2[:, 1:2], logits_s[:, evens[1]:evens[1] + 1], -1.0), (), "sS")
    for i, st in enumerate(evens):
        pr = i % 2
        p.op("scalar", lambda se, pr=pr: se.activation(
            tstage[:], LB[:], AF.Sign, bias=nlog2[:, pr:pr + 1],
            accum_out=cgalt2[:, pr:pr + 1]), (), "sS")
        if i + 2 < len(evens):
            p.op("scalar", lambda se, i=i: se.mul(
                nlog2[:, i % 2:i % 2 + 1],
                logits_s[:, evens[i + 2]:evens[i + 2] + 1], -1.0), (), "sS")
        else:
            p.op("scalar", lambda se: se.copy(sdum[:], onecol_f_s[:]),
                 (), "sS")
        if i >= 1:
            p.op("scalar", lambda se, i=i: se.activation(
                cg_s[:, evens[i - 1]:evens[i - 1] + 1],
                cgalt2[:, (i - 1) % 2:(i - 1) % 2 + 1], AF.Identity,
                bias=hkcol[:], scale=0.5), (), "sS")
    p.op("scalar", lambda se: se.copy(sdum[:], onecol_f_s[:]), (), "sS")
    p.op("scalar", lambda se: se.copy(sdum[:], onecol_f_s[:]), (), "sS")
    p.op("scalar", lambda se: se.activation(
        cg_s[:, evens[-1]:evens[-1] + 1], cgalt2[:, 1:2], AF.Identity,
        bias=hkcol[:], scale=0.5), (), "sS")
    sS_cnt = p.now("sS")
    p.op("vector", lambda v: v.tensor_copy(vdum[:], cg_s[:, 0:1]),
         [("sS", sS_cnt)], "vS")
    p.op("vector", lambda v: v.tensor_copy(vdum[:], cg_s[:, 0:1]), (), "vS")
    p.op("vector", lambda v: v.tensor_scalar(
        m_s[:], cg_s[:], float(KSEL), None, OP.is_lt), (), "vS", serial=True)
    p.op("vector", lambda v: v.tensor_copy(mb_s[:], m_s[:]), (), "vS", serial=True)
    vS_m = p.now("vS")
    dbg_dump("dbg_cg", cg_s[:], [("vS", vS_m)], f32src=True)
    dbg_dump("dbg_m", m_s[:], [("vS", vS_m)], f32src=True)
    dbg_dump("dbg_logits", logits_s[:], [("vS", vS_m)], f32src=True)
    p.op("tensor", lambda t: t.matmul(PS[0:16, 3584:3585], mb_s[:],
                                      onescol_b_s[:], start=True, stop=True),
         [("vS", vS_m)] + bw(7), "tS", serial=True)
    tS_c1 = p.now("tS")
    p.op("vector", lambda v: v.tensor_copy(ts16_s[:], PS[0:16, 3584:3585]),
         [("tS", tS_c1)], "vS", serial=True)
    vS_c1 = p.now("vS")
    p.op("tensor", lambda t: t.matmul(PS[0:16, 3712:3713], triu16s_s[:],
                                      ts16_s[:], start=True, stop=True),
         [("vS", vS_c1)], "tS", serial=True)
    tS_c2 = p.now("tS")
    p.op("vector", lambda v: v.tensor_copy(offs_s[:], PS[0:16, 3712:3713]),
         [("tS", tS_c2)], "vS", serial=True)
    vS_c2 = p.now("vS")
    p.op("tensor", lambda t: t.matmul(PS[0:1, 3840:3856], offs_s[:],
                                      i128b_s[0:16, 0:16], start=True,
                                      stop=True), [("vS", vS_c2)], "tS",
         serial=True)
    tS_c3 = p.now("tS")
    p.op("vector", lambda v: v.tensor_copy(offsrow_s[:], PS[0:1, 3840:3856]),
         [("tS", tS_c3)], "vS", serial=True)
    vS_c3 = p.now("vS")
    p.op("tensor", lambda t: t.matmul(PS[:, 3328:3344], triu128s_s[:],
                                      mb_s[:], start=True, stop=False),
         [("vS", vS_c3)], "tS", serial=True)
    p.op("tensor", lambda t: t.matmul(PS[:, 3328:3344], onesrow_b_s[:],
                                      offsrow_s[:], start=False, stop=True),
         (), "tS", serial=True)
    tS_pos = p.now("tS")
    p.op("vector", lambda v: v.tensor_scalar(
        t1_s[:], m_s[:], -BIG, BIG, OP.mult, OP.add), [("tS", tS_pos)], "vS", serial=True)
    p.op("vector", lambda v: v.tensor_copy(vdum[:], t1_s[:, 0:1]), (), "vS")
    p.op("vector", lambda v: v.tensor_copy(vdum[:], t1_s[:, 1:2]), (), "vS")
    p.op("vector", lambda v: v.tensor_add(q_s[:], t1_s[:], PS[:, 3328:3344]),
         (), "vS", serial=True)
    p.op("vector", lambda v: v.tensor_copy(vdum[:], q_s[:, 0:1]), (), "vS")
    brel[7] = ("vS", p.now("vS"))
    vS_q = p.now("vS")
    dbg_dump("dbg_q", q_s[:], [("vS", vS_q)], f32src=True)
    vS_ptc = []
    for st in range(NT):
        p.op("vector", lambda v, st=st: v.tensor_scalar(
            PT[:, st * 256:(st + 1) * 256], iotab_s[:], q_s[:, st:st + 1],
            None, OP.is_equal), (), "vS", serial=True)
        vS_ptc.append(p.now("vS"))
    vS_pt = p.now("vS")
    fx_evac = []
    for et in range(NE):
        bk_ = et % 2
        for st in range(NT):
            waits = ()
            if et == 0:
                waits = [("vS", vS_ptc[st])] + (bw(bk_) if st == 0 else [])
            elif st == 0:
                waits = bw(bk_)
            p.op("tensor", lambda t, et=et, st=st, bk_=bk_: t.matmul(
                PS[:, bk_ * 512:bk_ * 512 + 256],
                y1[:, st * E + et * 128: st * E + (et + 1) * 128],
                PT[:, st * 256:(st + 1) * 256],
                start=(st == 0), stop=(st == NT - 1)), waits, "tS")
        tS_g = p.now("tS")
        brel[bk_] = ("sS", p.op("scalar", lambda se, et=et, bk_=bk_: se.activation(
            fxT[:, et * 256:(et + 1) * 256], PS[:, bk_ * 512:bk_ * 512 + 256],
            AF.Identity), [("tS", tS_g)], "sS"))
        fx_evac.append(p.now("sS"))
    sS_fx = p.now("sS")
    tS_gather = p.now("tS")
    psel = {}
    vS_P_l = [None]

    def emit_postsel():
        p.op("vector", lambda v: v.tensor_scalar(
            lm_s[:], m_s[:], BIG, -BIG, OP.mult, OP.add), [("vS", vS_m)], "vS", serial=True)
        p.op("vector", lambda v: v.tensor_copy(vdum[:], lm_s[:, 0:1]), (), "vS")
        p.op("vector", lambda v: v.tensor_add(lm2_s[:], lm_s[:], logits_s[:]),
             (), "vS", serial=True)
        p.op("vector", lambda v: v.tensor_copy(vdum[:], lm2_s[:, 0:1]), (), "vS")
        p.op("vector", lambda v: v.tensor_reduce(mx1_s[:], lm2_s[:], AX, OP.max),
             (), "vS", serial=True)
        vS_mx = p.now("vS")
        p.op("tensor", lambda t: t.matmul(PS[0:1, 3072:3200], mx1_s[:],
                                          i128f_s[:], start=True, stop=True),
             [("vS", vS_mx)] + bw(6), "tS")
        tS_mx = p.now("tS")
        p.op("vector", lambda v: v.tensor_reduce(
            negmx_s[0:1, :], PS[0:1, 3072:3200], AX, OP.max), [("tS", tS_mx)], "vS")
        vS_mm = p.now("vS")
        p.op("tensor", lambda t: t.matmul(PS[:, 3200:3201], onesrow_f_s[:],
                                          negmx_s[0:1, :], start=True, stop=True),
             [("vS", vS_mm)], "tS")
        tS_mb = p.now("tS")
        p.op("scalar", lambda se: se.mul(negmx_s[:], PS[:, 3200:3201], -1.0),
             [("tS", tS_mb)], "sS", serial=True)
        brel[6] = ("sS", p.now("sS"))
        p.op("scalar", lambda se: se.copy(sdum[:], negmx_s[:]), (), "sS")
        p.op("scalar", lambda se: se.activation(
            wu_s[:], lm2_s[:], AF.Exp, bias=negmx_s[:]), (), "sS", serial=True)
        sS_wu = p.now("sS")
        p.op("vector", lambda v: v.tensor_reduce(zc_s[:], wu_s[:], AX, OP.add),
             [("sS", sS_wu)], "vS")
        vS_z1 = p.now("vS")
        p.op("tensor", lambda t: t.matmul(PS[0:1, 3072:3200], zc_s[:],
                                          i128f_s[:], start=True, stop=True),
             [("vS", vS_z1)] + bw(6), "tS")
        tS_z = p.now("tS")
        p.op("vector", lambda v: v.tensor_reduce(
            zc_s[0:1, :], PS[0:1, 3072:3200], AX, OP.add), [("tS", tS_z)], "vS", serial=True)
        p.op("vector", lambda v: v.tensor_copy(vdum[:], cg_s[:, 0:1]), (), "vS")
        p.op("vector", lambda v: v.tensor_copy(vdum[:], cg_s[:, 1:2]), (), "vS")
        p.op("vector", lambda v: v.reciprocal(rz_s[:], zc_s[0:1, :]), (), "vS", serial=True)
        vS_rzz = p.now("vS")
        p.op("tensor", lambda t: t.matmul(PS[:, 3200:3201], onesrow_f_s[:],
                                          rz_s[:], start=True, stop=True),
             [("vS", vS_rzz)], "tS")
        tS_rb = p.now("tS")
        p.op("vector", lambda v: v.tensor_copy(rzc_s[:], PS[:, 3200:3201]),
             [("tS", tS_rb)], "vS", serial=True)
        brel[6] = ("vS", p.now("vS"))
        p.op("vector", lambda v: v.tensor_copy(vdum[:], rzc_s[:]), (), "vS")
        p.op("vector", lambda v: v.tensor_scalar(
            w_s[:], wu_s[:], rzc_s[:], None, OP.mult), (), "vS", serial=True)
        p.op("vector", lambda v: v.tensor_copy(vdum[:], w_s[:, 0:1]), (), "vS")
        p.op("vector", lambda v: v.tensor_copy(vdum[:], w_s[:, 1:2]), (), "vS")
        p.op("vector", lambda v: v.tensor_copy(wb_s[:], w_s[:]), (), "vS", serial=True)
        vS_w = p.now("vS")
        dbg_dump("dbg_w", w_s[:], [("vS", vS_w)], f32src=True)
        for ih in range(2):
            for sg in range(4):
                bk_ = 6 if (ih * 4 + sg) % 2 == 0 else 7
                for j in range(4):
                    st = sg * 4 + j
                    twaits = ([("vS", vS_pt)] + bw(bk_)) if j == 0 else ()
                    p.op("tensor", lambda t, st=st, ih=ih, bk_=bk_, j=j: t.matmul(
                        PS[:, bk_ * 512 + j * 128: bk_ * 512 + (j + 1) * 128],
                        PT[:, st * 256 + ih * 128: st * 256 + ih * 128 + 128],
                        i128b_s[:], start=True, stop=True), twaits, "tS")
                tS_p = p.now("tS")
                brel[bk_] = ("vS", p.op(
                    "vector", lambda v, ih=ih, sg=sg, bk_=bk_: v.tensor_copy(
                        Psb[:, ih * S + sg * 512:(ih * S + (sg + 1) * 512)],
                        bank(bk_)), [("tS", tS_p)], "vS"))
        vS_P_l[0] = ("vS", p.now("vS"))
        for ih in range(2):
            bk_ = 2 + ih
            for st in range(NT):
                waits = ([("vS", vS_w)] + bw(bk_)) if st == 0 else ()
                p.op("tensor", lambda t, ih=ih, st=st, bk_=bk_: t.matmul(
                    PS[:, bk_ * 512:bk_ * 512 + 1],
                    PT[:, st * 256 + ih * 128: st * 256 + ih * 128 + 128],
                    wb_s[:, st:st + 1], start=(st == 0), stop=(st == NT - 1)),
                    waits, "tS")
            tS_wg = p.now("tS")
            brel[bk_] = ("sS", p.op("scalar", lambda se, ih=ih, bk_=bk_: se.copy(
                wg_s[:, ih:ih + 1], PS[:, bk_ * 512:bk_ * 512 + 1]),
                [("tS", tS_wg)], "sS"))

        psel["vS_wg"] = ("sS", p.now("sS"))
        psel["vS_P"] = vS_P_l[0]
        dbg_dump("dbg_wg", wg_s[:], [psel["vS_wg"]], f32src=True)

    dbg_dump("dbg_fxT", fxT[:], [("sS", sS_fx)])
    dbg_dump("dbg_y1", y1[:, 0:E], [("sS", sS_fx)])

    p.op("gpsimd", lambda g: g.dma_start(
        b1row[:], b1t.ap().rearrange("(a f) -> a f", a=1)),
        [("tS", tS_gather)], "dB1", 16)
    dW_b1 = p.now("dB1")
    hT_evac = []
    for slq in range(16):
        rsl = slq % 4
        if slq >= 4:
            sem = f"dW1{rsl}"
            w1dma.append((sem, p.op("gpsimd", lambda g, slq=slq, rsl=rsl: g.dma_start(
                w1ring[rsl][:].rearrange("p (t f) -> p t f", t=16),
                w1t.ap()[:, slq * 512:(slq + 1) * 512].rearrange(
                    "(t p) f -> p t f", p=128)),
                [w1_rel[rsl]], sem, 16)))
        hb = [0, 1] if slq % 2 == 0 else [2, 3]
        for et in range(NE):
            for ih in range(2):
                bk_ = hb[ih]
                waits = ()
                if et == 0:
                    waits = [w1dma[slq], ("sS", sS_fx),
                             ("dB1", dW_b1)] + bw(bk_)
                p.op("tensor", lambda t, slq=slq, rsl=rsl, et=et, ih=ih,
                     bk_=bk_: t.matmul(
                         bank(bk_),
                         fxT[:, et * 256 + ih * 128: et * 256 + ih * 128 + 128],
                         w1ring[rsl][:, et * 512:(et + 1) * 512],
                         start=(et == 0), stop=False), waits, "tS")
        for ih in range(2):
            p.op("tensor", lambda t, slq=slq, ih=ih, bk_=hb[ih]: t.matmul(
                bank(bk_), onesrow_b_s[:],
                b1row[0:1, slq * 512:(slq + 1) * 512],
                start=False, stop=True), (), "tS")
        w1_rel[rsl] = ("tS", p.now("tS"))
        tS_h = p.now("tS")
        for ih in range(2):
            bk_ = hb[ih]
            hsl = 2 * (slq % 2) + ih
            brel[bk_] = ("sS", p.op(
                "scalar", lambda se, hsl=hsl, bk_=bk_: se.activation(
                    htile[hsl][:], bank(bk_), AF.Gelu), [("tS", tS_h)], "sS"))
            sS_g = p.now("sS")
            tbk = 4 if slq % 2 == 0 else 5
            for j in range(4):
                twaits = ([("sS", sS_g)] + bw(tbk)) if j == 0 else ()
                p.op("tensor", lambda t, hsl=hsl, j=j, tbk=tbk: t.matmul(
                    PS[:, tbk * 512 + j * 128: tbk * 512 + (j + 1) * 128],
                    htile[hsl][:, j * 128:(j + 1) * 128], i128b_s[:],
                    start=True, stop=True), twaits, "tS")
            tS_t = p.now("tS")
            def hev(v, slq=slq, ih=ih, tbk=tbk):
                dst = hT[:, slq * 1024: (slq + 1) * 1024].rearrange(
                    "p (j c) -> p j c", j=4)[:, :, ih * 128: ih * 128 + 128]
                return v.tensor_copy(dst, bank(tbk))
            brel[tbk] = ("vS", p.op("vector", hev, [("tS", tS_t)], "vS"))
            hT_evac.append(p.now("vS"))
        if slq == 0:
            emit_postsel()
    vS_hT = p.now("vS")
    tS_W1 = p.now("tS")
    dbg_dump("dbg_hT", hT[:, 0:2048], [("vS", vS_hT)])

    p.op("gpsimd", lambda g: g.dma_start(
        b2row[:], b2t.ap().rearrange("(a f) -> a f", a=1)),
        [x1dma[NT - 1], y1evac[NT - 1]], "dB2", 16)
    dW_b2 = p.now("dB2")
    w2dma = []
    w2_rel = {}
    w2_first = [("tS", tS_gather)] * 4 + [("tS", tS_W1)] * 3
    def emit_w2dma(ch):
        rsl = ch % 7
        waits = [w2_first[rsl]] if ch < 7 else [w2_rel[rsl]]
        sem = f"dW2{rsl}"
        w2dma.append((sem, p.op("gpsimd", lambda g, ch=ch, rsl=rsl: g.dma_start(
            w2ring[rsl][:].rearrange("p (j e) -> p j e", j=2),
            w2t.ap()[ch * 256:(ch + 1) * 256, :].rearrange(
                "(j p) e -> p j e", p=128)), waits, sem, 16)))
    for ch in range(7):
        emit_w2dma(ch)
    for ch in range(32):
        rsl = ch % 7
        for j in range(2):
            ft = ch * 2 + j
            for ih in range(2):
                for ec in range(4):
                    bk_ = ih * 4 + ec
                    waits = ()
                    if ft == 0 and ih == 0 and ec == 0:
                        waits = [w2dma[ch], ("vS", vS_hT)]
                        for bb in range(8):
                            waits += bw(bb)
                    elif ih == 0 and ec == 0:
                        waits = [w2dma[ch]]
                    p.op("tensor", lambda t, ft=ft, j=j, ih=ih, ec=ec, rsl=rsl,
                         bk_=bk_: t.matmul(
                             bank(bk_),
                             hT[:, ft * 256 + ih * 128: ft * 256 + ih * 128 + 128],
                             w2ring[rsl][:, j * E + ec * 512: j * E + (ec + 1) * 512],
                             start=(ft == 0), stop=False), waits, "tS")
        w2_rel[rsl] = ("tS", p.now("tS"))
        if ch + 7 < 32:
            emit_w2dma(ch + 7)
    for ih in range(2):
        for ec in range(4):
            bk_ = ih * 4 + ec
            p.op("tensor", lambda t, ec=ec, bk_=bk_: t.matmul(
                bank(bk_), onesrow_b_s[:], b2row[0:1, ec * 512:(ec + 1) * 512],
                start=False, stop=True), [("dB2", dW_b2)] if bk_ == 0 else (),
                "tS")
    tS_W2 = p.now("tS")
    for ih in range(2):
        for ec in range(4):
            bk_ = ih * 4 + ec
            brel[bk_] = ("sS", p.op(
                "scalar", lambda se, ih=ih, ec=ec, bk_=bk_: se.activation(
                    xwsb[ih][:, ec * 512:(ec + 1) * 512], bank(bk_),
                    AF.Identity, scale=wg_s[:, ih:ih + 1]),
                [("tS", tS_W2), psel["vS_wg"]] if bk_ == 0 else (), "sS"))
    sS_xw = p.now("sS")
    dbg_dump("dbg_xw", xwsb[0][:], [("sS", sS_xw)])

    x1dmar = []
    x1r_rel = {}
    def emit_x1dma(st):
        rsl = st % 6
        waits = [("dXA", p.now("dXA")), ("dXB", p.now("dXB")),
                 ("tS", tS_W1)] if st < 6 else [x1r_rel[rsl]]
        sem = f"dR{rsl}"
        x1dmar.append((sem, p.op("sync", lambda s, st=st, rsl=rsl: s.dma_start(
            x1re[rsl][:], x1s.ap()[st * 128:(st + 1) * 128, :]), waits,
            sem, 16)))
    for st in range(6):
        emit_x1dma(st)
    out_rel = [None, None]
    ts_rel = None
    for st in range(NT):
        rsl = st % 6
        osl = st % 2
        bks = [0, 1, 2, 3] if st % 2 == 0 else [4, 5, 6, 7]
        for ec in range(4):
            bk_ = bks[ec]
            for ih in range(2):
                waits = ()
                if ec == 0 and ih == 0:
                    waits = [("sS", sS_xw), psel["vS_P"]]
                    for bb in bks:
                        waits += bw(bb)
                p.op("tensor", lambda t, st=st, ec=ec, ih=ih, bk_=bk_: t.matmul(
                    bank(bk_), Psb[:, ih * S + st * 128: ih * S + (st + 1) * 128],
                    xwsb[ih][:, ec * 512:(ec + 1) * 512],
                    start=(ih == 0), stop=(ih == 1)), waits, "tS")
        tS_d = p.now("tS")
        twaits = [x1dmar[st]]
        if ts_rel is not None:
            twaits.append(ts_rel)
        p.op("scalar", lambda se, st=st, rsl=rsl: se.activation(
            tstage[:], x1re[rsl][:], AF.Identity,
            bias=nbias1_s[:, st:st + 1], scale=rstdp1_s[:, st:st + 1]),
            twaits, "sS")
        sS_t = p.now("sS")
        x1r_rel[rsl] = ("sS", sS_t)
        if st + 6 < NT:
            emit_x1dma(st + 6)
        ovw = [("sS", sS_t), ("tS", tS_d)]
        if out_rel[osl] is not None:
            ovw.append(out_rel[osl])
        vv = None
        for ec in range(4):
            bk_ = bks[ec]
            vv = p.op("vector", lambda v, osl=osl, ec=ec, bk_=bk_: v.tensor_add(
                outstage[osl][:, ec * 512:(ec + 1) * 512],
                tstage[:, ec * 512:(ec + 1) * 512], bank(bk_)),
                ovw if ec == 0 else (), "vS")
            brel[bk_] = ("vS", vv)
        ts_rel = ("vS", vv)
        sem = f"dO{osl}"
        oc = p.op("sync", lambda s, st=st, osl=osl: s.dma_start(
            outt.ap()[st * 128:(st + 1) * 128, :], outstage[osl][:]),
            [("vS", vv)], sem, 16)
        out_rel[osl] = (sem, oc)
    fin = [("dO0", p.now("dO0")), ("dO1", p.now("dO1"))]
    if debug:
        fin.append(("dD", p.now("dD")))
    p.op("sync", lambda s: s.nop(), fin)

    for e in Plan.ENGINES:
        for fn, w, inc, n, _srl in p.ops[e]:
            for snm, v in w:
                assert v <= p.now(snm), (e, snm, v, p.now(snm))


    semnames = set(p.cnt)
    for e in Plan.ENGINES:
        for fn, w, inc, n, _srl in p.ops[e]:
            for snm, v in w:
                semnames.add(snm)
    from contextlib import ExitStack
    with ExitStack() as ctx:
        sems = {s: ctx.enter_context(nc.semaphore(s))
                for s in sorted(semnames)}
        blk = ctx.enter_context(nc.Block())

        def mk(engname):
            def body(eng):
                lw = {}
                cnt = {}
                for fn, waits, inc, inc_n, srl in p.ops[engname]:
                    for snm, v in waits:
                        if lw.get(snm, 0) < v:
                            eng.wait_ge(sems[snm], v)
                            lw[snm] = v
                    instr = fn(eng)
                    if inc is not None:
                        cnt[inc] = cnt.get(inc, 0) + inc_n
                        if instr is None:
                            eng.sem_inc(sems[inc], inc_n)
                        else:
                            instr.then_inc(sems[inc], inc_n)
                        if srl:
                            eng.wait_ge(sems[inc], cnt[inc])
                            lw[inc] = cnt[inc]
            return body
        blk.tensor(mk("tensor"))
        blk.vector(mk("vector"))
        blk.scalar(mk("scalar"))
        blk.sync(mk("sync"))
        blk.gpsimd(mk("gpsimd"))
    return nc


_NC_CACHE = {}


def _get_nc(debug=False):
    if debug not in _NC_CACHE:
        _NC_CACHE[debug] = build(debug=debug)
    return _NC_CACHE[debug]


def kernel(L0_attn_x, L0_attn_Wq, L0_attn_bq, L0_attn_Wk, L0_attn_bk,
           L0_attn_Wv, L0_attn_bv, L0_ffn_Wr, L0_ffn_br, L0_ffn_W1,
           L0_ffn_b1, L0_ffn_W2, L0_ffn_b2):
    f = lambda a: np.ascontiguousarray(np.asarray(a, dtype=np.float32))
    shared = {
        "wq": f(L0_attn_Wq), "bq": f(L0_attn_bq), "wk": f(L0_attn_Wk),
        "bk": f(L0_attn_bk), "wv": f(L0_attn_Wv), "bv": f(L0_attn_bv),
        "wr": f(L0_ffn_Wr), "br": f(L0_ffn_br), "w1": f(L0_ffn_W1),
        "b1": f(L0_ffn_b1), "w2": f(L0_ffn_W2), "b2": f(L0_ffn_b2),
    }
    x = f(L0_attn_x)
    nc = _get_nc(False)
    in_maps = [dict(shared, x0=np.ascontiguousarray(x[b])) for b in range(8)]
    res = run_bass_kernel_spmd(nc, in_maps, core_ids=list(range(8)))
    return np.stack([r["out"] for r in res.results], axis=0)


# version 25
# speedup vs baseline: 1.0360x; 1.0343x over previous
import numpy as np
import ml_dtypes
import concourse.bass as bass
import concourse.mybir as mybir
from concourse.bass_utils import run_bass_kernel_spmd

F32 = mybir.dt.float32
BF16 = mybir.dt.bfloat16
OP = mybir.AluOpType
AF = mybir.ActivationFunctionType
AX = mybir.AxisListType.X
BF = ml_dtypes.bfloat16

S, E, H, G, D = 2048, 2048, 16, 4, 128
NT, NE = 16, 16
KSEL = 256
DFF = 8192
SCL = float(np.float32(1.0) / np.sqrt(np.float32(D)))
EPS = 1e-5
BIG = 1.0e6


class Plan:
    ENGINES = ("tensor", "vector", "scalar", "sync", "gpsimd")

    def __init__(self):
        self.ops = {e: [] for e in self.ENGINES}
        self.cnt = {}

    def now(self, sem):
        return self.cnt.get(sem, 0)

    def op(self, eng, fn, waits=(), inc=None, inc_n=1, serial=False):
        w = []
        for s, v in waits:
            assert v <= self.now(s), f"forward wait {s}>={v} (now {self.now(s)})"
            if v > 0:
                w.append((s, v))
        self.ops[eng].append((fn, w, inc, inc_n, serial))
        if inc is not None:
            self.cnt[inc] = self.cnt.get(inc, 0) + inc_n
        return self.cnt.get(inc, 0)


def _host_consts():
    i128b = np.eye(128, dtype=BF)
    i128f = np.eye(128, dtype=np.float32)
    triu128s = np.triu(np.ones((128, 128), dtype=np.float32), 1).astype(BF)
    triu16s = np.triu(np.ones((16, 16), dtype=np.float32), 1).astype(BF)
    m = np.tril(np.ones((128, 128), dtype=np.float32), -1) * (-1200.0)
    mask512 = np.tile(m, (1, 4)).astype(BF)
    iotab = np.broadcast_to(np.arange(KSEL, dtype=np.float32), (128, KSEL)).astype(BF)
    onesrow_b = np.ones((1, 128), dtype=BF)
    onesrow_f = np.ones((1, 128), dtype=np.float32)
    onescol_b = np.ones((128, 1), dtype=BF)
    epscol = np.full((128, 1), EPS, dtype=np.float32)
    onecol_f = np.ones((128, 1), dtype=np.float32)
    return dict(epscol=epscol, onecol_f=onecol_f, i128b=i128b, i128f=i128f, triu128s=triu128s, triu16s=triu16s,
                mask512=mask512, iotab=iotab, onesrow_b=onesrow_b,
                onesrow_f=onesrow_f, onescol_b=onescol_b)


def build(debug=False):
    nc = bass.Bass("TRN2", target_bir_lowering=False,
                   detect_race_conditions=False)
    p = Plan()

    x0t = nc.dram_tensor("x0", [S, E], F32, kind="ExternalInput")
    wqt = nc.dram_tensor("wq", [H, E, D], F32, kind="ExternalInput")
    bqt = nc.dram_tensor("bq", [H, D], F32, kind="ExternalInput")
    wkt = nc.dram_tensor("wk", [G, E, D], F32, kind="ExternalInput")
    bkt = nc.dram_tensor("bk", [G, D], F32, kind="ExternalInput")
    wvt = nc.dram_tensor("wv", [G, E, D], F32, kind="ExternalInput")
    bvt = nc.dram_tensor("bv", [G, D], F32, kind="ExternalInput")
    wrt = nc.dram_tensor("wr", [E, 1], F32, kind="ExternalInput")
    brt = nc.dram_tensor("br", [1], F32, kind="ExternalInput")
    w1t = nc.dram_tensor("w1", [E, DFF], F32, kind="ExternalInput")
    b1t = nc.dram_tensor("b1", [DFF], F32, kind="ExternalInput")
    w2t = nc.dram_tensor("w2", [DFF, E], F32, kind="ExternalInput")
    b2t = nc.dram_tensor("b2", [E], F32, kind="ExternalInput")
    outt = nc.dram_tensor("out", [S, E], F32, kind="ExternalOutput")
    x1s = nc.dram_tensor("x1s", [S, E], F32,
                         kind="ExternalOutput" if debug else "Internal")
    dbgt = {}
    if debug:
        for nm, sh in [("dbg_logits", [128, 16]), ("dbg_m", [128, 16]),
                       ("dbg_q", [128, 16]), ("dbg_w", [128, 16]),
                       ("dbg_cg", [128, 16]),
                       ("dbg_fxT", [128, NE * 256]), ("dbg_y0T", [128, S]),
                       ("dbg_kt", [128, S]), ("dbg_qh", [128, S]),
                       ("dbg_hT", [128, 2048]), ("dbg_xw", [128, E]),
                       ("dbg_y1", [128, E]), ("dbg_wg", [128, 2]),
                       ("dbg_rp", [128, 16]), ("dbg_y1T", [128, E]),
                       ("dbg_sums", [128, 16]), ("dbg_sumsq", [128, 16])]:
            dbgt[nm] = nc.dram_tensor(nm, sh, F32, kind="ExternalOutput")

    cn = _host_consts()
    inl = {k: nc.inline_tensor(v, k) for k, v in cn.items()}

    A_ = lambda name, shape, dt, off: nc.alloc_sbuf_tensor_at(
        name, shape, dt, offset=off)
    o_const = 16512
    o_M = 23936
    o_F = 28032
    o_xw = 23936
    o_A = 32128
    o_B = 97664
    o_C = 163200
    o_D = 196480
    o_E = 204672

    off = o_const
    def calloc(name, shape, dt):
        nonlocal off
        nb = (shape[1] * mybir.dt.size(dt) + 31) // 32 * 32
        t = A_(name, shape, dt, off)
        off += nb
        return t
    i128b_s = calloc("i128b", [128, 128], BF16)
    triu128s_s = calloc("triu128s", [128, 128], BF16)
    mask512_s = calloc("mask512", [128, 512], BF16)
    iotab_s = calloc("iotab", [128, KSEL], BF16)
    i128f_s = calloc("i128f", [128, 128], F32)
    triu16s_s = calloc("triu16s", [16, 16], BF16)
    onesrow_b_s = calloc("onesrow_b", [1, 128], BF16)
    onesrow_f_s = calloc("onesrow_f", [1, 128], F32)
    onescol_b_s = calloc("onescol_b", [128, 1], BF16)
    epscol_s = calloc("epscol", [128, 1], F32)
    onecol_f_s = calloc("onecol_f", [128, 1], F32)
    logits_s = calloc("logits", [128, 16], F32)
    cg_s = calloc("cg", [128, 16], F32)
    m_s = calloc("msel", [128, 16], F32)
    mb_s = calloc("mb", [128, 16], BF16)
    q_s = calloc("qsel", [128, 16], F32)
    t1_s = calloc("t1", [128, 16], F32)
    lm_s = calloc("lm", [128, 16], F32)
    wu_s = calloc("wu", [128, 16], F32)
    w_s = calloc("wsel", [128, 16], F32)
    wb_s = calloc("wb", [128, 16], BF16)
    wg_s = calloc("wg", [128, 2], F32)
    rstdp1_s = calloc("rstdp1", [128, 16], F32)
    nbias1_s = calloc("nbias1", [128, 16], F32)
    bqS = calloc("bqS", [128, 16], F32)
    bkS = calloc("bkS", [128, 4], F32)
    wrS = calloc("wrS", [128, 16], BF16)
    brcol = calloc("brcol", [128, 1], F32)
    bv_row = calloc("bv_row", [1, 512], BF16)
    diagstage = A_("diagstage", [128, 512], BF16,
                   nc.lookup_mloc(bv_row).addr)
    stat_a = calloc("stat_a", [128, 1], F32)
    stat_b = calloc("stat_b", [128, 1], F32)
    stat_c = calloc("stat_c", [128, 1], F32)
    stat_d = calloc("stat_d", [128, 1], F32)
    stat_e = calloc("stat_e", [128, 1], F32)
    stat_f = calloc("stat_f", [128, 1], F32)
    stat_g = calloc("stat_g", [128, 1], F32)
    stat_h = calloc("stat_h", [128, 1], F32)
    stat_r = calloc("stat_r", [128, 1], F32)
    stat_c2 = calloc("stat_c2", [128, 1], F32)
    stata4 = calloc("stata4", [128, 4], F32)
    statb4 = calloc("statb4", [128, 4], F32)
    sdum4 = calloc("sdum4", [128, 4], F32)
    sdum4b = calloc("sdum4b", [128, 4], F32)
    stat_a2 = calloc("stat_a2", [128, 1], F32)
    stat_b2 = calloc("stat_b2", [128, 1], F32)
    stat_e2 = calloc("stat_e2", [128, 1], F32)
    stat_r2 = calloc("stat_r2", [128, 1], F32)
    stat_a3 = calloc("stat_a3", [128, 1], F32)
    stat_b3 = calloc("stat_b3", [128, 1], F32)
    stat_e3 = calloc("stat_e3", [128, 1], F32)
    stat_r3 = calloc("stat_r3", [128, 1], F32)
    stat_a4 = calloc("stat_a4", [128, 1], F32)
    stat_b4 = calloc("stat_b4", [128, 1], F32)
    stat_e4 = calloc("stat_e4", [128, 1], F32)
    stat_r4 = calloc("stat_r4", [128, 1], F32)
    sdum = calloc("sdum", [128, 1], F32)
    hkcol = calloc("hkcol", [128, 1], F32)
    nlog2 = calloc("nlog2", [128, 2], F32)
    cgalt2 = calloc("cgalt2", [128, 2], F32)
    vdum = calloc("vdum", [128, 1], F32)
    lm2_s = calloc("lm2", [128, 16], F32)
    dbg_sums = calloc("dbg_sums_sb", [128, 16], F32)
    dbg_sumsq = calloc("dbg_sumsq_sb", [128, 16], F32)
    recipz = calloc("recipz", [128, 4], F32)
    mx1_s = calloc("mx1", [128, 1], F32)
    negmx_s = calloc("negmx", [128, 1], F32)
    zc_s = calloc("zc", [128, 1], F32)
    rz_s = calloc("rz", [1, 1], F32)
    rzc_s = calloc("rzc", [128, 1], F32)
    ts16_s = calloc("ts16", [16, 1], BF16)
    offs_s = calloc("offs", [16, 1], BF16)
    offsrow_s = calloc("offsrow", [1, 16], BF16)
    assert off <= o_M, f"consts overflow: {off} > {o_M}"

    y0T = A_("y0T", [128, NE * S], BF16, o_A)
    y1 = A_("y1", [128, NT * E], BF16, o_A)
    hT = A_("hT", [128, 64 * 256], BF16, o_A)
    Qall = A_("Qall", [128, H * S], BF16, o_B)
    wkS = A_("wkS", [128, G * E], BF16, o_B)
    wvS = A_("wvS", [128, G * E], BF16, o_B + 16384)
    w1ring = [A_(f"w1r{i}", [128, NE * 512], BF16, o_B + i * 16384)
              for i in range(4)]
    x1re = [A_(f"x1re{i}", [128, E], F32, o_B + i * 8192) for i in range(6)]
    outstage = [A_(f"outst{i}", [128, E], F32, o_B + 49152 + i * 8192)
                for i in range(2)]
    tstage = A_("tstage", [128, E], F32, o_E)
    tstage1 = A_("tstage1", [128, E], F32, o_E + 8192)
    KT = A_("KT", [128, G * S], BF16, o_C)
    Vs = A_("Vs", [128, G * 16 * 132], BF16, o_C + 16384)
    fxT = A_("fxT", [128, NE * 256], BF16, o_C)
    logrow = A_("logrow", [1, S], F32, o_C)
    LB = A_("LB", [128, S], F32, o_C + 8192)
    b1row = A_("b1row", [1, DFF], BF16, o_C + 8192)
    PT = A_("PT", [128, NT * 256], BF16, o_E + 8192)
    gdump = A_("gdump", [128, S], F32, o_C + 16384)
    log16 = A_("log16", [16, 128], F32, o_C + 16384)
    Psb = A_("Psb", [128, 2 * S], BF16, o_C + 24576)
    ones2k = A_("ones2k", [128, S], F32, o_C + 24576)
    w2ring = ([A_(f"w2ra{i}", [128, 2 * E], BF16, o_A + 32768 + i * 8192)
               for i in range(4)] +
              [A_(f"w2rc{i}", [128, 2 * E], BF16, o_C + [16384, 0, 8192][i])
               for i in range(3)])
    wqS = [A_(f"wqS{i}", [128, E], BF16, o_D + i * 4096) for i in range(2)]
    y0tB = A_("y0tB", [128, E], BF16, o_D)
    attnrowB = A_("attnrowB", [128, E], F32, o_D)
    b2row = A_("b2row", [1, E], BF16, o_D)
    x0tile = [A_(f"x0t{i}", [128, E], F32, o_E + i * 8192) for i in range(2)]
    x0p = [A_(f"x0p{i}", [128, E], F32, o_B + 32768 + i * 8192)
           for i in range(4)]
    y0p = [A_(f"y0p{i}", [128, E], BF16, o_E + 8192 + i * 4096)
           for i in range(4)]
    y0tA = A_("y0tA", [128, E], BF16, o_E + 16384)
    attnrowA = A_("attnrowA", [128, E], F32, o_E + 16384)
    attnrow = [attnrowA, attnrowB]
    WrB = A_("WrB", [128, E], BF16, o_M)
    wrrow = A_("wrrow", [1, 512], BF16, o_F)
    expTbig = A_("expTbig", [128, 2048], BF16, o_F)
    expT = [expTbig[:, i * 512:(i + 1) * 512] for i in range(4)]
    expT.append(diagstage[:])
    bq_nat = A_("bq_nat", [16, 128], F32, o_F)
    bk_nat = A_("bk_nat", [4, 128], F32, o_F + 512)
    wr_nat = A_("wr_nat", [16, 128], F32, o_F + 1024)
    br_nat = A_("br_nat", [1, 1], F32, o_F + 1536)
    htile = [A_(f"htile{i}", [128, 512], BF16, o_F + i * 1024) for i in range(4)]
    xwsb = [A_(f"xw{i}", [128, E], BF16, o_xw + i * 4096) for i in range(2)]

    PS = nc.alloc_psum_tensor("PS", [128, 4096], F32)
    bank = lambda i: PS[:, i * 512:(i + 1) * 512]
    brel = {i: None for i in range(8)}

    def bw(i):
        return [brel[i]] if brel[i] is not None else []

    def dbg_dump(nm, src, waits, f32src=False):
        if not debug:
            return
        eng = "sync" if f32src else "gpsimd"
        p.op(eng, lambda s, nm=nm, src=src: s.dma_start(dbgt[nm].ap(), src),
             waits, "dD", 16)

    for name, sb in [("i128b", i128b_s), ("triu128s", triu128s_s),
                     ("mask512", mask512_s), ("iotab", iotab_s),
                     ("i128f", i128f_s), ("triu16s", triu16s_s),
                     ("onesrow_b", onesrow_b_s), ("onesrow_f", onesrow_f_s),
                     ("onescol_b", onescol_b_s), ("epscol", epscol_s),
                     ("onecol_f", onecol_f_s)]:
        ap = inl[name].ap()
        p.op("gpsimd", lambda g, sb=sb, ap=ap: g.dma_start(sb[:], ap), (), "dC", 16)
    p.op("sync", lambda s: s.dma_start(bq_nat[:], bqt.ap()), (), "dP", 16)
    p.op("sync", lambda s: s.dma_start(bk_nat[:], bkt.ap()), (), "dP", 16)
    p.op("sync", lambda s: s.dma_start(
        wr_nat[:], wrt.ap().rearrange("(t p) o -> t p o", t=16)), (), "dP", 16)
    p.op("sync", lambda s: s.dma_start(
        br_nat[:], brt.ap().rearrange("(a b) -> a b", a=1)), (), "dP", 16)
    p.op("gpsimd", lambda g: g.dma_start(
        bv_row[:], bvt.ap().rearrange("g d -> (g d)").rearrange(
            "(a f) -> a f", a=1)), (), "dC", 16)
    for g_ in range(G):
        p.op("gpsimd", lambda g, g_=g_: g.dma_start(
            wkS[:, g_ * E:(g_ + 1) * E].rearrange("p (t d) -> p t d", t=16),
            wkt.ap()[g_].rearrange("(t p) d -> p t d", p=128)), (), "dKV", 16)
        p.op("gpsimd", lambda g, g_=g_: g.dma_start(
            wvS[:, g_ * E:(g_ + 1) * E].rearrange("p (t d) -> p t d", t=16),
            wvt.ap()[g_].rearrange("(t p) d -> p t d", p=128)), (), "dKV", 16)
    dC_tot = p.now("dC")
    dKV_tot = p.now("dKV")
    w_cd = [("dP", p.now("dP")), ("dC", dC_tot)]
    p.op("tensor", lambda t: t.matmul(PS[:, 3072:3088], bq_nat[:],
                                      i128f_s[0:16, 0:16], start=True, stop=True),
         w_cd, "tS")
    p.op("tensor", lambda t: t.matmul(PS[:, 3088:3092], bk_nat[:],
                                      i128f_s[0:4, 0:4], start=True, stop=True),
         (), "tS")
    p.op("tensor", lambda t: t.matmul(PS[:, 3104:3120], wr_nat[:],
                                      i128f_s[0:16, 0:16], start=True, stop=True),
         (), "tS")
    p.op("tensor", lambda t: t.matmul(PS[:, 3120:3121], onesrow_f_s[:],
                                      br_nat[:], start=True, stop=True), (), "tS")
    tS_par = p.now("tS")

    x0dma = []
    x0rel = [None, None, None, None]
    y0rel = [None, None, None, None]
    def emit_x0dma(st):
        sl = st % 4
        waits = [x0rel[sl]] if x0rel[sl] is not None else []
        sem = f"dX{sl}"
        x0dma.append((sem, p.op("sync", lambda s, st=st, sl=sl: s.dma_start(
            x0p[sl][:], x0t.ap()[st * 128:(st + 1) * 128, :]), waits, sem, 16)))
    for _st in range(4):
        emit_x0dma(_st)
    SA4 = [stat_a, stat_a2, stat_a3, stat_a4]
    SB4 = [stat_b, stat_b2, stat_b3, stat_b4]
    SE4 = [stat_e, stat_e2, stat_e3, stat_e4]
    SR4 = [stat_r, stat_r2, stat_r3, stat_r4]

    def emit_stats(st):
        sl = st % 4
        sb = SB4[st % 4]
        wd = [x0dma[st]]
        ywaits0 = [y0rel[sl]] if y0rel[sl] is not None else []
        p.op("vector", lambda v, sl=sl, sb=sb: v.scalar_tensor_tensor(
            y0p[sl][:], x0p[sl][:], 1.0, x0p[sl][:], OP.mult, OP.mult,
            accum_out=sb[:]), wd + ywaits0, "vS")
        return p.now("vS")

    def emit_sum(st):
        sl = st % 4
        sa = SA4[st % 4]
        p.op("scalar", lambda se, sl=sl, sa=sa: se.activation(
            tstage[:], x0p[sl][:], AF.Identity, accum_out=sa[:]),
            [x0dma[st]], "sS")

    vS_stats_l = {}
    y0T_tdone = {}
    y0ev = {}

    def emit_p0_part1(st):
        sl = st % 4
        y0t_ = y0p[sl]
        sa, sb = SA4[st % 4], SB4[st % 4]
        se_, sr_ = SE4[st % 4], SR4[st % 4]
        p.op("scalar", lambda se: se.copy(sdum[:], onecol_f_s[:]),
             [("vS", vS_stats_l[st])], "sS")
        p.op("scalar", lambda se: se.copy(sdum[:], onecol_f_s[:]), (), "sS")
        if debug:
            p.op("scalar", lambda se, st=st, sa=sa: se.copy(
                dbg_sums[:, st:st + 1], sa[:]), (), "sS")
            p.op("scalar", lambda se, st=st, sb=sb: se.copy(
                dbg_sumsq[:, st:st + 1], sb[:]), (), "sS")
        p.op("scalar", lambda se, sa=sa: se.mul(
            stat_c[:], sa[:], 1.0 / E), (), "sS")
        p.op("scalar", lambda se, sb=sb: se.mul(
            stat_d[:], sb[:], 1.0 / E), (), "sS")
        p.op("scalar", lambda se: se.activation(
            stat_f[:], stat_c[:], AF.Square), (), "sS")
        p.op("scalar", lambda se: se.activation(
            stat_g[:], stat_f[:], AF.Identity, bias=stat_d[:], scale=-1.0),
            (), "sS")
        p.op("scalar", lambda se: se.activation(
            stat_h[:], stat_g[:], AF.Ln, bias=epscol_s[:]), (), "sS")
        p.op("scalar", lambda se, sr_=sr_: se.activation(
            sr_[:], stat_h[:], AF.Exp, scale=-0.5), (), "sS")
        p.op("scalar", lambda se: se.mul(stat_c2[:], stat_c[:], -1.0), (), "sS")
        p.op("scalar", lambda se, se_=se_, sr_=sr_: se.activation(
            se_[:], stat_c2[:], AF.Identity, scale=sr_[:]), (), "sS")
        sS_ch0 = p.now("sS")
        ywaits = [("sS", sS_ch0)]
        if y0rel[sl] is not None:
            ywaits.append(y0rel[sl])
        p.op("vector", lambda v, sl=sl, y0t_=y0t_, se_=se_, sr_=sr_:
             v.tensor_scalar(y0t_[:], x0p[sl][:], sr_[:], se_[:],
                             OP.mult, OP.add), ywaits, "vS")
        y0ev[st] = ("vS", p.now("vS"))
        x0rel[sl] = y0ev[st]
        if st + 4 < NT:
            emit_x0dma(st + 4)

    def emit_p0_part2(st):
        sl = st % 4
        y0t_ = y0p[sl]
        for grp in range(4):
            bk_ = 6 if grp % 2 == 0 else 7
            for j in range(4):
                et = grp * 4 + j
                twaits = [y0ev[st]] + bw(bk_) if j == 0 else ()
                p.op("tensor", lambda t, y0t_=y0t_, et=et, bk_=bk_, j=j: t.matmul(
                    PS[:, bk_ * 512 + j * 128: bk_ * 512 + (j + 1) * 128],
                    y0t_[:, et * 128:(et + 1) * 128], i128b_s[:],
                    start=True, stop=True), twaits, "tS")
            tS_tr = p.now("tS")
            def ev(v, grp=grp, st=st, bk_=bk_):
                dst = y0T[:].rearrange("p (e s) -> p e s", e=NE)[
                    :, grp * 4:grp * 4 + 4, st * 128:(st + 1) * 128]
                return v.tensor_copy(dst, bank(bk_))
            brel[bk_] = ("vS", p.op("vector", ev, [("tS", tS_tr)], "vS"))
        y0rel[sl] = ("tS", p.now("tS"))
        y0T_tdone[st] = ("vS", p.now("vS"))

    for st in range(4):
        emit_sum(st)
        vS_stats_l[st] = emit_stats(st)
    for st in range(4):
        emit_p0_part1(st)
    p.op("vector", lambda v: v.tensor_copy(wrS[:], PS[:, 3104:3120]),
         [("tS", tS_par)], "vS")
    p.op("vector", lambda v: v.tensor_copy(brcol[:], PS[:, 3120:3121]), (), "vS")
    p.op("vector", lambda v: v.memset(
        Vs[:].rearrange("p (g k c) -> p g k c", g=G, k=16)[:, :, :, 128:129],
        1.0), (), "vS")
    vS_par2 = p.now("vS")
    p.op("scalar", lambda se: se.copy(bqS[:], PS[:, 3072:3088]),
         [("tS", tS_par), ("vS", vS_par2)], "sS")
    p.op("scalar", lambda se: se.copy(bkS[:], PS[:, 3088:3092]), (), "sS")
    brel[6] = ("sS", p.now("sS"))
    brel[7] = brel[6]
    for cch in range(4):
        for st in range(4 * cch, 4 * cch + 4):
            emit_p0_part2(st)
        for g_ in range(G):
            bk_ = 4 + (g_ % 2)
            for et in range(NE):
                waits = ()
                if et == 0:
                    waits = [y0T_tdone[4 * cch + 3]] + bw(bk_)
                    if cch == 0 and g_ == 0:
                        waits = waits + [("dKV", dKV_tot)]
                p.op("tensor", lambda t, g_=g_, cch=cch, et=et, bk_=bk_: t.matmul(
                    bank(bk_), wkS[:, g_ * E + et * 128:g_ * E + (et + 1) * 128],
                    y0T[:, et * S + cch * 512: et * S + (cch + 1) * 512],
                    start=(et == 0), stop=(et == NE - 1)), waits, "tS")
            tS_k = p.now("tS")
            brel[bk_] = ("sS", p.op(
                "scalar", lambda se, g_=g_, cch=cch, bk_=bk_: se.activation(
                    KT[:, g_ * S + cch * 512: g_ * S + (cch + 1) * 512],
                    bank(bk_), AF.Identity, bias=bkS[:, g_:g_ + 1]),
                [("tS", tS_k)], "sS"))
        if cch + 1 < 4:
            for st in range(4 * cch + 4, 4 * cch + 8):
                emit_sum(st)
                vS_stats_l[st] = emit_stats(st)
        for st in range(4 * cch, 4 * cch + 4):
            for et in range(NE):
                for g_ in range(G):
                    waits = ()
                    if et == 0:
                        waits = bw(g_)
                        if st == 0 and g_ == 0:
                            waits = waits + [("dKV", dKV_tot)]
                    p.op("tensor", lambda t, st=st, et=et, g_=g_: t.matmul(
                        PS[:, g_ * 512:g_ * 512 + 128],
                        y0T[:, et * S + st * 128: et * S + (st + 1) * 128],
                        wvS[:, g_ * E + et * 128: g_ * E + (et + 1) * 128],
                        start=(et == 0), stop=False), waits, "tS")
            for g_ in range(G):
                p.op("tensor", lambda t, g_=g_: t.matmul(
                    PS[:, g_ * 512:g_ * 512 + 128], onesrow_b_s[:],
                    bv_row[0:1, g_ * 128:(g_ + 1) * 128], start=False,
                    stop=True), (), "tS")
            tS_v = p.now("tS")
            for g_ in range(G):
                brel[g_] = ("vS", p.op(
                    "vector", lambda v, st=st, g_=g_: v.tensor_copy(
                        Vs[:, g_ * 2112 + st * 132: g_ * 2112 + st * 132 + 128],
                        PS[:, g_ * 512:g_ * 512 + 128]), [("tS", tS_v)], "vS"))
        if cch + 1 < 4:
            for st in range(4 * cch + 4, 4 * cch + 8):
                emit_p0_part1(st)
    vS_y0T = p.now("vS")
    tS_P0 = p.now("tS")
    for c_ in range(4):
        dwr = p.op("gpsimd", lambda g, c_=c_: g.dma_start(
            wrrow[:], wrt.ap()[c_ * 512:(c_ + 1) * 512, :].rearrange(
                "(a f) o -> a (f o)", a=1)),
            [] if c_ == 0 else [("sS", p.now("sS"))], "dWR", 16)
        p.op("tensor", lambda t, c_=c_: t.matmul(
            PS[:, 3584:4096], onesrow_b_s[:], wrrow[:],
            start=True, stop=True), [("dWR", dwr)] + (bw(7) if c_ == 0 else []),
            "tS")
        p.op("scalar", lambda se, c_=c_: se.copy(
            WrB[:, c_ * 512:(c_ + 1) * 512], PS[:, 3584:4096]),
            [("tS", p.now("tS"))], "sS")
    brel[7] = ("sS", p.now("sS"))
    sS_wrb = p.now("sS")
    dbg_dump("dbg_y0T", y0T[:, 0:S], [("vS", vS_y0T)])
    dbg_dump("dbg_sums", dbg_sums[:], [("vS", vS_y0T)], f32src=True)
    dbg_dump("dbg_sumsq", dbg_sumsq[:], [("vS", vS_y0T)], f32src=True)
    sS_K = p.now("sS")
    dbg_dump("dbg_kt", KT[:, 0:S], [("sS", sS_K)])
    sS_P1 = p.now("sS")
    vS_P1 = p.now("vS")

    wqdma = []
    wq_rel = [None, None]
    def emit_wqdma(h_):
        sl = h_ % 2
        waits = []
        if wq_rel[sl] is not None:
            waits.append(wq_rel[sl])
        sem = f"dQ{sl}"
        wqdma.append((sem, p.op("gpsimd", lambda g, h_=h_, sl=sl: g.dma_start(
            wqS[sl][:].rearrange("p (t d) -> p t d", t=16),
            wqt.ap()[h_].rearrange("(t p) d -> p t d", p=128)),
            waits, sem, 16)))
    emit_wqdma(0)
    emit_wqdma(1)
    qevac = []
    qpair = 0
    for h_ in range(H):
        sl = h_ % 2
        for pair in range(2):
            b0 = 4 + 2 * (qpair % 2)
            qpair += 1
            for et in range(NE):
                for cc in range(2):
                    cch = pair * 2 + cc
                    bk_ = b0 + cc
                    waits = ()
                    if et == 0 and cc == 0:
                        waits = [wqdma[h_]] + bw(b0) + bw(b0 + 1)
                    p.op("tensor",
                         lambda t, sl=sl, cch=cch, et=et, bk_=bk_: t.matmul(
                             bank(bk_), wqS[sl][:, et * 128:(et + 1) * 128],
                             y0T[:, et * S + cch * 512: et * S + (cch + 1) * 512],
                             start=(et == 0), stop=(et == NE - 1)), waits, "tS")
            tS_q = p.now("tS")
            for cc in range(2):
                cch = pair * 2 + cc
                bk_ = b0 + cc
                brel[bk_] = ("sS", p.op(
                    "scalar", lambda se, h_=h_, cch=cch, bk_=bk_: se.activation(
                        Qall[:, h_ * S + cch * 512: h_ * S + (cch + 1) * 512],
                        bank(bk_), AF.Identity, bias=bqS[:, h_:h_ + 1]),
                    [("tS", tS_q)], "sS"))
        qevac.append(p.now("sS"))
        wq_rel[sl] = ("tS", p.now("tS"))
        if h_ + 2 < H:
            emit_wqdma(h_ + 2)
    dbg_dump("dbg_qh", Qall[:, 0:S], [("sS", qevac[0])])

    chunk_avdone = []
    chunk_expdone = []
    x0dma3 = []
    x0rel3 = [None, None]
    attnrow_rel = [None, None]
    av_rel = [("vS", vS_P1)]
    x1dma = []
    y1evac = []
    tail_defer = []

    def flush_tail():
        for fn in tail_defer:
            fn()
        tail_defer.clear()
    chunk_id = 0
    qall_r = Qall[:].rearrange("p (h s) -> p h s", h=H)
    vs_r = Vs[:]
    for qt in range(NT):
        sl3 = qt % 2
        ar = qt % 2
        waits = [x0rel3[sl3]] if x0rel3[sl3] is not None else [("sS", qevac[0])]
        sem = f"dX{sl3}"
        x0dma3.append((sem, p.op("sync", lambda s, qt=qt, sl3=sl3: s.dma_start(
            x0tile[sl3][:], x0t.ap()[qt * 128:(qt + 1) * 128, :]), waits,
            sem, 16)))
        slot_user = {}
        uctr = [0]
        carry = [None]
        for g_ in range(G):
            pendq = []

            def emit_av(pend, g_=g_):
                for (cid, esl, kt, avw, sflag, eflag) in pend:
                    if sflag:
                        avw = avw + av_rel
                    for m_ in range(4):
                        p.op("tensor",
                             lambda t, m_=m_, esl=esl, g_=g_, kt=kt,
                             sflag=sflag, eflag=eflag:
                             t.matmul(
                                 PS[:, m_ * 512:m_ * 512 + 129],
                                 expT[esl][:, m_ * 128:(m_ + 1) * 128],
                                 vs_r[:, g_ * 2112 + kt * 132:
                                      g_ * 2112 + kt * 132 + 129],
                                 start=sflag, stop=eflag),
                             avw if m_ == 0 else (), "tS")
                    while len(chunk_avdone) <= cid:
                        chunk_avdone.append(0)
                    chunk_avdone[cid] = p.now("tS")
                    slot_user[esl] = p.now("tS")

            def emit_diag(gg, qt=qt):
                nonlocal chunk_id
                uslot = uctr[0] % 2
                uctr[0] += 1
                cid = chunk_id
                chunk_id += 1
                sbk = 4 + 2 * uslot
                swaits = [("sS", qevac[gg * 4 + 3])] + bw(sbk)
                p.op("tensor", lambda t, gg=gg, qt=qt, sbk=sbk: t.matmul(
                    bank(sbk),
                    KT[:, gg * S + qt * 128:gg * S + (qt + 1) * 128],
                    qall_r[:, gg * 4:(gg + 1) * 4, qt * 128:(qt + 1) * 128],
                    start=True, stop=False), swaits, "tS")
                p.op("tensor", lambda t, sbk=sbk: t.matmul(
                    bank(sbk), i128b_s[:], mask512_s[:],
                    start=False, stop=True), (), "tS")
                tS_sc = p.now("tS")
                ewaits = [("tS", tS_sc)]
                if 4 in slot_user:
                    ewaits.append(("tS", slot_user[4]))
                rel = p.op("scalar", lambda se, sbk=sbk: se.activation(
                    expT[4][:], bank(sbk), AF.Exp, scale=SCL), ewaits, "sS")
                brel[sbk] = ("sS", rel)
                return [(cid, 4, qt, [("sS", rel)], True, qt == 0)]

            units = [(qt,)]
            nkt = qt + 1
            j = 0
            while j + 1 < nkt - 1:
                units.append((j, j + 1))
                j += 2
            if j < nkt - 1:
                units.append((j,))
                j += 1
            cbase = 0
            if carry[0] is not None:
                pendq.append(carry[0])
                carry[0] = None
                units = units[1:]
                cbase = 1

            for ui, unit in enumerate(units):
                uslot = uctr[0] % 2
                uctr[0] += 1
                scw_extra = []
                if qt == 0 and g_ == 0 and ui == 0:
                    scw_extra = [("vS", vS_P1), ("sS", sS_P1)]
                ch = []
                for k_i, kt in enumerate(unit):
                    cid = chunk_id
                    chunk_id += 1
                    if len(unit) == 2:
                        esl = 2 * uslot + k_i
                    else:
                        esl = 4 if unit[0] == qt else 2 * uslot
                    sbk = 4 + 2 * uslot + k_i if len(unit) == 2 else 4 + 2 * uslot
                    swaits = [("sS", qevac[g_ * 4 + 3])] + scw_extra + bw(sbk)
                    scw_extra = []
                    is_d = (kt == qt)
                    p.op("tensor",
                         lambda t, g_=g_, qt=qt, kt=kt, sbk=sbk, is_d=is_d:
                         t.matmul(
                             bank(sbk),
                             KT[:, g_ * S + kt * 128:g_ * S + (kt + 1) * 128],
                             qall_r[:, g_ * 4:(g_ + 1) * 4,
                                    qt * 128:(qt + 1) * 128],
                             start=True, stop=not is_d), swaits, "tS")
                    if is_d:
                        mkw = [("dC", dC_tot)] if (qt == 0 and g_ == 0) else ()
                        p.op("tensor", lambda t, sbk=sbk: t.matmul(
                            bank(sbk), i128b_s[:], mask512_s[:],
                            start=False, stop=True), mkw, "tS")
                    ch.append((cid, esl, kt, sbk))
                tS_sc = p.now("tS")
                is_diag = (unit[-1] == qt)
                ewaits = [("tS", tS_sc)]
                for (cid, esl, kt, sbk) in ch:
                    if esl in slot_user:
                        ewaits.append(("tS", slot_user[esl]))
                if len(ch) == 2:
                    (c0, e0, k0, b0_), (c1, e1, k1, b1_) = ch
                    rel = p.op("scalar", lambda se, e0=e0, b0_=b0_: se.activation(
                        expTbig[:, e0 * 512:(e0 + 2) * 512],
                        PS[:, b0_ * 512:(b0_ + 2) * 512], AF.Exp, scale=SCL),
                        ewaits, "sS")
                    brel[b0_] = ("sS", rel)
                    brel[b0_ + 1] = ("sS", rel)
                    avw0 = [("sS", rel)]
                    avw1 = []
                else:
                    (c0, e0, k0, b0_) = ch[0]
                    rel = p.op("scalar", lambda se, e0=e0, b0_=b0_:
                               se.activation(expT[e0][:], bank(b0_), AF.Exp,
                                             scale=SCL), ewaits, "sS")
                    brel[b0_] = ("sS", rel)
                    avw0 = [("sS", rel)]
                    if is_diag and g_ == 0 and tail_defer:
                        flush_tail()
                pend = []
                nch_before = cbase + sum(len(u) for u in units[:ui])
                for idx, (cid, esl, kt, sbk) in enumerate(ch):
                    avw = avw0 if idx == 0 else []
                    first_chunk = (nch_before + idx == 0)
                    last_chunk = (nch_before + idx == nkt - 1)
                    pend.append((cid, esl, kt, avw, first_chunk, last_chunk))
                pendq.append(pend)
                if len(pendq) > 1:
                    emit_av(pendq.pop(0))
            for pend in pendq:
                emit_av(pend)
            if g_ < 3:
                carry[0] = emit_diag(g_ + 1)
            tS_av = p.now("tS")
            zap = PS[:].rearrange("p (b c) -> p b c", b=8)[:, 0:4, 128:129]
            zap = zap.rearrange("p b o -> p (b o)")
            vS_rz = p.op("vector", lambda v, zap=zap: v.reciprocal(
                recipz[:], zap), [("tS", tS_av)], "vS")
            p.op("vector", lambda v: v.tensor_copy(vdum[:], recipz[:, 0:1]),
                 (), "vS")
            arw = [x0dma3[qt]]
            if g_ == 0 and attnrow_rel[ar] is not None:
                arw.append(attnrow_rel[ar])
            for m_ in range(4):
                p.op("vector", lambda v, m_=m_, g_=g_, ar=ar, sl3=sl3:
                     v.scalar_tensor_tensor(
                         attnrow[ar][:, g_ * 512 + m_ * 128:
                                     g_ * 512 + (m_ + 1) * 128],
                         PS[:, m_ * 512:m_ * 512 + 128],
                         recipz[:, m_:m_ + 1],
                         x0tile[sl3][:, g_ * 512 + m_ * 128:
                                     g_ * 512 + (m_ + 1) * 128],
                         OP.mult, OP.add), arw if m_ == 0 else (), "vS")
            av_rel = [("vS", p.now("vS"))]
            p.op("vector", lambda v, g_=g_, ar=ar: v.tensor_reduce(
                stata4[:, g_:g_ + 1],
                attnrow[ar][:, g_ * 512:(g_ + 1) * 512], AX, OP.add), (), "vS")
            p.op("vector", lambda v, g_=g_, ar=ar, qt=qt:
                 v.scalar_tensor_tensor(
                     y1[:, qt * E + g_ * 512:qt * E + (g_ + 1) * 512],
                     attnrow[ar][:, g_ * 512:(g_ + 1) * 512], 1.0,
                     attnrow[ar][:, g_ * 512:(g_ + 1) * 512],
                     OP.mult, OP.mult, accum_out=statb4[:, g_:g_ + 1]),
                 (), "vS")
        vS_add = p.now("vS")
        x0rel3[sl3] = ("vS", vS_add)
        sem = "dXA" if ar == 0 else "dXB"
        x1dma.append((sem, p.op("sync", lambda s, qt=qt, ar=ar: s.dma_start(
            x1s.ap()[qt * 128:(qt + 1) * 128, :], attnrow[ar][:]),
            [("vS", vS_add)], sem, 16)))
        vS_stats = p.now("vS")

        def tail_all(qt=qt, ar=ar, sl3=sl3, vS_stats=vS_stats):
            p.op("scalar", lambda se: se.copy(sdum[:], onecol_f_s[:]),
                 [("vS", vS_stats)], "sS")
            p.op("scalar", lambda se: se.copy(sdum[:], onecol_f_s[:]), (), "sS")
            p.op("scalar", lambda se: se.activation(
                sdum4[:], stata4[:], AF.Identity, accum_out=stat_a[:]),
                (), "sS")
            p.op("scalar", lambda se: se.activation(
                sdum4b[:], statb4[:], AF.Identity, accum_out=stat_b[:]),
                (), "sS")
            p.op("scalar", lambda se: se.copy(sdum[:], onecol_f_s[:]), (), "sS")
            p.op("scalar", lambda se: se.copy(sdum[:], onecol_f_s[:]), (), "sS")
            p.op("scalar", lambda se: se.mul(
                stat_c[:], stat_a[:], 1.0 / E), (), "sS")
            p.op("scalar", lambda se: se.mul(
                stat_d[:], stat_b[:], 1.0 / E), (), "sS")
            p.op("scalar", lambda se: se.activation(
                stat_f[:], stat_c[:], AF.Square), (), "sS")
            p.op("scalar", lambda se: se.activation(
                stat_g[:], stat_f[:], AF.Identity, bias=stat_d[:], scale=-1.0),
                (), "sS")
            p.op("scalar", lambda se: se.activation(
                stat_h[:], stat_g[:], AF.Ln, bias=epscol_s[:]), (), "sS")
            p.op("scalar", lambda se: se.activation(
                stat_r[:], stat_h[:], AF.Exp, scale=-0.5), (), "sS")
            p.op("scalar", lambda se: se.mul(
                stat_c2[:], stat_c[:], -1.0), (), "sS")
            p.op("scalar", lambda se: se.activation(
                stat_e[:], stat_c2[:], AF.Identity, scale=stat_r[:]), (), "sS")
            p.op("scalar", lambda se, qt=qt: se.copy(
                nbias1_s[:, qt:qt + 1], stat_e[:]), (), "sS")
            p.op("scalar", lambda se, qt=qt: se.activation(
                rstdp1_s[:, qt:qt + 1], stat_r[:], AF.Identity,
                bias=onecol_f_s[:]), (), "sS")
            sS_chain = p.now("sS")
            p.op("vector", lambda v, ar=ar, qt=qt: v.tensor_scalar(
                y1[:, qt * E:(qt + 1) * E], attnrow[ar][:], stat_r[:],
                stat_e[:], OP.mult, OP.add), [("sS", sS_chain)], "vS")
            y1evac.append(("vS", p.now("vS")))
            attnrow_rel[ar] = x1dma[qt]
            p.op("vector", lambda v, qt=qt, sl3=sl3: v.scalar_tensor_tensor(
                x0tile[sl3][:], y1[:, qt * E:(qt + 1) * E], 1.0, WrB[:],
                OP.mult, OP.mult, accum_out=stat_f[:]),
                [("sS", sS_wrb)], "vS")
            x0rel3[sl3] = ("vS", p.now("vS"))
            p.op("vector", lambda v: v.tensor_copy(vdum[:], stat_f[:]), (), "vS")
            p.op("vector", lambda v, qt=qt: v.tensor_add(
                logits_s[:, qt:qt + 1], stat_f[:], brcol[:]), (), "vS")
        tail_defer.append(tail_all)
    flush_tail()
    tS_attn = p.now("tS")
    vS_attn = p.now("vS")
    sS_attn = p.now("sS")

    w1dma = []
    w1_rel = {}
    for slq in range(4):
        sem = f"dW1{slq}"
        w1dma.append((sem, p.op("gpsimd", lambda g, slq=slq: g.dma_start(
            w1ring[slq][:].rearrange("p (t f) -> p t f", t=16),
            w1t.ap()[:, slq * 512:(slq + 1) * 512].rearrange(
                "(t p) f -> p t f", p=128)), [("tS", tS_attn)], sem, 16)))

    p.op("tensor", lambda t: t.matmul(PS[0:16, 3072:3200], logits_s[:],
                                      i128f_s[:], start=True, stop=True),
         [("vS", vS_attn)] + bw(6), "tS")
    tS_lt = p.now("tS")
    vS_lr = p.op("vector", lambda v: v.tensor_copy(
        log16[:], PS[0:16, 3072:3200]), [("tS", tS_lt)], "vS")
    brel[6] = ("vS", vS_lr)
    dX_lr = p.op("sync", lambda s: s.dma_start(logrow[:], log16[:]),
                 [("vS", vS_lr)], "dLR", 16)
    for cch in range(4):
        bk_ = 7 if cch % 2 == 0 else 6
        p.op("tensor", lambda t, cch=cch, bk_=bk_: t.matmul(
            bank(bk_), onesrow_f_s[:],
            logrow[0:1, cch * 512:(cch + 1) * 512], start=True, stop=True),
            [("dLR", dX_lr)] + bw(bk_), "tS")
        tS_lb = p.now("tS")
        brel[bk_] = ("sS", p.op("scalar", lambda se, cch=cch, bk_=bk_: se.activation(
            LB[:, cch * 512:(cch + 1) * 512], bank(bk_), AF.Identity),
            [("tS", tS_lb)], "sS"))
    sS_lb = p.now("sS")
    p.op("vector", lambda v: v.memset(ones2k[:], 1.0), [("sS", sS_lb)], "vS")
    for st in range(1, NT, 2):
        p.op("vector", lambda v, st=st: v.scalar_tensor_tensor(
            gdump[:], LB[:], logits_s[:, st:st + 1], ones2k[:],
            OP.is_gt, OP.mult, accum_out=cg_s[:, st:st + 1]), (), "vS")
    evens = list(range(0, NT, 2))
    p.op("scalar", lambda se: se.mul(hkcol[:], onecol_f_s[:], 1023.5),
         [("vS", vS_attn)], "sS")
    p.op("scalar", lambda se: se.mul(
        nlog2[:, 0:1], logits_s[:, evens[0]:evens[0] + 1], -1.0), (), "sS")
    p.op("scalar", lambda se: se.mul(
        nlog2[:, 1:2], logits_s[:, evens[1]:evens[1] + 1], -1.0), (), "sS")
    for i, st in enumerate(evens):
        pr = i % 2
        p.op("scalar", lambda se, pr=pr: se.activation(
            tstage[:], LB[:], AF.Sign, bias=nlog2[:, pr:pr + 1],
            accum_out=cgalt2[:, pr:pr + 1]), (), "sS")
        if i + 2 < len(evens):
            p.op("scalar", lambda se, i=i: se.mul(
                nlog2[:, i % 2:i % 2 + 1],
                logits_s[:, evens[i + 2]:evens[i + 2] + 1], -1.0), (), "sS")
        else:
            p.op("scalar", lambda se: se.copy(sdum[:], onecol_f_s[:]),
                 (), "sS")
        if i >= 1:
            p.op("scalar", lambda se, i=i: se.activation(
                cg_s[:, evens[i - 1]:evens[i - 1] + 1],
                cgalt2[:, (i - 1) % 2:(i - 1) % 2 + 1], AF.Identity,
                bias=hkcol[:], scale=0.5), (), "sS")
    p.op("scalar", lambda se: se.copy(sdum[:], onecol_f_s[:]), (), "sS")
    p.op("scalar", lambda se: se.copy(sdum[:], onecol_f_s[:]), (), "sS")
    p.op("scalar", lambda se: se.activation(
        cg_s[:, evens[-1]:evens[-1] + 1], cgalt2[:, 1:2], AF.Identity,
        bias=hkcol[:], scale=0.5), (), "sS")
    sS_cnt = p.now("sS")
    p.op("vector", lambda v: v.tensor_copy(vdum[:], cg_s[:, 0:1]),
         [("sS", sS_cnt)], "vS")
    p.op("vector", lambda v: v.tensor_copy(vdum[:], cg_s[:, 0:1]), (), "vS")
    p.op("vector", lambda v: v.tensor_scalar(
        m_s[:], cg_s[:], float(KSEL), None, OP.is_lt), (), "vS", serial=True)
    p.op("vector", lambda v: v.tensor_copy(mb_s[:], m_s[:]), (), "vS", serial=True)
    vS_m = p.now("vS")
    dbg_dump("dbg_cg", cg_s[:], [("vS", vS_m)], f32src=True)
    dbg_dump("dbg_m", m_s[:], [("vS", vS_m)], f32src=True)
    dbg_dump("dbg_logits", logits_s[:], [("vS", vS_m)], f32src=True)
    p.op("tensor", lambda t: t.matmul(PS[0:16, 3584:3585], mb_s[:],
                                      onescol_b_s[:], start=True, stop=True),
         [("vS", vS_m)] + bw(7), "tS", serial=True)
    tS_c1 = p.now("tS")
    p.op("vector", lambda v: v.tensor_copy(ts16_s[:], PS[0:16, 3584:3585]),
         [("tS", tS_c1)], "vS", serial=True)
    vS_c1 = p.now("vS")
    p.op("tensor", lambda t: t.matmul(PS[0:16, 3712:3713], triu16s_s[:],
                                      ts16_s[:], start=True, stop=True),
         [("vS", vS_c1)], "tS", serial=True)
    tS_c2 = p.now("tS")
    p.op("vector", lambda v: v.tensor_copy(offs_s[:], PS[0:16, 3712:3713]),
         [("tS", tS_c2)], "vS", serial=True)
    vS_c2 = p.now("vS")
    p.op("tensor", lambda t: t.matmul(PS[0:1, 3840:3856], offs_s[:],
                                      i128b_s[0:16, 0:16], start=True,
                                      stop=True), [("vS", vS_c2)], "tS",
         serial=True)
    tS_c3 = p.now("tS")
    p.op("vector", lambda v: v.tensor_copy(offsrow_s[:], PS[0:1, 3840:3856]),
         [("tS", tS_c3)], "vS", serial=True)
    vS_c3 = p.now("vS")
    p.op("tensor", lambda t: t.matmul(PS[:, 3328:3344], triu128s_s[:],
                                      mb_s[:], start=True, stop=False),
         [("vS", vS_c3)], "tS", serial=True)
    p.op("tensor", lambda t: t.matmul(PS[:, 3328:3344], onesrow_b_s[:],
                                      offsrow_s[:], start=False, stop=True),
         (), "tS", serial=True)
    tS_pos = p.now("tS")
    p.op("vector", lambda v: v.tensor_scalar(
        t1_s[:], m_s[:], -BIG, BIG, OP.mult, OP.add), [("tS", tS_pos)], "vS", serial=True)
    p.op("vector", lambda v: v.tensor_copy(vdum[:], t1_s[:, 0:1]), (), "vS")
    p.op("vector", lambda v: v.tensor_copy(vdum[:], t1_s[:, 1:2]), (), "vS")
    p.op("vector", lambda v: v.tensor_add(q_s[:], t1_s[:], PS[:, 3328:3344]),
         (), "vS", serial=True)
    p.op("vector", lambda v: v.tensor_copy(vdum[:], q_s[:, 0:1]), (), "vS")
    brel[7] = ("vS", p.now("vS"))
    vS_q = p.now("vS")
    dbg_dump("dbg_q", q_s[:], [("vS", vS_q)], f32src=True)
    vS_ptc = []
    for st in range(NT):
        p.op("vector", lambda v, st=st: v.tensor_scalar(
            PT[:, st * 256:(st + 1) * 256], iotab_s[:], q_s[:, st:st + 1],
            None, OP.is_equal), (), "vS", serial=True)
        vS_ptc.append(p.now("vS"))
    vS_pt = p.now("vS")
    fx_evac = []
    for et in range(NE):
        bk_ = et % 2
        for st in range(NT):
            waits = ()
            if et == 0:
                waits = [("vS", vS_ptc[st])] + (bw(bk_) if st == 0 else [])
            elif st == 0:
                waits = bw(bk_)
            p.op("tensor", lambda t, et=et, st=st, bk_=bk_: t.matmul(
                PS[:, bk_ * 512:bk_ * 512 + 256],
                y1[:, st * E + et * 128: st * E + (et + 1) * 128],
                PT[:, st * 256:(st + 1) * 256],
                start=(st == 0), stop=(st == NT - 1)), waits, "tS")
        tS_g = p.now("tS")
        brel[bk_] = ("sS", p.op("scalar", lambda se, et=et, bk_=bk_: se.activation(
            fxT[:, et * 256:(et + 1) * 256], PS[:, bk_ * 512:bk_ * 512 + 256],
            AF.Identity), [("tS", tS_g)], "sS"))
        fx_evac.append(p.now("sS"))
    sS_fx = p.now("sS")
    tS_gather = p.now("tS")
    psel = {}
    vS_P_l = [None]

    def emit_postsel():
        p.op("vector", lambda v: v.tensor_scalar(
            lm_s[:], m_s[:], BIG, -BIG, OP.mult, OP.add), [("vS", vS_m)], "vS", serial=True)
        p.op("vector", lambda v: v.tensor_copy(vdum[:], lm_s[:, 0:1]), (), "vS")
        p.op("vector", lambda v: v.tensor_add(lm2_s[:], lm_s[:], logits_s[:]),
             (), "vS", serial=True)
        p.op("vector", lambda v: v.tensor_copy(vdum[:], lm2_s[:, 0:1]), (), "vS")
        p.op("vector", lambda v: v.tensor_reduce(mx1_s[:], lm2_s[:], AX, OP.max),
             (), "vS", serial=True)
        vS_mx = p.now("vS")
        p.op("tensor", lambda t: t.matmul(PS[0:1, 3072:3200], mx1_s[:],
                                          i128f_s[:], start=True, stop=True),
             [("vS", vS_mx)] + bw(6), "tS")
        tS_mx = p.now("tS")
        p.op("vector", lambda v: v.tensor_reduce(
            negmx_s[0:1, :], PS[0:1, 3072:3200], AX, OP.max), [("tS", tS_mx)], "vS")
        vS_mm = p.now("vS")
        p.op("tensor", lambda t: t.matmul(PS[:, 3200:3201], onesrow_f_s[:],
                                          negmx_s[0:1, :], start=True, stop=True),
             [("vS", vS_mm)], "tS")
        tS_mb = p.now("tS")
        p.op("scalar", lambda se: se.mul(negmx_s[:], PS[:, 3200:3201], -1.0),
             [("tS", tS_mb)], "sS", serial=True)
        brel[6] = ("sS", p.now("sS"))
        p.op("scalar", lambda se: se.copy(sdum[:], negmx_s[:]), (), "sS")
        p.op("scalar", lambda se: se.activation(
            wu_s[:], lm2_s[:], AF.Exp, bias=negmx_s[:]), (), "sS", serial=True)
        sS_wu = p.now("sS")
        p.op("vector", lambda v: v.tensor_reduce(zc_s[:], wu_s[:], AX, OP.add),
             [("sS", sS_wu)], "vS")
        vS_z1 = p.now("vS")
        p.op("tensor", lambda t: t.matmul(PS[0:1, 3072:3200], zc_s[:],
                                          i128f_s[:], start=True, stop=True),
             [("vS", vS_z1)] + bw(6), "tS")
        tS_z = p.now("tS")
        p.op("vector", lambda v: v.tensor_reduce(
            zc_s[0:1, :], PS[0:1, 3072:3200], AX, OP.add), [("tS", tS_z)], "vS", serial=True)
        p.op("vector", lambda v: v.tensor_copy(vdum[:], cg_s[:, 0:1]), (), "vS")
        p.op("vector", lambda v: v.tensor_copy(vdum[:], cg_s[:, 1:2]), (), "vS")
        p.op("vector", lambda v: v.reciprocal(rz_s[:], zc_s[0:1, :]), (), "vS", serial=True)
        vS_rzz = p.now("vS")
        p.op("tensor", lambda t: t.matmul(PS[:, 3200:3201], onesrow_f_s[:],
                                          rz_s[:], start=True, stop=True),
             [("vS", vS_rzz)], "tS")
        tS_rb = p.now("tS")
        p.op("vector", lambda v: v.tensor_copy(rzc_s[:], PS[:, 3200:3201]),
             [("tS", tS_rb)], "vS", serial=True)
        brel[6] = ("vS", p.now("vS"))
        p.op("vector", lambda v: v.tensor_copy(vdum[:], rzc_s[:]), (), "vS")
        p.op("vector", lambda v: v.tensor_scalar(
            w_s[:], wu_s[:], rzc_s[:], None, OP.mult), (), "vS", serial=True)
        p.op("vector", lambda v: v.tensor_copy(vdum[:], w_s[:, 0:1]), (), "vS")
        p.op("vector", lambda v: v.tensor_copy(vdum[:], w_s[:, 1:2]), (), "vS")
        p.op("vector", lambda v: v.tensor_copy(wb_s[:], w_s[:]), (), "vS", serial=True)
        vS_w = p.now("vS")
        dbg_dump("dbg_w", w_s[:], [("vS", vS_w)], f32src=True)
        for ih in range(2):
            for sg in range(4):
                bk_ = 6 if (ih * 4 + sg) % 2 == 0 else 7
                for j in range(4):
                    st = sg * 4 + j
                    twaits = ([("vS", vS_pt)] + bw(bk_)) if j == 0 else ()
                    p.op("tensor", lambda t, st=st, ih=ih, bk_=bk_, j=j: t.matmul(
                        PS[:, bk_ * 512 + j * 128: bk_ * 512 + (j + 1) * 128],
                        PT[:, st * 256 + ih * 128: st * 256 + ih * 128 + 128],
                        i128b_s[:], start=True, stop=True), twaits, "tS")
                tS_p = p.now("tS")
                brel[bk_] = ("vS", p.op(
                    "vector", lambda v, ih=ih, sg=sg, bk_=bk_: v.tensor_copy(
                        Psb[:, ih * S + sg * 512:(ih * S + (sg + 1) * 512)],
                        bank(bk_)), [("tS", tS_p)], "vS"))
        vS_P_l[0] = ("vS", p.now("vS"))
        for ih in range(2):
            bk_ = 2 + ih
            for st in range(NT):
                waits = ([("vS", vS_w)] + bw(bk_)) if st == 0 else ()
                p.op("tensor", lambda t, ih=ih, st=st, bk_=bk_: t.matmul(
                    PS[:, bk_ * 512:bk_ * 512 + 1],
                    PT[:, st * 256 + ih * 128: st * 256 + ih * 128 + 128],
                    wb_s[:, st:st + 1], start=(st == 0), stop=(st == NT - 1)),
                    waits, "tS")
            tS_wg = p.now("tS")
            brel[bk_] = ("sS", p.op("scalar", lambda se, ih=ih, bk_=bk_: se.copy(
                wg_s[:, ih:ih + 1], PS[:, bk_ * 512:bk_ * 512 + 1]),
                [("tS", tS_wg)], "sS"))

        psel["vS_wg"] = ("sS", p.now("sS"))
        psel["vS_P"] = vS_P_l[0]
        dbg_dump("dbg_wg", wg_s[:], [psel["vS_wg"]], f32src=True)

    dbg_dump("dbg_fxT", fxT[:], [("sS", sS_fx)])
    dbg_dump("dbg_y1", y1[:, 0:E], [("sS", sS_fx)])

    p.op("gpsimd", lambda g: g.dma_start(
        b1row[:], b1t.ap().rearrange("(a f) -> a f", a=1)),
        [("tS", tS_gather)], "dB1", 16)
    dW_b1 = p.now("dB1")
    hT_evac = []
    for slq in range(16):
        rsl = slq % 4
        if slq >= 4:
            sem = f"dW1{rsl}"
            w1dma.append((sem, p.op("gpsimd", lambda g, slq=slq, rsl=rsl: g.dma_start(
                w1ring[rsl][:].rearrange("p (t f) -> p t f", t=16),
                w1t.ap()[:, slq * 512:(slq + 1) * 512].rearrange(
                    "(t p) f -> p t f", p=128)),
                [w1_rel[rsl]], sem, 16)))
        hb = [0, 1] if slq % 2 == 0 else [2, 3]
        for et in range(NE):
            for ih in range(2):
                bk_ = hb[ih]
                waits = ()
                if et == 0:
                    waits = [w1dma[slq], ("sS", sS_fx),
                             ("dB1", dW_b1)] + bw(bk_)
                p.op("tensor", lambda t, slq=slq, rsl=rsl, et=et, ih=ih,
                     bk_=bk_: t.matmul(
                         bank(bk_),
                         fxT[:, et * 256 + ih * 128: et * 256 + ih * 128 + 128],
                         w1ring[rsl][:, et * 512:(et + 1) * 512],
                         start=(et == 0), stop=False), waits, "tS")
        for ih in range(2):
            p.op("tensor", lambda t, slq=slq, ih=ih, bk_=hb[ih]: t.matmul(
                bank(bk_), onesrow_b_s[:],
                b1row[0:1, slq * 512:(slq + 1) * 512],
                start=False, stop=True), (), "tS")
        w1_rel[rsl] = ("tS", p.now("tS"))
        tS_h = p.now("tS")
        for ih in range(2):
            bk_ = hb[ih]
            hsl = 2 * (slq % 2) + ih
            brel[bk_] = ("sS", p.op(
                "scalar", lambda se, hsl=hsl, bk_=bk_: se.activation(
                    htile[hsl][:], bank(bk_), AF.Gelu), [("tS", tS_h)], "sS"))
            sS_g = p.now("sS")
            tbk = 4 if slq % 2 == 0 else 5
            for j in range(4):
                twaits = ([("sS", sS_g)] + bw(tbk)) if j == 0 else ()
                p.op("tensor", lambda t, hsl=hsl, j=j, tbk=tbk: t.matmul(
                    PS[:, tbk * 512 + j * 128: tbk * 512 + (j + 1) * 128],
                    htile[hsl][:, j * 128:(j + 1) * 128], i128b_s[:],
                    start=True, stop=True), twaits, "tS")
            tS_t = p.now("tS")
            def hev(v, slq=slq, ih=ih, tbk=tbk):
                dst = hT[:, slq * 1024: (slq + 1) * 1024].rearrange(
                    "p (j c) -> p j c", j=4)[:, :, ih * 128: ih * 128 + 128]
                return v.tensor_copy(dst, bank(tbk))
            brel[tbk] = ("vS", p.op("vector", hev, [("tS", tS_t)], "vS"))
            hT_evac.append(p.now("vS"))
        if slq == 0:
            emit_postsel()
    vS_hT = p.now("vS")
    tS_W1 = p.now("tS")
    dbg_dump("dbg_hT", hT[:, 0:2048], [("vS", vS_hT)])

    p.op("gpsimd", lambda g: g.dma_start(
        b2row[:], b2t.ap().rearrange("(a f) -> a f", a=1)),
        [x1dma[NT - 1], y1evac[NT - 1]], "dB2", 16)
    dW_b2 = p.now("dB2")
    w2dma = []
    w2_rel = {}
    w2_first = [("tS", tS_gather)] * 4 + [("tS", tS_W1)] * 3
    def emit_w2dma(ch):
        rsl = ch % 7
        waits = [w2_first[rsl]] if ch < 7 else [w2_rel[rsl]]
        sem = f"dW2{rsl}"
        w2dma.append((sem, p.op("gpsimd", lambda g, ch=ch, rsl=rsl: g.dma_start(
            w2ring[rsl][:].rearrange("p (j e) -> p j e", j=2),
            w2t.ap()[ch * 256:(ch + 1) * 256, :].rearrange(
                "(j p) e -> p j e", p=128)), waits, sem, 16)))
    for ch in range(7):
        emit_w2dma(ch)
    for ch in range(32):
        rsl = ch % 7
        for j in range(2):
            ft = ch * 2 + j
            for ih in range(2):
                for ec in range(4):
                    bk_ = ih * 4 + ec
                    waits = ()
                    if ft == 0 and ih == 0 and ec == 0:
                        waits = [w2dma[ch], ("vS", vS_hT)]
                        for bb in range(8):
                            waits += bw(bb)
                    elif ih == 0 and ec == 0:
                        waits = [w2dma[ch]]
                    p.op("tensor", lambda t, ft=ft, j=j, ih=ih, ec=ec, rsl=rsl,
                         bk_=bk_: t.matmul(
                             bank(bk_),
                             hT[:, ft * 256 + ih * 128: ft * 256 + ih * 128 + 128],
                             w2ring[rsl][:, j * E + ec * 512: j * E + (ec + 1) * 512],
                             start=(ft == 0), stop=False), waits, "tS")
        w2_rel[rsl] = ("tS", p.now("tS"))
        if ch + 7 < 32:
            emit_w2dma(ch + 7)
    for ih in range(2):
        for ec in range(4):
            bk_ = ih * 4 + ec
            p.op("tensor", lambda t, ec=ec, bk_=bk_: t.matmul(
                bank(bk_), onesrow_b_s[:], b2row[0:1, ec * 512:(ec + 1) * 512],
                start=False, stop=True), [("dB2", dW_b2)] if bk_ == 0 else (),
                "tS")
    tS_W2 = p.now("tS")
    for ih in range(2):
        for ec in range(4):
            bk_ = ih * 4 + ec
            brel[bk_] = ("sS", p.op(
                "scalar", lambda se, ih=ih, ec=ec, bk_=bk_: se.activation(
                    xwsb[ih][:, ec * 512:(ec + 1) * 512], bank(bk_),
                    AF.Identity, scale=wg_s[:, ih:ih + 1]),
                [("tS", tS_W2), psel["vS_wg"]] if bk_ == 0 else (), "sS"))
    sS_xw = p.now("sS")
    dbg_dump("dbg_xw", xwsb[0][:], [("sS", sS_xw)])

    x1dmar = []
    x1r_rel = {}
    def emit_x1dma(st):
        rsl = st % 6
        waits = [("dXA", p.now("dXA")), ("dXB", p.now("dXB")),
                 ("tS", tS_W1)] if st < 6 else [x1r_rel[rsl]]
        sem = f"dR{rsl}"
        x1dmar.append((sem, p.op("sync", lambda s, st=st, rsl=rsl: s.dma_start(
            x1re[rsl][:], x1s.ap()[st * 128:(st + 1) * 128, :]), waits,
            sem, 16)))
    for st in range(6):
        emit_x1dma(st)
    out_rel = [None, None]
    ts_rel = [None, None]
    for st in range(NT):
        rsl = st % 6
        osl = st % 2
        tst_ = tstage if st % 2 == 0 else tstage1
        bks = [0, 1, 2, 3] if st % 2 == 0 else [4, 5, 6, 7]
        for ec in range(4):
            bk_ = bks[ec]
            for ih in range(2):
                waits = ()
                if ec == 0 and ih == 0:
                    waits = [("sS", sS_xw), psel["vS_P"]]
                    for bb in bks:
                        waits += bw(bb)
                p.op("tensor", lambda t, st=st, ec=ec, ih=ih, bk_=bk_: t.matmul(
                    bank(bk_), Psb[:, ih * S + st * 128: ih * S + (st + 1) * 128],
                    xwsb[ih][:, ec * 512:(ec + 1) * 512],
                    start=(ih == 0), stop=(ih == 1)), waits, "tS")
        tS_d = p.now("tS")
        twaits = [x1dmar[st]]
        if ts_rel[osl] is not None:
            twaits.append(ts_rel[osl])
        p.op("scalar", lambda se, st=st, rsl=rsl, tst_=tst_: se.activation(
            tst_[:], x1re[rsl][:], AF.Identity,
            bias=nbias1_s[:, st:st + 1], scale=rstdp1_s[:, st:st + 1]),
            twaits, "sS")
        sS_t = p.now("sS")
        x1r_rel[rsl] = ("sS", sS_t)
        if st + 6 < NT:
            emit_x1dma(st + 6)
        ovw = [("sS", sS_t), ("tS", tS_d)]
        if out_rel[osl] is not None:
            ovw.append(out_rel[osl])
        vv = None
        for ec in range(4):
            bk_ = bks[ec]
            vv = p.op("vector", lambda v, osl=osl, ec=ec, bk_=bk_,
                      tst_=tst_: v.tensor_add(
                outstage[osl][:, ec * 512:(ec + 1) * 512],
                tst_[:, ec * 512:(ec + 1) * 512], bank(bk_)),
                ovw if ec == 0 else (), "vS")
            brel[bk_] = ("vS", vv)
        ts_rel[osl] = ("vS", vv)
        sem = f"dO{osl}"
        oc = p.op("sync", lambda s, st=st, osl=osl: s.dma_start(
            outt.ap()[st * 128:(st + 1) * 128, :], outstage[osl][:]),
            [("vS", vv)], sem, 16)
        out_rel[osl] = (sem, oc)
    fin = [("dO0", p.now("dO0")), ("dO1", p.now("dO1"))]
    if debug:
        fin.append(("dD", p.now("dD")))
    p.op("sync", lambda s: s.nop(), fin)

    for e in Plan.ENGINES:
        for fn, w, inc, n, _srl in p.ops[e]:
            for snm, v in w:
                assert v <= p.now(snm), (e, snm, v, p.now(snm))


    semnames = set(p.cnt)
    for e in Plan.ENGINES:
        for fn, w, inc, n, _srl in p.ops[e]:
            for snm, v in w:
                semnames.add(snm)
    from contextlib import ExitStack
    with ExitStack() as ctx:
        sems = {s: ctx.enter_context(nc.semaphore(s))
                for s in sorted(semnames)}
        blk = ctx.enter_context(nc.Block())

        def mk(engname):
            def body(eng):
                lw = {}
                cnt = {}
                for fn, waits, inc, inc_n, srl in p.ops[engname]:
                    for snm, v in waits:
                        if lw.get(snm, 0) < v:
                            eng.wait_ge(sems[snm], v)
                            lw[snm] = v
                    instr = fn(eng)
                    if inc is not None:
                        cnt[inc] = cnt.get(inc, 0) + inc_n
                        if instr is None:
                            eng.sem_inc(sems[inc], inc_n)
                        else:
                            instr.then_inc(sems[inc], inc_n)
                        if srl:
                            eng.wait_ge(sems[inc], cnt[inc])
                            lw[inc] = cnt[inc]
            return body
        blk.tensor(mk("tensor"))
        blk.vector(mk("vector"))
        blk.scalar(mk("scalar"))
        blk.sync(mk("sync"))
        blk.gpsimd(mk("gpsimd"))
    return nc


_NC_CACHE = {}


def _get_nc(debug=False):
    if debug not in _NC_CACHE:
        _NC_CACHE[debug] = build(debug=debug)
    return _NC_CACHE[debug]


def kernel(L0_attn_x, L0_attn_Wq, L0_attn_bq, L0_attn_Wk, L0_attn_bk,
           L0_attn_Wv, L0_attn_bv, L0_ffn_Wr, L0_ffn_br, L0_ffn_W1,
           L0_ffn_b1, L0_ffn_W2, L0_ffn_b2):
    f = lambda a: np.ascontiguousarray(np.asarray(a, dtype=np.float32))
    shared = {
        "wq": f(L0_attn_Wq), "bq": f(L0_attn_bq), "wk": f(L0_attn_Wk),
        "bk": f(L0_attn_bk), "wv": f(L0_attn_Wv), "bv": f(L0_attn_bv),
        "wr": f(L0_ffn_Wr), "br": f(L0_ffn_br), "w1": f(L0_ffn_W1),
        "b1": f(L0_ffn_b1), "w2": f(L0_ffn_W2), "b2": f(L0_ffn_b2),
    }
    x = f(L0_attn_x)
    nc = _get_nc(False)
    in_maps = [dict(shared, x0=np.ascontiguousarray(x[b])) for b in range(8)]
    res = run_bass_kernel_spmd(nc, in_maps, core_ids=list(range(8)))
    return np.stack([r["out"] for r in res.results], axis=0)
